# version 27
# baseline (speedup 1.0000x reference)
import numpy as np
from contextlib import ExitStack

import concourse.bass as bass
import concourse.mybir as mybir
from concourse.bass_utils import run_bass_kernel_spmd

F32 = mybir.dt.float32
BF16 = mybir.dt.bfloat16
AF = mybir.ActivationFunctionType
ALU = mybir.AluOpType
AX = mybir.AxisListType

D = 2048
C = 16
P = 128
T = 2048
TX = 2049
CW = 2052
HW = 1028
TH = 1024
MAGIC = float(3 << 22)
NCORE = 8

LN_EPS = 1e-5
RMS_EPS = 1e-6
A_EPS = 1e-5
W_EPS = 1e-5

ENGS = ("sync", "tensor", "vector", "scalar", "gpsimd")
DMA_ENGS = ("sync", "scalar", "gpsimd")
DMA_POOL = 6

TAP = ()
NO_CC = False
STOP_AT = 99
LOADTEST = None


def SL(total, start=0, maxw=512):
    out = []
    o = 0
    while o < total:
        w = min(maxw, total - o)
        out.append([start + o, w])
        o += w
    if len(out) >= 2 and out[-1][1] < 64:
        spare = out[-2][1] + out[-1][1]
        out[-2][1] = spare // 2
        out[-1] = [out[-2][0] + spare // 2, spare - spare // 2]
    return [tuple(x) for x in out]


class Op:
    __slots__ = ("eng", "emit", "after", "dma", "ms", "sem", "val", "prev",
                 "tag")

    def __init__(self, eng, emit, after, dma, tag, strict=()):
        self.eng, self.emit, self.dma, self.tag = eng, emit, dma, tag
        self.after = [a for a in after
                      if a is not None and (a.eng != eng or a.dma)]
        self.after += [a for a in strict if a is not None]
        self.ms = False
        self.sem = None
        self.val = None
        self.prev = 0


class Seq:
    def __init__(self, nc, est):
        self.nc = nc
        self.ops = []
        self.last = {e: None for e in ENGS}
        self.isem = {e: est.enter_context(nc.semaphore(f"i_{e}"))
                     for e in ENGS}
        self.icnt = {e: 0 for e in ENGS}
        self.dsem = {e: [est.enter_context(nc.semaphore(f"d_{e}{i}"))
                         for i in range(DMA_POOL)] for e in DMA_ENGS}
        self.dcnt = {e: [0] * DMA_POOL for e in DMA_ENGS}
        self.duse = {e: 0 for e in DMA_ENGS}
        self.dlog = {e: [] for e in DMA_ENGS}

    def op(self, eng, emit, after=(), tag="", strict=()):
        o = Op(eng, emit, after, False, tag, strict)
        self.ops.append(o)
        self.last[eng] = o
        return o

    def dma(self, eng, out, in_, after=(), tag="", strict=()):
        def emit(e, out=out, in_=in_):
            return e.dma_start(out=out, in_=in_)
        o = Op(eng, emit, after, True, tag, strict)
        self.ops.append(o)
        self.last[eng] = o
        self.dlog[eng].append(o)
        return o

    def barrier(self):
        snap = [self.last[e] for e in ENGS if self.last[e] is not None]
        for e in DMA_ENGS:
            snap.extend(self.dlog[e][-DMA_POOL:])
        seen, out = set(), []
        for o in snap:
            if id(o) not in seen:
                seen.add(id(o))
                out.append(o)
        return out

    def finalize(self):
        for o in self.ops:
            for a in o.after:
                a.ms = True
        for o in self.ops:
            e = o.eng
            if o.dma:
                slot = self.duse[e] % DMA_POOL
                self.duse[e] += 1
                o.sem = self.dsem[e][slot]
                o.prev = self.dcnt[e][slot]
                self.dcnt[e][slot] += 16
                o.val = self.dcnt[e][slot]
                o.ms = True
            elif o.ms:
                self.icnt[e] += 1
                o.sem = self.isem[e]
                o.val = self.icnt[e]

    def run_engine(self, eng_name, handle):
        for o in self.ops:
            if o.eng != eng_name:
                continue
            for a in o.after:
                handle.wait_ge(a.sem, a.val)
            if o.dma and o.prev:
                handle.wait_ge(o.sem, o.prev)
            inst = o.emit(handle)
            if o.ms:
                if inst is None:
                    raise RuntimeError(f"milestoned op '{o.tag}' returned "
                                       "no instruction")
                inst.then_inc(o.sem, 16 if o.dma else 1)


def build_nc(tap=TAP):
    nc = bass.Bass("TRN2", target_bir_lowering=False)

    def din(name, shape, dt=F32):
        return nc.dram_tensor(name, shape, dt, kind="ExternalInput").ap()

    def dscr(name, shape, dt=F32):
        if name in tap:
            return nc.dram_tensor(name, shape, dt, kind="ExternalOutput").ap()
        return nc.dram_tensor(name, shape, dt).ap()

    x_h = din("x", [D, CW])
    wk_h = din("wk", [D, D])
    wv_h = din("wv", [D, D])
    wr_h = din("wr", [D, D])
    wo_h = din("wo", [D, D])
    wt_h = din("wt", [D, D])
    vecs_h = {n: din(n, [P, C]) for n in
              ("lnsc", "lnb", "mk", "mv", "mr", "tf", "td", "gam")}
    ident_h = din("ident", [P, P])
    onesrow_h = din("onesrow", [1, P])
    onescol_h = din("onescol", [P, 1])
    pmask_h = din("pmask", [P, 1])
    selmask_h = din("selmask", [P, 256])

    y_h = nc.dram_tensor("y", [D, T], F32, kind="ExternalOutput").ap()

    wiT_h = dscr("wiT", [D, D], BF16)
    x1_h = dscr("x1s", [D, CW])
    xp_h = dscr("xps", [D, CW])
    kxhi_h = dscr("kxhi", [D, T], BF16)
    kxlo_h = dscr("kxlo", [D, T], BF16)
    vxhi_h = dscr("vxhi", [D, T], BF16)
    vxlo_h = dscr("vxlo", [D, T], BF16)
    rxhi_h = dscr("rxhi", [D, T], BF16)
    rxlo_h = dscr("rxlo", [D, T], BF16)
    wshi_h = {w: dscr(w + "hi", [D, D], BF16) for w in ("wk", "wv", "wr")}
    wslo_h = {w: dscr(w + "lo", [D, D], BF16) for w in ("wk", "wv", "wr")}
    ek_h = dscr("eks", [D, T])
    ekv_h = dscr("ekvs", [D, T])
    rw_h = dscr("rws", [D, T])
    a2_h = dscr("a2s", [D, T])
    cin_h = nc.dram_tensor("cin_b", [P, 32], F32).ap()
    cout_h = nc.dram_tensor("cout_b", [NCORE * P, 32], F32).ap()

    est = ExitStack()
    sbt = lambda n, s, dt=F32: est.enter_context(nc.sbuf_tensor(n, s, dt))

    with est:
        A1 = sbt("A1", [P, C * CW])
        HB = sbt("HB", [P, C * HW])
        RB = sbt("RB", [P, 2 * 1028])
        XI = A1[:, 0:(C * CW) // 2].bitcast(BF16)
        xic = lambda c: XI[:, c * CW:(c + 1) * CW]
        SH = 8 * CW

        def shslot(i):
            return A1[:, SH + i * HW:SH + (i + 1) * HW]
        R_S, R_Q, R_M, R_T1, R_T2, R_RS, R_RM, R_A = (shslot(i)
                                                      for i in range(8))
        SC1, SC2 = shslot(8), shslot(9)
        SC12 = A1[:, SH + 8 * HW:SH + 10 * HW]
        SC3 = R_T2
        MIXO = [R_S, R_Q, R_A]
        QS = A1[:, SH + 10 * HW:SH + 10 * HW + CW]
        IDENT = A1[:, SH + 10 * HW + CW:SH + 10 * HW + CW + P]
        SB2 = SH + 10 * HW + CW + P
        W0STG = [A1[:, SB2:SB2 + 1024], A1[:, SB2 + 1024:SB2 + 2048]]
        W0HI = A1[:, SB2 + 2048:SB2 + 3072].bitcast(BF16)
        W0LO = RB[:, 1028:2056].bitcast(BF16)
        SPH = [A1[:, SB2 + j * 512:SB2 + (j + 1) * 512
                  ].bitcast(BF16)[:, 0:TH] for j in range(3)]
        SPL = [A1[:, SB2 + 1536 + j * 512:SB2 + 1536 + (j + 1) * 512
                  ].bitcast(BF16)[:, 0:TH] for j in range(3)]
        A1BF = A1[:, :].bitcast(BF16)
        hbc = lambda c: HB[:, c * HW:(c + 1) * HW]

        def hrow(i):
            return HB[:, i * CW:(i + 1) * CW]
        ROWA = [hrow(0), hrow(1)]
        ROWB = [hrow(2), hrow(3)]
        ROWC = [hrow(4), hrow(5)]
        WCOL = [hrow(6), hrow(7)]
        SA = RB[:, 0:1028]
        SBt = RB[:, 1028:2056]

        VEC = sbt("vec", [P, C * 11])
        vt = {}
        for i, n in enumerate(("lnsc", "lnb", "mk", "mv", "mr", "tf", "td",
                               "gam", "lam", "eu", "scr")):
            vt[n] = VEC[:, i * C:(i + 1) * C]
        ONESC = sbt("onesc", [P, 1])
        PMASK = sbt("pmaskt", [P, 1])
        CARRY = sbt("carry", [P, 32])
        WSC = sbt("wsc", [P, 1])
        RROW = sbt("rrow", [1, 1152])
        s_row = RROW[0:1, 0:512]
        q_row = RROW[0:1, 512:1024]
        a_row = RROW[0:1, 0:512]
        ONESR = RROW[0:1, 1024:1152]
        AMT2 = RB[0:16, 0:P]
        AMTM = RB[:, 128:144]
        GATH = RB[:, 256:512]
        SELM = RB[:, 512:768]

        PB = [est.enter_context(nc.psum_tensor(f"pb{i}", [P, 512], F32))
              for i in range(8)]

        sq = Seq(nc, est)

        ld_vec = []
        for ap_in, tile in [
            (vecs_h["lnsc"], vt["lnsc"]), (vecs_h["lnb"], vt["lnb"]),
            (vecs_h["mk"], vt["mk"]), (vecs_h["mv"], vt["mv"]),
            (vecs_h["mr"], vt["mr"]), (vecs_h["tf"], vt["tf"]),
            (vecs_h["td"], vt["td"]), (vecs_h["gam"], vt["gam"]),
            (onesrow_h, ONESR), (onescol_h, ONESC[:, :]),
            (pmask_h, PMASK[:, :]),
        ]:
            ld_vec.append(sq.dma("sync", tile, ap_in, tag="ld_vec"))
        ld_wt = [sq.dma("sync", A1[:, c * CW:c * CW + D],
                        wt_h[c * P:(c + 1) * P, :], tag=f"ld_wt{c}")
                 for c in range(C)]

        sq.op("scalar", lambda e: e.activation(vt["scr"], vt["td"], AF.Exp),
              after=ld_vec, tag="p0exp1")
        sq.op("scalar", lambda e: e.activation(vt["lam"], vt["scr"], AF.Exp,
                                               scale=-1.0), tag="p0lam")
        lam_done = sq.op("scalar",
                         lambda e: e.activation(vt["eu"], vt["tf"], AF.Exp),
                         tag="p0eu")
        def p0_abs(e):
            t = None
            for c in range(C):
                t = e.tensor_reduce(vt["scr"][:, c:c + 1],
                                    A1[:, c * CW:c * CW + D],
                                    AX.X, ALU.add, apply_absolute_value=True)
            return t
        wabs0 = sq.op("vector", p0_abs, after=ld_wt + [lam_done],
                      tag="p0abs0")
        wabs = sq.op("vector",
                     lambda e: e.tensor_reduce(WSC[:, 0:1], vt["scr"],
                                               AX.X, ALU.add),
                     strict=[wabs0], tag="p0abs")

        wsum = sq.op("tensor",
                     lambda e: e.matmul(PB[0][0:1, 0:1], WSC[:, 0:1],
                                        ONESC[:, :], start=True, stop=True),
                     after=[wabs], tag="p0sum")

        w0 = sq.op("vector",
                   lambda e: e.tensor_copy(RROW[0:1, 0:1], PB[0][0:1, 0:1]),
                   after=[wsum], tag="p0w0")
        w1 = sq.op("vector",
                   lambda e: e.tensor_single_scalar(RROW[0:1, 1:2],
                                                    RROW[0:1, 0:1],
                                                    1.0 / (D * D), ALU.mult),
                   strict=[w0], tag="p0w1")
        w2 = sq.op("vector",
                   lambda e: e.tensor_single_scalar(RROW[0:1, 2:3],
                                                    RROW[0:1, 1:2], W_EPS,
                                                    ALU.add),
                   strict=[w1], tag="p0w2")
        wsrow = sq.op("vector",
                      lambda e: e.reciprocal(RROW[0:1, 3:4], RROW[0:1, 2:3]),
                      strict=[w2], tag="p0wsrow")

        wsbc = sq.op("tensor",
                     lambda e: e.matmul(PB[1][:, 0:1], ONESR,
                                        RROW[0:1, 3:4], start=True,
                                        stop=True),
                     after=[wsrow], tag="p0wsbc")
        wsev = sq.op("vector",
                     lambda e: e.tensor_copy(WSC[:, 0:1], PB[1][:, 0:1]),
                     after=[wsbc], tag="p0wsev")

        wq_sp = {}
        for c in range(C):
            par = c % 2
            buf = hrow(par).bitcast(BF16)[:, 0:D]

            def p0_q(e, c=c, buf=buf):
                wtc = A1[:, c * CW:c * CW + D]
                e.tensor_single_scalar(wtc, wtc, WSC[:, 0:1], ALU.mult)
                e.tensor_scalar(wtc, wtc, MAGIC, MAGIC, ALU.add,
                                ALU.subtract)
                return e.tensor_scalar(buf, wtc, 1.0, -1.0, ALU.min,
                                       ALU.max)
            qo = sq.op("vector", p0_q,
                       after=[wq_sp[c - 2]] if c >= 2 else [],
                       strict=[wsev] if c < 2 else [],
                       tag=f"p0q{c}")
            wq_sp[c] = sq.dma(
                "gpsimd",
                wiT_h.rearrange("(ob p) (dc ol) -> p ob dc ol",
                                p=P, ol=P)[:, :, c, :],
                buf.rearrange("p (ob ol) -> p ob ol", ol=P),
                after=[qo], tag=f"p0qsp{c}")
        ph0_done = sq.barrier()

        wsplit_done = {}


        def st_sum(i):
            return (PB[6], 32 * i) if i < 3 else (PB[7], 32 * (i - 3))

        def st_sq(i):
            return (PB[4], 32 * i) if i < 3 else (PB[5], 32 * (i - 3))

        def mk_bounds(t0, widths):
            bs = [t0]
            for w in widths:
                bs.append(bs[-1] + w)
            return bs

        def bpieces(bs, a, b):
            out = []
            g = a
            while g < b:
                i = max(j for j in range(len(bs) - 1) if bs[j] <= g)
                end = min(b, bs[i + 1])
                out.append((i, g - bs[i], g - a, end - g))
                g = end
            return out

        def stats_acc(tag, src_fn, sq_fn, bs, first, last, dep):
            def p_mm(e):
                mm = None
                for i in range(len(bs) - 1):
                    o_, w = bs[i], bs[i + 1] - bs[i]
                    bank, base = st_sum(i)
                    e.matmul(bank[base:base + 1, 0:w], ONESC[:, :],
                             src_fn(o_, w), start=first, stop=last)
                    bank, base = st_sq(i)
                    mm = e.matmul(bank[base:base + 1, 0:w], ONESC[:, :],
                                  sq_fn(o_, w), start=first, stop=last)
                return mm
            return sq.op("tensor", p_mm, after=dep, tag=tag)

        def stats_sweep_cls(tag, src_row_fn, t0, W, entry_fn, order=None):
            bs = mk_bounds(t0, [w for _, w in SL(W)])
            order = order if order is not None else list(range(C))
            lds = {}
            mm_ops = []
            for idx, c in enumerate(order):
                lds[c] = sq.dma("sync", hbc(c)[:, 0:W],
                                src_row_fn(c)[:, t0:t0 + W],
                                after=entry_fn(c), tag=f"{tag}ld{c}")
                xsq = SC1 if idx % 2 == 0 else SC2
                so = sq.op("scalar",
                           lambda e, c=c, xsq=xsq, W=W:
                               e.activation(xsq[:, 0:W], hbc(c)[:, 0:W],
                                            AF.Square),
                           after=[lds[c]] + ([mm_ops[idx - 2]]
                                             if idx >= 2 else []),
                           tag=f"{tag}sq{c}")
                mm_ops.append(stats_acc(
                    f"{tag}mm{c}",
                    lambda o_, w, c=c: hbc(c)[:, o_ - t0:o_ - t0 + w],
                    lambda o_, w, xsq=xsq, t0=t0:
                        xsq[:, o_ - t0:o_ - t0 + w],
                    bs, idx == 0, idx == C - 1, [so]))
            return lds, mm_ops[-1], bs

        def stats_rows(tag, bs, t0, W, entry):
            bc_prev = None
            for i, (o_, w) in enumerate(SL(W)):
                pcs = bpieces(bs, t0 + o_, t0 + o_ + w)

                def p_cp(e, pcs=pcs):
                    t = None
                    for (sl, off, dst, ln) in pcs:
                        bank, base = st_sum(sl)
                        e.copy(s_row[0:1, dst:dst + ln],
                               bank[base:base + 1, off:off + ln])
                        bank, base = st_sq(sl)
                        t = e.copy(q_row[0:1, dst:dst + ln],
                                   bank[base:base + 1, off:off + ln])
                    return t
                cp = sq.op("scalar", p_cp,
                           after=list(entry) + ([bc_prev] if bc_prev
                                                else []),
                           tag=f"{tag}rc{i}")
                mm = sq.op("tensor",
                           lambda e, w=w:
                               (e.matmul(PB[2][:, 0:w], ONESR,
                                         s_row[0:1, 0:w], start=True,
                                         stop=True),
                                e.matmul(PB[3][:, 0:w], ONESR,
                                         q_row[0:1, 0:w], start=True,
                                         stop=True))[-1],
                           after=[cp], tag=f"{tag}bc{i}")
                bc_prev = sq.op("scalar",
                                lambda e, o_=o_, w=w:
                                    (e.copy(R_S[:, o_:o_ + w],
                                            PB[2][:, 0:w]),
                                     e.copy(R_Q[:, o_:o_ + w],
                                            PB[3][:, 0:w]))[-1],
                                after=[mm], tag=f"{tag}be{i}")
            return bc_prev

        def rowmath(tag, W, entry):
            def p_rm1(e, W=W):
                inv_d = 1.0 / D
                e.tensor_single_scalar(R_M[:, 0:W], R_S[:, 0:W], inv_d,
                                       ALU.mult)
                e.tensor_single_scalar(R_Q[:, 0:W], R_Q[:, 0:W], inv_d,
                                       ALU.mult)
                e.tensor_tensor(R_A[:, 0:W], R_M[:, 0:W], R_M[:, 0:W],
                                ALU.mult)
                e.tensor_tensor(R_Q[:, 0:W], R_Q[:, 0:W], R_A[:, 0:W],
                                ALU.subtract)
                return e.tensor_single_scalar(R_T1[:, 0:W], R_Q[:, 0:W],
                                              LN_EPS, ALU.add)
            rm1 = sq.op("vector", p_rm1, after=[entry], tag=f"{tag}rm1")
            sq1 = sq.op("scalar",
                        lambda e, W=W: e.activation(R_T2[:, 0:W],
                                                    R_T1[:, 0:W], AF.Sqrt),
                        after=[rm1], tag=f"{tag}s1")

            def p_rm2(e, W=W):
                e.reciprocal(R_RS[:, 0:W], R_T2[:, 0:W])
                e.tensor_tensor(R_T2[:, 0:W], R_RS[:, 0:W], R_RS[:, 0:W],
                                ALU.mult)
                e.tensor_tensor(R_T2[:, 0:W], R_T2[:, 0:W], R_T1[:, 0:W],
                                ALU.mult)
                e.tensor_scalar(R_T2[:, 0:W], R_T2[:, 0:W], -0.5, 1.5,
                                ALU.mult, ALU.add)
                return e.tensor_tensor(R_RS[:, 0:W], R_RS[:, 0:W],
                                       R_T2[:, 0:W], ALU.mult)
            return sq.op("vector", p_rm2, after=[sq1], tag=f"{tag}rm2")

        def second_rsqrt(tag, W, eps, entry):
            def p_a(e):
                e.reciprocal(R_RM[:, 0:W], R_T1[:, 0:W])
                e.tensor_tensor(R_RM[:, 0:W], R_RM[:, 0:W], R_Q[:, 0:W],
                                ALU.mult)
                return e.tensor_single_scalar(R_T1[:, 0:W], R_RM[:, 0:W],
                                              eps, ALU.add)
            a = sq.op("vector", p_a, after=[entry], tag=f"{tag}r2a")
            s = sq.op("scalar",
                      lambda e: e.activation(R_T2[:, 0:W], R_T1[:, 0:W],
                                             AF.Sqrt),
                      after=[a], tag=f"{tag}r2s")

            def p_b(e):
                e.reciprocal(R_RM[:, 0:W], R_T2[:, 0:W])
                e.tensor_tensor(R_T2[:, 0:W], R_RM[:, 0:W], R_RM[:, 0:W],
                                ALU.mult)
                e.tensor_tensor(R_T2[:, 0:W], R_T2[:, 0:W], R_T1[:, 0:W],
                                ALU.mult)
                e.tensor_scalar(R_T2[:, 0:W], R_T2[:, 0:W], -0.5, 1.5,
                                ALU.mult, ALU.add)
                return e.tensor_tensor(R_RM[:, 0:W], R_RM[:, 0:W],
                                       R_T2[:, 0:W], ALU.mult)
            return sq.op("vector", p_b, after=[s], tag=f"{tag}r2b")

        def quant_phase(tag, src_row_fn, halves, entry,
                        sweep0_entry_fn=None, sweep0_order=None,
                        packed=None, guard=()):
            idld = sq.dma("sync", IDENT, ident_h, after=entry,
                          tag=f"{tag}id")
            amz = sq.op("vector", lambda e: e.memset(AMTM, 0.0),
                        after=entry, tag=f"{tag}amz")
            prev = entry
            fins = []
            for hi, (t0, W) in enumerate(halves):
                slh = SL(W)
                nblk = (W + P - 1) // P
                if hi == 0 and sweep0_entry_fn is not None:
                    efn = sweep0_entry_fn
                    order = sweep0_order
                else:
                    efn = lambda c, prev=prev: list(prev)
                    order = None
                if packed is None:
                    lds, lastmm, bs = stats_sweep_cls(f"{tag}{hi}",
                                                      src_row_fn, t0, W,
                                                      efn, order)
                    acc = [lastmm]
                else:
                    bs, acc = packed
                    lds = {c: sq.dma("sync", hbc(c)[:, 0:W],
                                     src_row_fn(c)[:, t0:t0 + W],
                                     after=efn(c),
                                     tag=f"{tag}{hi}ld{c}")
                           for c in range(C)}
                be = stats_rows(f"{tag}{hi}", bs, t0, W,
                                list(acc) + (list(prev) if hi else []))
                rm2 = rowmath(f"{tag}{hi}", W, be)
                rmm = second_rsqrt(f"{tag}{hi}", W, RMS_EPS, rm2)

                subg_ops = []
                for c in range(C):
                    eng = "vector"

                    def p_sg(e, c=c, W=W):
                        e.tensor_tensor(hbc(c)[:, 0:W], hbc(c)[:, 0:W],
                                        R_M[:, 0:W], ALU.subtract)
                        return e.tensor_single_scalar(hbc(c)[:, 0:W],
                                                      hbc(c)[:, 0:W],
                                                      vt["gam"][:, c:c + 1],
                                                      ALU.mult)
                    subg_ops.append(sq.op(eng, p_sg, after=[rmm, lds[c]],
                                          tag=f"{tag}{hi}sg{c}"))
                tree_ops = []
                for c in range(C):
                    buf = SC3 if c == 0 else (SC1 if c % 2 == 1 else SC2)
                    adep = [subg_ops[c]] + ([tree_ops[c - 2]]
                                            if c >= 3 else [])
                    ab = sq.op("scalar",
                               lambda e, c=c, buf=buf, W=W:
                                   e.activation(buf[:, 0:W], hbc(c)[:, 0:W],
                                                AF.Abs),
                               after=adep, tag=f"{tag}{hi}ab{c}")
                    if c == 0:
                        tree_ops.append(ab)
                    else:
                        tree_ops.append(sq.op(
                            "vector",
                            lambda e, buf=buf, W=W:
                                e.tensor_tensor(SC3[:, 0:W], SC3[:, 0:W],
                                                buf[:, 0:W], ALU.max),
                            after=[ab], tag=f"{tag}{hi}mx{c}"))
                tree = tree_ops[-1]

                red_ops = []
                for j in range(nblk):
                    w = min(P, W - j * P)
                    pb = PB[j % 2]
                    tr = sq.op("tensor",
                               lambda e, pb=pb, j=j, w=w:
                                   e.transpose(pb[0:w, 0:P],
                                               SC3[:, j * P:j * P + w],
                                               IDENT),
                               after=[tree, idld] + ([red_ops[j - 2]]
                                                     if j >= 2 else []),
                               tag=f"{tag}{hi}tp{j}")
                    red_ops.append(sq.op(
                        "vector",
                        lambda e, pb=pb, j=j, w=w:
                            e.tensor_reduce(AMTM[0:w, j:j + 1], pb[0:w, 0:P],
                                            AX.X, ALU.max),
                        after=[tr, amz], tag=f"{tag}{hi}rd{j}"))
                tr2 = sq.op("tensor",
                            lambda e: e.transpose(PB[2][0:16, 0:P],
                                                  AMTM, IDENT),
                            after=red_ops, tag=f"{tag}{hi}tp2")
                amev = sq.op("vector",
                             lambda e: e.tensor_copy(AMT2,
                                                     PB[2][0:16, 0:P]),
                             after=[tr2], tag=f"{tag}{hi}ame")
                bc_ev = None
                for i, (o_, w) in enumerate(slh):
                    jb = o_ // P
                    nfull = w // P
                    rem = w - nfull * P
                    adep = [amev] + ([bc_ev] if bc_ev else [])
                    ads = []
                    if nfull:
                        ads.append(sq.dma(
                            "gpsimd", a_row[0:1, 0:nfull * P],
                            AMT2[jb:jb + nfull, :], after=adep,
                            tag=f"{tag}{hi}ad{i}"))
                    if rem:
                        ads.append(sq.dma(
                            "gpsimd",
                            a_row[0:1, nfull * P:nfull * P + rem],
                            AMT2[jb + nfull:jb + nfull + 1, 0:rem],
                            after=adep, tag=f"{tag}{hi}ar{i}"))
                    mm = sq.op("tensor",
                               lambda e, w=w:
                                   e.matmul(PB[3][:, 0:w], ONESR,
                                            a_row[0:1, 0:w], start=True,
                                            stop=True),
                               after=ads + ([bc_ev] if bc_ev else []),
                               tag=f"{tag}{hi}am{i}")
                    bc_ev = sq.op("scalar",
                                  lambda e, o_=o_, w=w:
                                      e.copy(R_A[:, o_:o_ + w],
                                             PB[3][:, 0:w]),
                                  after=[mm], tag=f"{tag}{hi}ae{i}")

                def p_finrow(e, W=W, t0=t0):
                    e.tensor_tensor(R_T1[:, 0:W], R_RS[:, 0:W], R_RM[:, 0:W],
                                    ALU.mult)
                    e.tensor_tensor(R_A[:, 0:W], R_A[:, 0:W], R_T1[:, 0:W],
                                    ALU.mult)
                    e.tensor_single_scalar(R_A[:, 0:W], R_A[:, 0:W], A_EPS,
                                           ALU.add)
                    e.reciprocal(R_A[:, 0:W], R_A[:, 0:W])
                    e.tensor_single_scalar(R_A[:, 0:W], R_A[:, 0:W], 127.0,
                                           ALU.mult)
                    e.tensor_single_scalar(R_T2[:, 0:W], R_A[:, 0:W],
                                           WSC[:, 0:1], ALU.mult)
                    e.reciprocal(R_T2[:, 0:W], R_T2[:, 0:W])
                    e.tensor_single_scalar(QS[:, t0:t0 + W], R_T2[:, 0:W],
                                           0.5, ALU.mult)
                    return e.tensor_tensor(R_T1[:, 0:W], R_T1[:, 0:W],
                                           R_A[:, 0:W], ALU.mult)
                finrow = sq.op("vector", p_finrow,
                               after=[bc_ev] + list(guard),
                               tag=f"{tag}{hi}fr")
                fin = None
                for c in range(C):
                    def p_fx(e, c=c, W=W, t0=t0):
                        e.tensor_tensor(hbc(c)[:, 0:W], hbc(c)[:, 0:W],
                                        R_T1[:, 0:W], ALU.mult)
                        return e.tensor_scalar(xic(c)[:, t0:t0 + W],
                                               hbc(c)[:, 0:W], MAGIC,
                                               MAGIC, ALU.add,
                                               ALU.subtract)
                    fin = sq.op("vector", p_fx, after=[finrow],
                                tag=f"{tag}{hi}fx{c}")
                fins.append(fin)
                prev = [fin]
            return fins

        if LOADTEST is not None and LOADTEST < 1:
            p1_fins = [ph0_done[0], ph0_done[0]]
        else:
            def p1_entry_fn(c):
                if c in (0, 1):
                    return [wq_sp[14]]
                if c in (2, 3):
                    return [wq_sp[15]]
                return []
            p1_fins = quant_phase(
                "p1", lambda c: x_h[c * P:(c + 1) * P, :],
                [(0, 1024), (1024, 1025)], ph0_done,
                sweep0_entry_fn=p1_entry_fn,
                sweep0_order=list(range(4, 16)) + list(range(4)),
                guard=ph0_done)
        if "xid" in tap:
            xid_h = nc.dram_tensor("xid", [D, CW], BF16,
                                   kind="ExternalOutput").ap()
            qsd_h = nc.dram_tensor("qsd", [P, CW], F32,
                                   kind="ExternalOutput").ap()
            for c in range(C):
                sq.dma("gpsimd", xid_h[c * P:(c + 1) * P, :], xic(c),
                       after=[p1_fins[-1]], tag=f"xid{c}")
            sq.dma("gpsimd", qsd_h, QS, after=[p1_fins[-1]], tag="qsd")

        def ffn_mm_phase(tag, tx, addsrc_fn, dst_fn, entryA, entryB,
                         fold=False, filler=None):
            slt = SL(tx)
            bsf = mk_bounds(0, [w for _, w in slt])
            ns = len(slt)
            ev_flat = []
            sp_ops = []
            mm_flat = []
            ev2_last = {}
            fsqr = {}
            ffmm = {}
            for o in range(C):
                par = o % 2
                wdep = list(entryA) if o < 2 else \
                    [mm_flat[(o - 2) * ns + ns - 1]]
                wld = sq.dma(
                    "sync",
                    WCOL[par].bitcast(BF16)[:, 0:C * P],
                    wiT_h[o * P:(o + 1) * P, :],
                    after=wdep, tag=f"{tag}w{o}")
                xdep = list(entryA) if o < 2 else [ev2_last[o - 2]]
                xld = sq.dma("sync", ROWA[par][:, 0:tx], addsrc_fn(o),
                             after=xdep, tag=f"{tag}x{o}")
                ev2 = None
                for i, (o_, w) in enumerate(slt):
                    g = o * ns + i
                    pb = PB[g % 2]

                    def p_mm(e, par=par, o_=o_, w=w, pb=pb):
                        mm = None
                        for d in range(C):
                            mm = e.matmul(
                                pb[:, 0:w],
                                WCOL[par].bitcast(BF16)[:,
                                                        d * P:(d + 1) * P],
                                xic(d)[:, o_:o_ + w],
                                start=(d == 0), stop=(d == C - 1))
                        return mm
                    mdep = [wld] + ([ev_flat[g - 2]] if g >= 2 else [])
                    if o == 0:
                        mdep += list(entryA) if (o_ + w <= 1024) else \
                            list(entryB)
                    mm = sq.op("tensor", p_mm, after=mdep,
                               tag=f"{tag}m{o}_{i}")
                    mm_flat.append(mm)

                    def p_ev(e, par=par, o_=o_, w=w, pb=pb):
                        return e.tensor_tensor(ROWB[par][:, o_:o_ + w],
                                               pb[:, 0:w],
                                               QS[:, o_:o_ + w], ALU.mult)
                    edep = [mm] + ([sp_ops[o - 2]]
                                   if (o >= 2 and i == 0) else [])
                    if fold and o >= 2 and i == 0:
                        edep += [fsqr[o - 2], ffmm.get(o - 2)]
                        edep = [d for d in edep if d is not None]
                    ev = sq.op("vector", p_ev, after=edep,
                               tag=f"{tag}e{o}_{i}")
                    ev_flat.append(ev)

                    def p_ev2(e, par=par, o_=o_, w=w):
                        return e.tensor_tensor(ROWB[par][:, o_:o_ + w],
                                               ROWB[par][:, o_:o_ + w],
                                               ROWA[par][:, o_:o_ + w],
                                               ALU.add)
                    ev2 = sq.op("gpsimd", p_ev2, after=[ev, xld],
                                tag=f"{tag}g{o}_{i}")
                ev2_last[o] = ev2
                dst = dst_fn(o)
                sp_ops.append(sq.dma("gpsimd", dst, ROWB[par][:, 0:tx],
                                     after=[ev2], tag=f"{tag}s{o}"))
                if fold:
                    if o > 0:
                        po = o - 1
                        ffmm[po] = stats_acc(
                            f"{tag}f{po}",
                            lambda o_, w, pp=po % 2:
                                ROWB[pp][:, o_:o_ + w],
                            lambda o_, w: SC12[:, o_:o_ + w],
                            bsf, po == 0, False, [fsqr[po]])
                    fsqr[o] = sq.op(
                        "scalar",
                        lambda e, par=par, tx=tx:
                            e.activation(SC12[:, 0:tx], ROWB[par][:, 0:tx],
                                         AF.Square),
                        after=[ev2] + ([ffmm[o - 1]] if o > 0 else []),
                        tag=f"{tag}q{o}")
                if filler is not None:
                    filler(6)
            fold_out = None
            if fold:
                ffmm[C - 1] = stats_acc(
                    f"{tag}f{C - 1}",
                    lambda o_, w, pp=(C - 1) % 2: ROWB[pp][:, o_:o_ + w],
                    lambda o_, w: SC12[:, o_:o_ + w],
                    bsf, False, True, [fsqr[C - 1]])
                fold_out = (bsf, [ffmm[C - 1]])
            if filler is not None:
                filler(1000)
            return sp_ops, mm_flat[-1], fold_out

        def p0b_gen():
            pw = {"h": None, "l": None, "s1": None, "s2": None}
            qs = ["sync", "scalar"]
            pi = 0
            for wname, w_src in (("wr", wr_h),):
                for c in range(C):
                    sps = []
                    for h in range(2):
                        p = pi % 2
                        dep = [x for x in (pw["h"], pw["l"])
                               if x is not None] or list(ph0_done)
                        wl = sq.dma(qs[p], W0STG[p],
                                    w_src[c * P:(c + 1) * P,
                                          h * 1024:(h + 1) * 1024],
                                    after=dep, tag=f"b{wname}l{c}_{h}")
                        hi = sq.op(
                            "scalar",
                            lambda e, p=p, h=h:
                                e.copy(W0HI[:, h * 1024:(h + 1) * 1024],
                                       W0STG[p]),
                            after=[x for x in
                                   (wl, pw["l"], pw["s1"])
                                   if x is not None],
                            tag=f"b{wname}h{c}_{h}")
                        lo = sq.op(
                            "vector",
                            lambda e, p=p, h=h:
                                e.tensor_tensor(
                                    W0LO[:, h * 1024:(h + 1) * 1024],
                                    W0STG[p],
                                    W0HI[:, h * 1024:(h + 1) * 1024],
                                    ALU.subtract),
                            after=[x for x in (hi, pw["s2"])
                                   if x is not None],
                            tag=f"b{wname}o{c}_{h}")
                        pw["h"] = hi
                        pw["l"] = lo
                        pi += 1
                        if h == 1:
                            s1 = sq.dma(
                                "gpsimd",
                                wshi_h[wname][c * P:(c + 1) * P, :],
                                W0HI, after=[pw["h"], hi],
                                tag=f"b{wname}s{c}")
                            s2 = sq.dma(
                                "gpsimd",
                                wslo_h[wname][c * P:(c + 1) * P, :],
                                W0LO[:, 0:2048], after=[lo],
                                tag=f"b{wname}t{c}")
                            pw["s1"] = s1
                            pw["s2"] = s2
                        yield
            wsplit_done["all"] = [x for x in (pw["h"], pw["l"],
                                              pw["s1"], pw["s2"])
                                  if x is not None]

        p0b_it = p0b_gen()

        def p0b_filler(k):
            for _ in range(k):
                if next(p0b_it, "END") == "END":
                    return

        p2_sp, p2_lastmm, p2_fold = ffn_mm_phase(
            "p2", TX,
            addsrc_fn=lambda o: x_h[o * P:(o + 1) * P, 0:TX],
            dst_fn=lambda o: x1_h[o * P:(o + 1) * P, 0:TX],
            entryA=[p1_fins[1]], entryB=[p1_fins[1]],
            fold=True, filler=p0b_filler)

        mix_spills = []
        xp_spills = []
        kxsp = {}
        vxsp = {}
        rxsp = {}
        _p3split = {}
        _p3sp1 = {}
        _p3sp2 = {}
        for _j in range(3):
            _p3sp1[(-1, _j)] = wsplit_done.get("all", [])
            _p3sp2[(-1, _j)] = wsplit_done.get("all", [])
        prev3 = sq.barrier()
        p2_bar = prev3
        for hi, (t0, W) in enumerate([(0, 1025), (1024, 1025)]
                                     if (LOADTEST is None or LOADTEST >= 3)
                                     else []):
            if hi == 1:
                for _j in range(3):
                    _p3sp1[(-1, _j)] = _p3sp1.get((C - 1, _j), [])
                    _p3sp2[(-1, _j)] = _p3sp2.get((C - 1, _j), [])
                _p3split[-1] = _p3split.get(C - 1, [])
            lds = {c: sq.dma("sync", hbc(c)[:, 0:W],
                             x1_h[c * P:(c + 1) * P, t0:t0 + W],
                             after=list(prev3), tag=f"p3{hi}ld{c}")
                   for c in range(C)}
            be3 = stats_rows(f"p3{hi}", p2_fold[0], t0, W,
                             list(p2_fold[1]) + (list(prev3) if hi
                                                 else []))
            rm2 = rowmath(f"p3{hi}", W, be3)
            s2 = second_rsqrt(f"p3{hi}", W, LN_EPS, rm2)
            rr = sq.op("vector",
                       lambda e, W=W: e.tensor_tensor(R_T1[:, 0:W],
                                                      R_RS[:, 0:W],
                                                      R_RM[:, 0:W],
                                                      ALU.mult),
                       after=[s2], tag=f"p3{hi}rr")

            xn_ops = []
            for c in range(C):
                xsp = SC1 if c % 2 == 0 else SC2

                def p_xp(e, c=c, W=W, xsp=xsp):
                    e.tensor_tensor(hbc(c)[:, 0:W], hbc(c)[:, 0:W],
                                    R_M[:, 0:W], ALU.subtract)
                    return e.tensor_tensor(xsp[:, 0:W], hbc(c)[:, 0:W],
                                           R_RS[:, 0:W], ALU.mult)
                xp = sq.op("gpsimd", p_xp,
                           after=[rr, lds[c]] + ([xp_spills[-2]]
                                                 if c >= 2 else []),
                           tag=f"p3{hi}xp{c}")
                xp_spills.append(sq.dma(
                    "gpsimd", xp_h[c * P:(c + 1) * P, t0:t0 + W],
                    xsp[:, 0:W], after=[xp], tag=f"p3{hi}xsp{c}"))

                def p_xn(e, c=c, W=W, hi=hi):
                    e.tensor_tensor(hbc(c)[:, 0:W], hbc(c)[:, 0:W],
                                    R_T1[:, 0:W], ALU.mult)
                    t = e.tensor_scalar(hbc(c)[:, 0:W], hbc(c)[:, 0:W],
                                        vt["lnsc"][:, c:c + 1],
                                        vt["lnb"][:, c:c + 1],
                                        ALU.mult, ALU.add)
                    if hi == 0:
                        t = e.tensor_single_scalar(hbc(c)[:, 0:1],
                                                   hbc(c)[:, 0:1],
                                                   PMASK[:, 0:1], ALU.mult)
                    return t
                xn_ops.append(sq.op("vector", p_xn, after=[xp],
                                    tag=f"p3{hi}xn{c}"))

            last_mix = xn_ops[-1]
            for c in range(C):
                def p_mix(e, c=c, W=W):
                    e.tensor_tensor(SC3[:, 0:W - 1], hbc(c)[:, 1:W],
                                    hbc(c)[:, 0:W - 1], ALU.subtract)
                    t = None
                    for j, mv in enumerate(("mk", "mv", "mr")):
                        t = e.scalar_tensor_tensor(
                            MIXO[j][:, 0:W - 1], SC3[:, 0:W - 1],
                            vt[mv][:, c:c + 1], hbc(c)[:, 0:W - 1],
                            ALU.mult, ALU.add)
                    return t
                mdep = [last_mix, xn_ops[c]] + _p3split.get(c - 1, [])
                mx = sq.op("vector", p_mix, after=mdep, tag=f"p3{hi}mx{c}")
                last_mix = mx
                _p3split[c] = []
                for j, (hih, loh, spd) in enumerate((
                        (kxhi_h, kxlo_h, kxsp), (vxhi_h, vxlo_h, vxsp),
                        (rxhi_h, rxlo_h, rxsp))):
                    hic = sq.op("scalar",
                                lambda e, j=j, W=W:
                                    e.copy(SPH[j][:, 0:W - 1],
                                           MIXO[j][:, 0:W - 1]),
                                after=[mx] + _p3sp1.get((c - 1, j), []),
                                tag=f"p3{hi}hc{c}_{j}")
                    loc = sq.op("vector",
                                lambda e, j=j, W=W:
                                    e.tensor_tensor(SPL[j][:, 0:W - 1],
                                                    MIXO[j][:, 0:W - 1],
                                                    SPH[j][:, 0:W - 1],
                                                    ALU.subtract),
                                after=[hic] + _p3sp2.get((c - 1, j), []),
                                tag=f"p3{hi}lc{c}_{j}")
                    sp1 = sq.dma(
                        "gpsimd", hih[c * P:(c + 1) * P, t0:t0 + W - 1],
                        SPH[j][:, 0:W - 1], after=[hic],
                        tag=f"p3{hi}s1{c}_{j}")
                    sp2 = sq.dma(
                        "gpsimd", loh[c * P:(c + 1) * P, t0:t0 + W - 1],
                        SPL[j][:, 0:W - 1], after=[loc],
                        tag=f"p3{hi}s2{c}_{j}")
                    mix_spills += [sp1, sp2]
                    _p3sp1[(c, j)] = [sp1]
                    _p3sp2[(c, j)] = [sp2]
                    _p3split[c] += [hic, loc]
                    spd[(hi, c)] = [sp1, sp2]
            prev3 = [last_mix] + mix_spills[-6:]
        p3_bar = sq.barrier()

        def proj_phase(tag, w_hbm, res_dep_fn, res_hbm, entry, make_evac,
                       post_o, split3=None, fold_cb=None, otf=False):
            slt = SL(T)
            ns = len(slt)
            otf_lo = {}
            mmh_last = {}
            res_half = {0: [], 1: []}
            for c in range(C):
                for half in range(2):
                    if split3 is None:
                        res_half[half].append(sq.dma(
                            "sync",
                            A1[:, c * CW + half * TH:
                               c * CW + (half + 1) * TH],
                            res_hbm[c * P:(c + 1) * P,
                                    half * TH:(half + 1) * TH],
                            after=res_dep_fn(c, half),
                            tag=f"{tag}r{c}_{half}"))
                    else:
                        _, _, reshi, reslo = split3
                        dep = res_dep_fn(c, half)
                        res_half[half].append(sq.dma(
                            "sync",
                            A1BF[:, c * 2 * CW + half * TH:
                                 c * 2 * CW + (half + 1) * TH],
                            reshi[c * P:(c + 1) * P,
                                  half * TH:(half + 1) * TH],
                            after=dep, tag=f"{tag}rh{c}_{half}"))
                        res_half[half].append(sq.dma(
                            "sync",
                            A1BF[:, c * 2 * CW + T + half * TH:
                                 c * 2 * CW + T + (half + 1) * TH],
                            reslo[c * P:(c + 1) * P,
                                  half * TH:(half + 1) * TH],
                            after=dep, tag=f"{tag}rl{c}_{half}"))
            evl = {}
            mm_last = {}
            for o in range(C):
                par = o % 2
                wdep = list(entry) if o < 2 else [mm_last[o - 2]]
                if otf:
                    wdep2 = list(entry) if o < 2 else \
                        [otf_lo[o - 2]]
                    wl = sq.dma(
                        "sync",
                        ROWC[par][:, 0:C * P].rearrange(
                            "p (dc oc) -> p dc oc", dc=C),
                        w_hbm.rearrange("(dc p) o -> p dc o", p=P)[
                            :, :, o * P:(o + 1) * P],
                        after=wdep2, tag=f"{tag}w{o}")
                    wcb = WCOL[par].bitcast(BF16)
                    hi = sq.op(
                        "gpsimd",
                        lambda e, par=par, wcb=wcb:
                            e.tensor_copy(wcb[:, 0:C * P],
                                          ROWC[par][:, 0:C * P]),
                        after=[wl] + ([mmh_last[o - 2]] if o >= 2 else []),
                        tag=f"{tag}wh{o}")
                    lo = sq.op(
                        "gpsimd",
                        lambda e, par=par, wcb=wcb:
                            e.tensor_tensor(wcb[:, C * P:2 * C * P],
                                            ROWC[par][:, 0:C * P],
                                            wcb[:, 0:C * P],
                                            ALU.subtract),
                        after=[hi] + ([mm_last[o - 2]] if o >= 2 else []),
                        tag=f"{tag}wo{o}")
                    otf_lo[o] = lo
                    wlds = [hi, lo]
                elif split3 is None:
                    wlds = [sq.dma(
                        "sync",
                        WCOL[par][:, 0:C * P].rearrange(
                            "p (dc oc) -> p dc oc", dc=C),
                        w_hbm.rearrange("(dc p) o -> p dc o", p=P)[
                            :, :, o * P:(o + 1) * P],
                        after=wdep, tag=f"{tag}w{o}")]
                else:
                    whi, wlo = split3[0], split3[1]
                    wcb = WCOL[par].bitcast(BF16)
                    wlds = [
                        sq.dma("sync",
                               wcb[:, 0:C * P].rearrange(
                                   "p (dc oc) -> p dc oc", dc=C),
                               whi.rearrange("(dc p) o -> p dc o", p=P)[
                                   :, :, o * P:(o + 1) * P],
                               after=wdep, tag=f"{tag}wh{o}"),
                        sq.dma("sync",
                               wcb[:, C * P:2 * C * P].rearrange(
                                   "p (dc oc) -> p dc oc", dc=C),
                               wlo.rearrange("(dc p) o -> p dc o", p=P)[
                                   :, :, o * P:(o + 1) * P],
                               after=wdep, tag=f"{tag}wl{o}")]
                for i, (o_, w) in enumerate(slt):
                    g = o * ns + i
                    pb = PB[g % 2]

                    if split3 is None:
                        def p_mm(e, par=par, o_=o_, w=w, pb=pb):
                            mm = None
                            for d in range(C):
                                mm = e.matmul(
                                    pb[:, 0:w],
                                    WCOL[par][:, d * P:(d + 1) * P],
                                    A1[:, d * CW + o_:d * CW + o_ + w],
                                    start=(d == 0), stop=(d == C - 1))
                            return mm
                    else:
                        def p_mm_parts(e, par=par, o_=o_, w=w, pb=pb,
                                       passes=None):
                            wcb = WCOL[par].bitcast(BF16)
                            mm = None
                            for lb, rb, first, last in passes:
                                for d in range(C):
                                    mm = e.matmul(
                                        pb[:, 0:w],
                                        wcb[:, lb + d * P:lb + (d + 1) * P],
                                        A1BF[:, d * 2 * CW + rb + o_:
                                             d * 2 * CW + rb + o_ + w],
                                        start=(first and d == 0),
                                        stop=(last and d == C - 1))
                            return mm
                    if split3 is not None and otf:
                        base = [evl[g - 2]] if g >= 2 else []
                        if o == 0:
                            base = base + res_half[0 if i < 2 else 1]
                        mmh = sq.op(
                            "tensor",
                            lambda e, f=p_mm_parts:
                                f(e, passes=((0, 0, True, False),
                                             (0, T, False, False))),
                            after=[wlds[0]] + base, tag=f"{tag}mh{o}_{i}")
                        mm = sq.op(
                            "tensor",
                            lambda e, f=p_mm_parts:
                                f(e, passes=((C * P, 0, False, True),)),
                            after=[wlds[1], mmh], tag=f"{tag}m{o}_{i}")
                        mmh_last[o] = mmh
                    else:
                        if split3 is not None:
                            def p_mm(e, f=p_mm_parts):
                                return f(e, passes=((0, 0, True, False),
                                                    (0, T, False, False),
                                                    (C * P, 0, False,
                                                     True)))
                        mdep = list(wlds) + ([evl[g - 2]] if g >= 2 else [])
                        if o == 0:
                            mdep += res_half[0 if i < 2 else 1]
                        mm = sq.op("tensor", p_mm, after=mdep,
                                   tag=f"{tag}m{o}_{i}")
                    mm_last[o] = mm
                    evl[g] = make_evac(o, par, i, (o_, w), mm, pb)
                post_o(o, par)
                if fold_cb is not None:
                    fold_cb(o, par)
            return mm_last[C - 1]

        p4_entry = p3_bar
        p4_spills = []

        def p4_evac(o, par, i, sl, mm, pb):
            o_, w = sl
            dep = [mm] + ([p4_spills[o - 2]] if (o >= 2 and i == 0) else [])
            return sq.op("scalar",
                         lambda e, par=par, o_=o_, w=w, pb=pb:
                             e.activation(ROWA[par][:, o_:o_ + w],
                                          pb[:, 0:w], AF.Exp),
                         after=dep, tag=f"p4e{o}_{i}")

        p4_pend = {}

        def p4_post(o, par):
            p4_pend[o] = (sq.last["scalar"], par)
            if o > 0:
                dep, ppar = p4_pend.pop(o - 1)
                p4_spills.append(sq.dma(
                    "gpsimd", ek_h[(o - 1) * P:o * P, 0:T],
                    ROWA[ppar][:, 0:T], after=[dep], tag=f"p4sp{o - 1}"))

        def p4_flush():
            dep, ppar = p4_pend.pop(C - 1)
            p4_spills.append(sq.dma(
                "gpsimd", ek_h[(C - 1) * P:C * P, 0:T],
                ROWA[ppar][:, 0:T], after=[dep], tag=f"p4sp{C - 1}"))

        def p4_res_dep(c, half):
            d = list(kxsp.get((half, c), []))
            d += p2_bar if c < 8 else p3_bar
            return [x for x in d if x is not None]

        p4_lastmm = None
        if LOADTEST is None or LOADTEST >= 4:
            p4_lastmm = proj_phase(
                "p4", wk_h, p4_res_dep, None, p4_entry, p4_evac, p4_post,
                split3=(None, None, kxhi_h, kxlo_h), otf=True)
            p4_flush()

        p5_entry = sq.barrier()
        p5_spills = []

        def p5_res_dep(c, half):
            d = list(vxsp.get((half, c), [])) + [p4_lastmm]
            return [x for x in d if x is not None]
        p5_post = []
        p5_pend = {}
        p5_ekld = {}

        def p5_evac(o, par, i, sl, mm, pb):
            o_, w = sl
            if i == 0:
                p5_ekld[o] = sq.dma(
                    "sync", ROWA[par][:, 0:T], ek_h[o * P:(o + 1) * P, 0:T],
                    after=[p4_spills[o]] + (p5_entry if o < 2
                                            else [p5_post[o - 2]]),
                    tag=f"p5ek{o}")
            dep = [mm] + ([p5_post[o - 2]] if (o >= 2 and i == 0) else [])
            return sq.op("scalar",
                         lambda e, par=par, o_=o_, w=w, pb=pb:
                             e.copy(ROWB[par][:, o_:o_ + w], pb[:, 0:w]),
                         after=dep, tag=f"p5e{o}_{i}")

        def p5_post_fn(o, par):
            def p_ekv(e, par=par):
                return e.tensor_tensor(ROWB[par][:, 0:T], ROWA[par][:, 0:T],
                                       ROWB[par][:, 0:T], ALU.mult)
            ekv = sq.op("vector", p_ekv,
                        after=[p5_ekld[o], sq.last["scalar"]] +
                              ([p5_spills[o - 2]] if o >= 2 else []),
                        tag=f"p5ekv{o}")
            p5_pend[o] = (ekv, par)
            if o > 0:
                pekv, ppar = p5_pend.pop(o - 1)
                p5_spills.append(sq.dma(
                    "gpsimd", ekv_h[(o - 1) * P:o * P, 0:T],
                    ROWB[ppar][:, 0:T], after=[pekv], tag=f"p5sp{o - 1}"))

            lamb = vt["lam"][:, o:o + 1].to_broadcast([P, TH])

            def p_scan0(e, par=par, lamb=lamb):
                e.tensor_tensor_scan(SA[:, 1:TH + 1], lamb,
                                     ROWB[par][:, 0:TH], 0.0,
                                     ALU.mult, ALU.add)
                return e.tensor_tensor_scan(SBt[:, 1:TH + 1], lamb,
                                            ROWA[par][:, 0:TH], 0.0,
                                            ALU.mult, ALU.add)
            s0 = sq.op("vector", p_scan0, after=[ekv], tag=f"p5s0{o}")

            def p_cp01(e):
                e.tensor_copy(SA[:, 0:1], SA[:, TH:TH + 1])
                return e.tensor_copy(SBt[:, 0:1], SBt[:, TH:TH + 1])
            cp01 = sq.op("vector", p_cp01, strict=[s0], tag=f"p5cp{o}")

            def p_scan1(e, par=par, lamb=lamb):
                e.tensor_tensor_scan(SA[:, 1:TH + 1], lamb,
                                     ROWB[par][:, TH:T], SA[:, 0:1],
                                     ALU.mult, ALU.add)
                return e.tensor_tensor_scan(SBt[:, 1:TH + 1], lamb,
                                            ROWA[par][:, TH:T], SBt[:, 0:1],
                                            ALU.mult, ALU.add)
            s1 = sq.op("vector", p_scan1, strict=[cp01], tag=f"p5s1{o}")

            def p_car(e, o=o):
                e.tensor_copy(CARRY[:, o:o + 1], SA[:, TH:TH + 1])
                return e.tensor_copy(CARRY[:, C + o:C + o + 1],
                                     SBt[:, TH:TH + 1])
            p5_post.append(sq.op("vector", p_car, strict=[s1],
                                 tag=f"p5sc{o}"))

        p5_lastmm = None
        if LOADTEST is None or LOADTEST >= 5:
            p5_lastmm = proj_phase(
                "p5", wv_h, p5_res_dep, None, p5_entry, p5_evac, p5_post_fn,
                split3=(None, None, vxhi_h, vxlo_h), otf=True)
            _pekv, _ppar = p5_pend.pop(C - 1)
            p5_spills.append(sq.dma(
                "gpsimd", ekv_h[(C - 1) * P:C * P, 0:T],
                ROWB[_ppar][:, 0:T], after=[_pekv], tag=f"p5sp{C - 1}"))

        p6_entry = sq.barrier()
        cin_sp = sq.dma("gpsimd", cin_h, CARRY[:, :], after=p6_entry,
                        tag="p6in")
        selld = sq.dma("sync", SELM, selmask_h, after=p6_entry, tag="p6sel")

        def p_cc(e):
            return e.collective_compute(
                "AllGather", ALU.bypass,
                replica_groups=[list(range(NCORE))],
                ins=[cin_h.opt()], outs=[cout_h.opt()])
        if NO_CC:
            cc = sq.dma("gpsimd", cout_h[0:P, :], cin_h, after=[cin_sp],
                        tag="p6cc")
        else:
            cc = sq.op("gpsimd", p_cc, after=[cin_sp], tag="p6cc")
        gld = sq.dma("gpsimd",
                     GATH.rearrange("p (r f) -> p r f", r=NCORE),
                     cout_h.rearrange("(r p) f -> p r f", p=P),
                     strict=[cc], tag="p6g")

        f0 = sq.op("vector",
                   lambda e: e.tensor_tensor(GATH, GATH, SELM, ALU.mult),
                   after=[gld, selld], tag="p6f0")
        f1 = sq.op("vector",
                   lambda e: e.tensor_tensor(GATH[:, 0:128], GATH[:, 0:128],
                                             GATH[:, 128:256], ALU.add),
                   strict=[f0], tag="p6f1")
        f2 = sq.op("vector",
                   lambda e: e.tensor_tensor(GATH[:, 0:64], GATH[:, 0:64],
                                             GATH[:, 64:128], ALU.add),
                   strict=[f1], tag="p6f2")
        f3 = sq.op("vector",
                   lambda e: e.tensor_tensor(GATH[:, 0:32], GATH[:, 0:32],
                                             GATH[:, 32:64], ALU.add),
                   strict=[f2], tag="p6f3")
        csel = sq.op("vector",
                     lambda e: e.tensor_copy(CARRY[:, :], GATH[:, 0:32]),
                     strict=[f3], tag="p6csel")
        CIN = CARRY

        if STOP_AT <= 6 or LOADTEST is not None:
            sq.op("gpsimd", lambda e: None, after=sq.barrier(), tag="final")
            sq.finalize()
            with nc.Block() as block:
                @block.sync
                def _(eng):
                    sq.run_engine("sync", eng)

                @block.tensor
                def _(eng):
                    sq.run_engine("tensor", eng)

                @block.vector
                def _(eng):
                    sq.run_engine("vector", eng)

                @block.scalar
                def _(eng):
                    sq.run_engine("scalar", eng)

                @block.gpsimd
                def _(eng):
                    sq.run_engine("gpsimd", eng)
            return nc
        p7_entry = sq.barrier()
        rw_spills = []

        def p7_res_dep(c, half):
            d = list(rxsp.get((half, c), [])) + [p5_lastmm]
            return [x for x in d if x is not None]
        p7_post = []
        p7_lds = {}

        def p7_evac(o, par, i, sl, mm, pb):
            o_, w = sl
            if i == 0:
                ekvld = sq.dma(
                    "sync", ROWA[par][:, 0:T], ekv_h[o * P:(o + 1) * P, 0:T],
                    after=[p5_spills[o]] + (p7_entry if o < 2
                                            else [p7_post[o - 2]]),
                    tag=f"p7ekv{o}")
                ekld = sq.dma(
                    "sync", ROWB[par][:, 0:T], ek_h[o * P:(o + 1) * P, 0:T],
                    after=(p7_entry if o < 2 else [p7_post[o - 2]]),
                    tag=f"p7ek{o}")
                p7_lds[o] = (ekvld, ekld)
            dep = [mm] + ([rw_spills[o - 2]] if (o >= 2 and i == 0) else [])
            return sq.op("scalar",
                         lambda e, par=par, o_=o_, w=w, pb=pb:
                             e.activation(ROWC[par][:, o_:o_ + w],
                                          pb[:, 0:w], AF.Sigmoid),
                         after=dep, tag=f"p7sg{o}_{i}")

        def p7_post_fn(o, par):
            ekvld, ekld = p7_lds[o]
            sgdone = sq.last["scalar"]
            lamb = vt["lam"][:, o:o + 1].to_broadcast([P, TH])
            prev_half = None
            for th in range(2):
                sl = slice(th * TH, (th + 1) * TH)

                def p_cp(e, o=o, th=th):
                    if th == 0:
                        e.tensor_copy(SA[:, 0:1], CIN[:, o:o + 1])
                        return e.tensor_copy(SBt[:, 0:1],
                                             CIN[:, C + o:C + o + 1])
                    e.tensor_copy(SA[:, 0:1], SA[:, TH:TH + 1])
                    return e.tensor_copy(SBt[:, 0:1], SBt[:, TH:TH + 1])
                cpi = sq.op("vector", p_cp,
                            after=[ekvld, ekld, csel],
                            strict=[prev_half], tag=f"p7cp{o}_{th}")

                def p_scan2(e, par=par, sl=sl, lamb=lamb):
                    e.tensor_tensor_scan(SA[:, 1:TH + 1], lamb,
                                         ROWA[par][:, sl], SA[:, 0:1],
                                         ALU.mult, ALU.add)
                    return e.tensor_tensor_scan(SBt[:, 1:TH + 1], lamb,
                                                ROWB[par][:, sl],
                                                SBt[:, 0:1],
                                                ALU.mult, ALU.add)
                sc2 = sq.op("vector", p_scan2, strict=[cpi],
                            tag=f"p7sc{o}_{th}")

                def p_eu(e, par=par, o=o, sl=sl):
                    e.mul(ROWA[par][:, sl], ROWA[par][:, sl],
                          vt["eu"][:, o:o + 1])
                    return e.mul(ROWB[par][:, sl], ROWB[par][:, sl],
                                 vt["eu"][:, o:o + 1])
                eu = sq.op("scalar", p_eu, after=[sc2], tag=f"p7eu{o}_{th}")

                def p_wkv(e, par=par, sl=sl):
                    e.tensor_tensor(ROWA[par][:, sl], ROWA[par][:, sl],
                                    SA[:, 0:TH], ALU.add)
                    e.tensor_tensor(ROWB[par][:, sl], ROWB[par][:, sl],
                                    SBt[:, 0:TH], ALU.add)
                    e.reciprocal(ROWB[par][:, sl], ROWB[par][:, sl])
                    e.tensor_tensor(ROWA[par][:, sl], ROWA[par][:, sl],
                                    ROWB[par][:, sl], ALU.mult)
                    return e.tensor_tensor(ROWC[par][:, sl],
                                           ROWC[par][:, sl],
                                           ROWA[par][:, sl], ALU.mult)
                prev_half = sq.op("vector", p_wkv, after=[eu, sgdone],
                                  tag=f"p7wk{o}_{th}")
            p7_post.append(prev_half)
            rw_spills.append(sq.dma("gpsimd", rw_h[o * P:(o + 1) * P, 0:T],
                                    ROWC[par][:, 0:T], after=[prev_half],
                                    tag=f"p7sp{o}"))

        p7_lastmm = proj_phase(
            "p7", wr_h, p7_res_dep, None, p7_entry, p7_evac, p7_post_fn,
            split3=(wshi_h["wr"], wslo_h["wr"], rxhi_h, rxlo_h))

        p9_entry = sq.barrier()
        a2_spills = []
        p9_evs = {}
        p9_lds = {}

        def p9_evac(o, par, i, sl, mm, pb):
            o_, w = sl
            if i == 0:
                x1ld = sq.dma(
                    "sync", ROWA[par][:, 0:TX],
                    x1_h[o * P:(o + 1) * P, 0:TX],
                    after=(p9_entry if o < 2 else [p9_evs[o - 2]]),
                    tag=f"p9x1{o}")
                xpld = sq.dma(
                    "sync", ROWC[par][:, 0:TX],
                    xp_h[o * P:(o + 1) * P, 0:TX],
                    after=(p9_entry if o < 2 else [p9_evs[o - 2]]),
                    tag=f"p9xp{o}")
                p9_lds[o] = (x1ld, xpld)

            def p_ev(e, par=par, o_=o_, w=w, pb=pb):
                return e.tensor_tensor(ROWB[par][:, o_:o_ + w], pb[:, 0:w],
                                       ROWA[par][:, 1 + o_:1 + o_ + w],
                                       ALU.add)
            dep = [mm, p9_lds[o][0]] + ([a2_spills[o - 2]]
                                        if (o >= 2 and i == 0) else [])
            if o >= 2 and i == 0:
                dep += [d for d in (p9_fold_sqr.get(o - 2),
                                    p9_fold_mm.get(o - 2))
                        if d is not None]
            ev = sq.op("vector", p_ev, after=dep, tag=f"p9e{o}_{i}")

            def p_ev2(e, par=par, o_=o_, w=w):
                return e.tensor_tensor(ROWB[par][:, o_:o_ + w],
                                       ROWB[par][:, o_:o_ + w],
                                       ROWC[par][:, 1 + o_:1 + o_ + w],
                                       ALU.add)
            ev2 = sq.op("gpsimd", p_ev2, after=[ev, p9_lds[o][1]],
                        tag=f"p9g{o}_{i}")
            if i == 3:
                p9_evs[o] = ev2
            return ev

        def p9_post(o, par):
            a2_spills.append(sq.dma("gpsimd", a2_h[o * P:(o + 1) * P, 0:T],
                                    ROWB[par][:, 0:T],
                                    after=[p9_evs[o]], tag=f"p9sp{o}"))

        def p9_res_dep(c, half):
            d = [rw_spills[c] if c < len(rw_spills) else None, p7_lastmm]
            return [x for x in d if x is not None]

        p9_fold_sqr = {}
        p9_fold_mm = {}
        bsT = mk_bounds(0, [w for _, w in SL(T)])

        def p9_fold_cb(o, par):
            if o > 0:
                po = o - 1
                p9_fold_mm[po] = stats_acc(
                    f"p9f{po}",
                    lambda o_, w, pp=po % 2: ROWB[pp][:, o_:o_ + w],
                    lambda o_, w: RB[:, o_:o_ + w],
                    bsT, po == 0, False, [p9_fold_sqr[po]])
            p9_fold_sqr[o] = sq.op(
                "scalar",
                lambda e, par=par: e.activation(RB[:, 0:T],
                                                ROWB[par][:, 0:T],
                                                AF.Square),
                after=[p9_evs[o]] + ([p9_fold_mm[o - 1]] if o > 0 else []),
                tag=f"p9q{o}")

        proj_phase("p9", wo_h, p9_res_dep, rw_h, p9_entry, p9_evac,
                   p9_post, fold_cb=p9_fold_cb)
        p9_fold_mm[C - 1] = stats_acc(
            f"p9f{C - 1}",
            lambda o_, w, pp=(C - 1) % 2: ROWB[pp][:, o_:o_ + w],
            lambda o_, w: RB[:, o_:o_ + w],
            bsT, False, True, [p9_fold_sqr[C - 1]])

        p10_entry = sq.barrier()
        p10_fins = quant_phase("pa", lambda c: a2_h[c * P:(c + 1) * P, :],
                               [(0, 1024), (1024, 1024)], p10_entry,
                               packed=(bsT, [p9_fold_mm[C - 1]]))

        y_spills, _, _ = ffn_mm_phase(
            "pb", T,
            addsrc_fn=lambda o: a2_h[o * P:(o + 1) * P, 0:T],
            dst_fn=lambda o: y_h[o * P:(o + 1) * P, 0:T],
            entryA=[p10_fins[1]], entryB=[p10_fins[1]])

        sq.op("gpsimd", lambda e: None, after=y_spills, tag="final")

        sq.finalize()

        with nc.Block() as block:
            @block.sync
            def _(eng):
                sq.run_engine("sync", eng)

            @block.tensor
            def _(eng):
                sq.run_engine("tensor", eng)

            @block.vector
            def _(eng):
                sq.run_engine("vector", eng)

            @block.scalar
            def _(eng):
                sq.run_engine("scalar", eng)

            @block.gpsimd
            def _(eng):
                sq.run_engine("gpsimd", eng)

    return nc


def _vecfmt(v):
    return np.ascontiguousarray(v.reshape(C, P).T.astype(np.float32))


def prepare_in_maps(**inputs):
    x = np.asarray(inputs["L0_attn_x"], dtype=np.float32)
    Wk = np.ascontiguousarray(np.asarray(inputs["L0_attn_Wk"], np.float32))
    Wv = np.ascontiguousarray(np.asarray(inputs["L0_attn_Wv"], np.float32))
    Wr = np.ascontiguousarray(np.asarray(inputs["L0_attn_Wr"], np.float32))
    Wo = np.ascontiguousarray(np.asarray(inputs["L0_attn_Wo"], np.float32))
    Wt = np.ascontiguousarray(np.asarray(inputs["L0_ffn_W"], np.float32).T)
    vec = {
        "lnsc": _vecfmt(np.asarray(inputs["L0_attn_ln_scale"], np.float32)),
        "lnb": _vecfmt(np.asarray(inputs["L0_attn_ln_bias"], np.float32)),
        "mk": _vecfmt(np.asarray(inputs["L0_attn_mix_k"], np.float32)),
        "mv": _vecfmt(np.asarray(inputs["L0_attn_mix_v"], np.float32)),
        "mr": _vecfmt(np.asarray(inputs["L0_attn_mix_r"], np.float32)),
        "tf": _vecfmt(np.asarray(inputs["L0_attn_time_first"], np.float32)),
        "td": _vecfmt(np.asarray(inputs["L0_attn_time_decay"], np.float32)),
        "gam": _vecfmt(np.asarray(inputs["L0_ffn_gamma"], np.float32)),
    }
    ident = np.eye(P, dtype=np.float32)
    onesrow = np.ones((1, P), np.float32)
    onescol = np.ones((P, 1), np.float32)

    in_maps = []
    for core in range(NCORE):
        b, h = core // 2, core % 2
        t0 = h * T
        xf = np.zeros((D, CW), np.float32)
        xf[:, 1:TX] = x[b, t0:t0 + T, :].T
        if h == 1:
            xf[:, 0] = x[b, t0 - 1, :]
        pmask = np.full((P, 1), float(h), np.float32)
        selmask = np.zeros((P, NCORE, 32), np.float32)
        if h == 1:
            selmask[:, core - 1, :] = 1.0
        m = dict(x=np.ascontiguousarray(xf), wk=Wk, wv=Wv, wr=Wr, wo=Wo,
                 wt=Wt, ident=ident, onesrow=onesrow, onescol=onescol,
                 pmask=pmask,
                 selmask=np.ascontiguousarray(selmask.reshape(P,
                                                              NCORE * 32)))
        m.update(vec)
        in_maps.append(m)
    return in_maps


def kernel(**inputs):
    in_maps = prepare_in_maps(**inputs)
    nc = build_nc()
    res = run_bass_kernel_spmd(nc, in_maps, core_ids=list(range(NCORE)))
    out = np.empty((4, 4096, D), np.float32)
    for core in range(NCORE):
        b, h = core // 2, core % 2
        out[b, h * T:(h + 1) * T, :] = res.results[core]["y"].T
    return out


# version 29
# speedup vs baseline: 1.0163x; 1.0163x over previous
import numpy as np
from contextlib import ExitStack

import concourse.bass as bass
import concourse.mybir as mybir
from concourse.bass_utils import run_bass_kernel_spmd

F32 = mybir.dt.float32
BF16 = mybir.dt.bfloat16
AF = mybir.ActivationFunctionType
ALU = mybir.AluOpType
AX = mybir.AxisListType

D = 2048
C = 16
P = 128
T = 2048
TX = 2049
CW = 2052
HW = 1028
TH = 1024
MAGIC = float(3 << 22)
NCORE = 8

LN_EPS = 1e-5
RMS_EPS = 1e-6
A_EPS = 1e-5
W_EPS = 1e-5

ENGS = ("sync", "tensor", "vector", "scalar", "gpsimd")
DMA_ENGS = ("sync", "scalar", "gpsimd")
DMA_POOL = 6

TAP = ()
NO_CC = False
STOP_AT = 99
LOADTEST = None


def SL(total, start=0, maxw=512):
    out = []
    o = 0
    while o < total:
        w = min(maxw, total - o)
        out.append([start + o, w])
        o += w
    if len(out) >= 2 and out[-1][1] < 64:
        spare = out[-2][1] + out[-1][1]
        out[-2][1] = spare // 2
        out[-1] = [out[-2][0] + spare // 2, spare - spare // 2]
    return [tuple(x) for x in out]


class Op:
    __slots__ = ("eng", "emit", "after", "dma", "ms", "sem", "val", "prev",
                 "tag")

    def __init__(self, eng, emit, after, dma, tag, strict=()):
        self.eng, self.emit, self.dma, self.tag = eng, emit, dma, tag
        self.after = [a for a in after
                      if a is not None and (a.eng != eng or a.dma)]
        self.after += [a for a in strict if a is not None]
        self.ms = False
        self.sem = None
        self.val = None
        self.prev = 0


class Seq:
    def __init__(self, nc, est):
        self.nc = nc
        self.ops = []
        self.last = {e: None for e in ENGS}
        self.isem = {e: est.enter_context(nc.semaphore(f"i_{e}"))
                     for e in ENGS}
        self.icnt = {e: 0 for e in ENGS}
        self.dsem = {e: [est.enter_context(nc.semaphore(f"d_{e}{i}"))
                         for i in range(DMA_POOL)] for e in DMA_ENGS}
        self.dcnt = {e: [0] * DMA_POOL for e in DMA_ENGS}
        self.duse = {e: 0 for e in DMA_ENGS}
        self.dlog = {e: [] for e in DMA_ENGS}

    def op(self, eng, emit, after=(), tag="", strict=()):
        o = Op(eng, emit, after, False, tag, strict)
        self.ops.append(o)
        self.last[eng] = o
        return o

    def dma(self, eng, out, in_, after=(), tag="", strict=()):
        def emit(e, out=out, in_=in_):
            return e.dma_start(out=out, in_=in_)
        o = Op(eng, emit, after, True, tag, strict)
        self.ops.append(o)
        self.last[eng] = o
        self.dlog[eng].append(o)
        return o

    def barrier(self):
        snap = [self.last[e] for e in ENGS if self.last[e] is not None]
        for e in DMA_ENGS:
            snap.extend(self.dlog[e][-DMA_POOL:])
        seen, out = set(), []
        for o in snap:
            if id(o) not in seen:
                seen.add(id(o))
                out.append(o)
        return out

    def finalize(self):
        for o in self.ops:
            for a in o.after:
                a.ms = True
        for o in self.ops:
            e = o.eng
            if o.dma:
                slot = self.duse[e] % DMA_POOL
                self.duse[e] += 1
                o.sem = self.dsem[e][slot]
                o.prev = self.dcnt[e][slot]
                self.dcnt[e][slot] += 16
                o.val = self.dcnt[e][slot]
                o.ms = True
            elif o.ms:
                self.icnt[e] += 1
                o.sem = self.isem[e]
                o.val = self.icnt[e]

    def run_engine(self, eng_name, handle):
        for o in self.ops:
            if o.eng != eng_name:
                continue
            for a in o.after:
                handle.wait_ge(a.sem, a.val)
            if o.dma and o.prev:
                handle.wait_ge(o.sem, o.prev)
            inst = o.emit(handle)
            if o.ms:
                if inst is None:
                    raise RuntimeError(f"milestoned op '{o.tag}' returned "
                                       "no instruction")
                inst.then_inc(o.sem, 16 if o.dma else 1)


def build_nc(tap=TAP):
    nc = bass.Bass("TRN2", target_bir_lowering=False)

    def din(name, shape, dt=F32):
        return nc.dram_tensor(name, shape, dt, kind="ExternalInput").ap()

    def dscr(name, shape, dt=F32):
        if name in tap:
            return nc.dram_tensor(name, shape, dt, kind="ExternalOutput").ap()
        return nc.dram_tensor(name, shape, dt).ap()

    x_h = din("x", [D, CW])
    wk_h = din("wk", [D, D])
    wv_h = din("wv", [D, D])
    wr_h = din("wr", [D, D])
    wo_h = din("wo", [D, D])
    wt_h = din("wt", [D, D])
    vecs_h = {n: din(n, [P, C]) for n in
              ("lnsc", "lnb", "mk", "mv", "mr", "tf", "td", "gam")}
    ident_h = din("ident", [P, P])
    onesrow_h = din("onesrow", [1, P])
    onescol_h = din("onescol", [P, 1])
    pmask_h = din("pmask", [P, 1])
    selmask_h = din("selmask", [P, 256])

    y_h = nc.dram_tensor("y", [D, T], F32, kind="ExternalOutput").ap()

    wiT_h = dscr("wiT", [D, D], BF16)
    x1_h = dscr("x1s", [D, CW])
    xp_h = dscr("xps", [D, CW])
    kxhi_h = dscr("kxhi", [D, T], BF16)
    kxlo_h = dscr("kxlo", [D, T], BF16)
    vxhi_h = dscr("vxhi", [D, T], BF16)
    vxlo_h = dscr("vxlo", [D, T], BF16)
    rxhi_h = dscr("rxhi", [D, T], BF16)
    rxlo_h = dscr("rxlo", [D, T], BF16)
    wshi_h = {w: dscr(w + "hi", [D, D], BF16) for w in ("wk", "wv", "wr")}
    wslo_h = {w: dscr(w + "lo", [D, D], BF16) for w in ("wk", "wv", "wr")}
    ek_h = dscr("eks", [D, T])
    ekv_h = dscr("ekvs", [D, T])
    rw_h = dscr("rws", [D, T])
    a2_h = dscr("a2s", [D, T])
    cin_h = nc.dram_tensor("cin_b", [P, 32], F32).ap()
    cout_h = nc.dram_tensor("cout_b", [NCORE * P, 32], F32).ap()

    est = ExitStack()
    sbt = lambda n, s, dt=F32: est.enter_context(nc.sbuf_tensor(n, s, dt))

    with est:
        A1 = sbt("A1", [P, C * CW])
        HB = sbt("HB", [P, C * HW])
        RB = sbt("RB", [P, 2 * 1028])
        XI = A1[:, 0:(C * CW) // 2].bitcast(BF16)
        xic = lambda c: XI[:, c * CW:(c + 1) * CW]
        SH = 8 * CW

        def shslot(i):
            return A1[:, SH + i * HW:SH + (i + 1) * HW]
        R_S, R_Q, R_M, R_T1, R_T2, R_RS, R_RM, R_A = (shslot(i)
                                                      for i in range(8))
        SC1, SC2 = shslot(8), shslot(9)
        SC12 = A1[:, SH + 8 * HW:SH + 10 * HW]
        SC3 = R_T2
        MIXO = [R_S, R_Q, R_A]
        QS = A1[:, SH + 10 * HW:SH + 10 * HW + CW]
        IDENT = A1[:, SH + 10 * HW + CW:SH + 10 * HW + CW + P]
        SB2 = SH + 10 * HW + CW + P
        W0STG = [A1[:, SB2:SB2 + 1024], A1[:, SB2 + 1024:SB2 + 2048]]
        W0HI = A1[:, SB2 + 2048:SB2 + 3072].bitcast(BF16)
        W0LO = RB[:, 1028:2056].bitcast(BF16)
        SPH = [A1[:, SB2 + j * 512:SB2 + (j + 1) * 512
                  ].bitcast(BF16)[:, 0:TH] for j in range(3)]
        SPL = [A1[:, SB2 + 1536 + j * 512:SB2 + 1536 + (j + 1) * 512
                  ].bitcast(BF16)[:, 0:TH] for j in range(3)]
        A1BF = A1[:, :].bitcast(BF16)
        hbc = lambda c: HB[:, c * HW:(c + 1) * HW]

        def hrow(i):
            return HB[:, i * CW:(i + 1) * CW]
        ROWA = [hrow(0), hrow(1)]
        ROWB = [hrow(2), hrow(3)]
        ROWC = [hrow(4), hrow(5)]
        WCOL = [hrow(6), hrow(7)]
        SA = RB[:, 0:1028]
        SBt = RB[:, 1028:2056]

        VEC = sbt("vec", [P, C * 11])
        vt = {}
        for i, n in enumerate(("lnsc", "lnb", "mk", "mv", "mr", "tf", "td",
                               "gam", "lam", "eu", "scr")):
            vt[n] = VEC[:, i * C:(i + 1) * C]
        ONESC = sbt("onesc", [P, 1])
        PMASK = sbt("pmaskt", [P, 1])
        CARRY = sbt("carry", [P, 32])
        WSC = sbt("wsc", [P, 1])
        RROW = sbt("rrow", [1, 1152])
        s_row = RROW[0:1, 0:512]
        q_row = RROW[0:1, 512:1024]
        a_row = RROW[0:1, 0:512]
        ONESR = RROW[0:1, 1024:1152]
        AMT2 = RB[0:16, 0:P]
        AMTM = RB[:, 128:144]
        GATH = RB[:, 256:512]
        SELM = RB[:, 512:768]

        PB = [est.enter_context(nc.psum_tensor(f"pb{i}", [P, 512], F32))
              for i in range(8)]

        sq = Seq(nc, est)

        ld_vec = []
        for ap_in, tile in [
            (vecs_h["lnsc"], vt["lnsc"]), (vecs_h["lnb"], vt["lnb"]),
            (vecs_h["mk"], vt["mk"]), (vecs_h["mv"], vt["mv"]),
            (vecs_h["mr"], vt["mr"]), (vecs_h["tf"], vt["tf"]),
            (vecs_h["td"], vt["td"]), (vecs_h["gam"], vt["gam"]),
            (onesrow_h, ONESR), (onescol_h, ONESC[:, :]),
            (pmask_h, PMASK[:, :]),
        ]:
            ld_vec.append(sq.dma("sync", tile, ap_in, tag="ld_vec"))
        ld_wt = [sq.dma("sync", A1[:, c * CW:c * CW + D],
                        wt_h[c * P:(c + 1) * P, :], tag=f"ld_wt{c}")
                 for c in range(C)]

        sq.op("scalar", lambda e: e.activation(vt["scr"], vt["td"], AF.Exp),
              after=ld_vec, tag="p0exp1")
        sq.op("scalar", lambda e: e.activation(vt["lam"], vt["scr"], AF.Exp,
                                               scale=-1.0), tag="p0lam")
        lam_done = sq.op("scalar",
                         lambda e: e.activation(vt["eu"], vt["tf"], AF.Exp),
                         tag="p0eu")
        def p0_abs(e):
            t = None
            for c in range(C):
                t = e.tensor_reduce(vt["scr"][:, c:c + 1],
                                    A1[:, c * CW:c * CW + D],
                                    AX.X, ALU.add, apply_absolute_value=True)
            return t
        wabs0 = sq.op("vector", p0_abs, after=ld_wt + [lam_done],
                      tag="p0abs0")
        wabs = sq.op("vector",
                     lambda e: e.tensor_reduce(WSC[:, 0:1], vt["scr"],
                                               AX.X, ALU.add),
                     strict=[wabs0], tag="p0abs")

        wsum = sq.op("tensor",
                     lambda e: e.matmul(PB[0][0:1, 0:1], WSC[:, 0:1],
                                        ONESC[:, :], start=True, stop=True),
                     after=[wabs], tag="p0sum")

        w0 = sq.op("vector",
                   lambda e: e.tensor_copy(RROW[0:1, 0:1], PB[0][0:1, 0:1]),
                   after=[wsum], tag="p0w0")
        w1 = sq.op("vector",
                   lambda e: e.tensor_single_scalar(RROW[0:1, 1:2],
                                                    RROW[0:1, 0:1],
                                                    1.0 / (D * D), ALU.mult),
                   strict=[w0], tag="p0w1")
        w2 = sq.op("vector",
                   lambda e: e.tensor_single_scalar(RROW[0:1, 2:3],
                                                    RROW[0:1, 1:2], W_EPS,
                                                    ALU.add),
                   strict=[w1], tag="p0w2")
        wsrow = sq.op("vector",
                      lambda e: e.reciprocal(RROW[0:1, 3:4], RROW[0:1, 2:3]),
                      strict=[w2], tag="p0wsrow")

        wsbc = sq.op("tensor",
                     lambda e: e.matmul(PB[1][:, 0:1], ONESR,
                                        RROW[0:1, 3:4], start=True,
                                        stop=True),
                     after=[wsrow], tag="p0wsbc")
        wsev = sq.op("vector",
                     lambda e: e.tensor_copy(WSC[:, 0:1], PB[1][:, 0:1]),
                     after=[wsbc], tag="p0wsev")

        wq_sp = {}
        for c in range(C):
            par = c % 2
            buf = hrow(par).bitcast(BF16)[:, 0:D]
            scr = hrow(2)[:, 0:D] if par == 0 else hrow(3)[:, 0:D]

            def p0_q(e, c=c, buf=buf, scr=scr):
                e.tensor_single_scalar(scr, A1[:, c * CW:c * CW + D],
                                       WSC[:, 0:1], ALU.mult)
                e.tensor_scalar(scr, scr, MAGIC, MAGIC, ALU.add,
                                ALU.subtract)
                return e.tensor_scalar(buf, scr, 1.0, -1.0, ALU.min, ALU.max)
            qo = sq.op("vector", p0_q,
                       after=[wq_sp[c - 2]] if c >= 2 else [],
                       strict=[wsev] if c < 2 else [],
                       tag=f"p0q{c}")
            wq_sp[c] = sq.dma(
                "gpsimd",
                wiT_h.rearrange("(ob p) (dc ol) -> p ob dc ol",
                                p=P, ol=P)[:, :, c, :],
                buf.rearrange("p (ob ol) -> p ob ol", ol=P),
                after=[qo], tag=f"p0qsp{c}")
        ph0_done = sq.barrier()

        wsplit_done = {}


        def st_sum(i):
            return (PB[6], 32 * i) if i < 3 else (PB[7], 32 * (i - 3))

        def st_sq(i):
            return (PB[4], 32 * i) if i < 3 else (PB[5], 32 * (i - 3))

        def mk_bounds(t0, widths):
            bs = [t0]
            for w in widths:
                bs.append(bs[-1] + w)
            return bs

        def bpieces(bs, a, b):
            out = []
            g = a
            while g < b:
                i = max(j for j in range(len(bs) - 1) if bs[j] <= g)
                end = min(b, bs[i + 1])
                out.append((i, g - bs[i], g - a, end - g))
                g = end
            return out

        def stats_acc(tag, src_fn, sq_fn, bs, first, last, dep):
            def p_mm(e):
                mm = None
                for i in range(len(bs) - 1):
                    o_, w = bs[i], bs[i + 1] - bs[i]
                    bank, base = st_sum(i)
                    e.matmul(bank[base:base + 1, 0:w], ONESC[:, :],
                             src_fn(o_, w), start=first, stop=last)
                    bank, base = st_sq(i)
                    mm = e.matmul(bank[base:base + 1, 0:w], ONESC[:, :],
                                  sq_fn(o_, w), start=first, stop=last)
                return mm
            return sq.op("tensor", p_mm, after=dep, tag=tag)

        def stats_sweep_cls(tag, src_row_fn, t0, W, entry_fn, order=None):
            bs = mk_bounds(t0, [w for _, w in SL(W)])
            order = order if order is not None else list(range(C))
            lds = {}
            mm_ops = []
            for idx, c in enumerate(order):
                lds[c] = sq.dma("sync", hbc(c)[:, 0:W],
                                src_row_fn(c)[:, t0:t0 + W],
                                after=entry_fn(c), tag=f"{tag}ld{c}")
                xsq = SC1 if idx % 2 == 0 else SC2
                so = sq.op("scalar",
                           lambda e, c=c, xsq=xsq, W=W:
                               e.activation(xsq[:, 0:W], hbc(c)[:, 0:W],
                                            AF.Square),
                           after=[lds[c]] + ([mm_ops[idx - 2]]
                                             if idx >= 2 else []),
                           tag=f"{tag}sq{c}")
                mm_ops.append(stats_acc(
                    f"{tag}mm{c}",
                    lambda o_, w, c=c: hbc(c)[:, o_ - t0:o_ - t0 + w],
                    lambda o_, w, xsq=xsq, t0=t0:
                        xsq[:, o_ - t0:o_ - t0 + w],
                    bs, idx == 0, idx == C - 1, [so]))
            return lds, mm_ops[-1], bs

        def stats_rows(tag, bs, t0, W, entry):
            bc_prev = None
            for i, (o_, w) in enumerate(SL(W)):
                pcs = bpieces(bs, t0 + o_, t0 + o_ + w)

                def p_cp(e, pcs=pcs):
                    t = None
                    for (sl, off, dst, ln) in pcs:
                        bank, base = st_sum(sl)
                        e.copy(s_row[0:1, dst:dst + ln],
                               bank[base:base + 1, off:off + ln])
                        bank, base = st_sq(sl)
                        t = e.copy(q_row[0:1, dst:dst + ln],
                                   bank[base:base + 1, off:off + ln])
                    return t
                cp = sq.op("scalar", p_cp,
                           after=list(entry) + ([bc_prev] if bc_prev
                                                else []),
                           tag=f"{tag}rc{i}")
                mm = sq.op("tensor",
                           lambda e, w=w:
                               (e.matmul(PB[2][:, 0:w], ONESR,
                                         s_row[0:1, 0:w], start=True,
                                         stop=True),
                                e.matmul(PB[3][:, 0:w], ONESR,
                                         q_row[0:1, 0:w], start=True,
                                         stop=True))[-1],
                           after=[cp], tag=f"{tag}bc{i}")
                bc_prev = sq.op("scalar",
                                lambda e, o_=o_, w=w:
                                    (e.copy(R_S[:, o_:o_ + w],
                                            PB[2][:, 0:w]),
                                     e.copy(R_Q[:, o_:o_ + w],
                                            PB[3][:, 0:w]))[-1],
                                after=[mm], tag=f"{tag}be{i}")
            return bc_prev

        def rowmath(tag, W, entry):
            def p_rm1(e, W=W):
                inv_d = 1.0 / D
                e.tensor_single_scalar(R_M[:, 0:W], R_S[:, 0:W], inv_d,
                                       ALU.mult)
                e.tensor_single_scalar(R_Q[:, 0:W], R_Q[:, 0:W], inv_d,
                                       ALU.mult)
                e.tensor_tensor(R_A[:, 0:W], R_M[:, 0:W], R_M[:, 0:W],
                                ALU.mult)
                e.tensor_tensor(R_Q[:, 0:W], R_Q[:, 0:W], R_A[:, 0:W],
                                ALU.subtract)
                return e.tensor_single_scalar(R_T1[:, 0:W], R_Q[:, 0:W],
                                              LN_EPS, ALU.add)
            rm1 = sq.op("vector", p_rm1, after=[entry], tag=f"{tag}rm1")
            sq1 = sq.op("scalar",
                        lambda e, W=W: e.activation(R_T2[:, 0:W],
                                                    R_T1[:, 0:W], AF.Sqrt),
                        after=[rm1], tag=f"{tag}s1")

            def p_rm2(e, W=W):
                e.reciprocal(R_RS[:, 0:W], R_T2[:, 0:W])
                e.tensor_tensor(R_T2[:, 0:W], R_RS[:, 0:W], R_RS[:, 0:W],
                                ALU.mult)
                e.tensor_tensor(R_T2[:, 0:W], R_T2[:, 0:W], R_T1[:, 0:W],
                                ALU.mult)
                e.tensor_scalar(R_T2[:, 0:W], R_T2[:, 0:W], -0.5, 1.5,
                                ALU.mult, ALU.add)
                return e.tensor_tensor(R_RS[:, 0:W], R_RS[:, 0:W],
                                       R_T2[:, 0:W], ALU.mult)
            return sq.op("vector", p_rm2, after=[sq1], tag=f"{tag}rm2")

        def second_rsqrt(tag, W, eps, entry):
            def p_a(e):
                e.reciprocal(R_RM[:, 0:W], R_T1[:, 0:W])
                e.tensor_tensor(R_RM[:, 0:W], R_RM[:, 0:W], R_Q[:, 0:W],
                                ALU.mult)
                return e.tensor_single_scalar(R_T1[:, 0:W], R_RM[:, 0:W],
                                              eps, ALU.add)
            a = sq.op("vector", p_a, after=[entry], tag=f"{tag}r2a")
            s = sq.op("scalar",
                      lambda e: e.activation(R_T2[:, 0:W], R_T1[:, 0:W],
                                             AF.Sqrt),
                      after=[a], tag=f"{tag}r2s")

            def p_b(e):
                e.reciprocal(R_RM[:, 0:W], R_T2[:, 0:W])
                e.tensor_tensor(R_T2[:, 0:W], R_RM[:, 0:W], R_RM[:, 0:W],
                                ALU.mult)
                e.tensor_tensor(R_T2[:, 0:W], R_T2[:, 0:W], R_T1[:, 0:W],
                                ALU.mult)
                e.tensor_scalar(R_T2[:, 0:W], R_T2[:, 0:W], -0.5, 1.5,
                                ALU.mult, ALU.add)
                return e.tensor_tensor(R_RM[:, 0:W], R_RM[:, 0:W],
                                       R_T2[:, 0:W], ALU.mult)
            return sq.op("vector", p_b, after=[s], tag=f"{tag}r2b")

        def quant_phase(tag, src_row_fn, halves, entry,
                        sweep0_entry_fn=None, sweep0_order=None,
                        packed=None):
            idld = sq.dma("sync", IDENT, ident_h, after=entry,
                          tag=f"{tag}id")
            amz = sq.op("vector", lambda e: e.memset(AMTM, 0.0),
                        after=entry, tag=f"{tag}amz")
            prev = entry
            fins = []
            for hi, (t0, W) in enumerate(halves):
                slh = SL(W)
                nblk = (W + P - 1) // P
                if hi == 0 and sweep0_entry_fn is not None:
                    efn = sweep0_entry_fn
                    order = sweep0_order
                else:
                    efn = lambda c, prev=prev: list(prev)
                    order = None
                if packed is None:
                    lds, lastmm, bs = stats_sweep_cls(f"{tag}{hi}",
                                                      src_row_fn, t0, W,
                                                      efn, order)
                    acc = [lastmm]
                else:
                    bs, acc = packed
                    lds = {c: sq.dma("sync", hbc(c)[:, 0:W],
                                     src_row_fn(c)[:, t0:t0 + W],
                                     after=efn(c),
                                     tag=f"{tag}{hi}ld{c}")
                           for c in range(C)}
                be = stats_rows(f"{tag}{hi}", bs, t0, W,
                                list(acc) + (list(prev) if hi else []))
                rm2 = rowmath(f"{tag}{hi}", W, be)
                rmm = second_rsqrt(f"{tag}{hi}", W, RMS_EPS, rm2)

                subg_ops = []
                for c in range(C):
                    eng = "vector"

                    def p_sg(e, c=c, W=W):
                        e.tensor_tensor(hbc(c)[:, 0:W], hbc(c)[:, 0:W],
                                        R_M[:, 0:W], ALU.subtract)
                        return e.tensor_single_scalar(hbc(c)[:, 0:W],
                                                      hbc(c)[:, 0:W],
                                                      vt["gam"][:, c:c + 1],
                                                      ALU.mult)
                    subg_ops.append(sq.op(eng, p_sg, after=[rmm, lds[c]],
                                          tag=f"{tag}{hi}sg{c}"))
                tree_ops = []
                for c in range(C):
                    buf = SC3 if c == 0 else (SC1 if c % 2 == 1 else SC2)
                    adep = [subg_ops[c]] + ([tree_ops[c - 2]]
                                            if c >= 3 else [])
                    ab = sq.op("scalar",
                               lambda e, c=c, buf=buf, W=W:
                                   e.activation(buf[:, 0:W], hbc(c)[:, 0:W],
                                                AF.Abs),
                               after=adep, tag=f"{tag}{hi}ab{c}")
                    if c == 0:
                        tree_ops.append(ab)
                    else:
                        tree_ops.append(sq.op(
                            "vector",
                            lambda e, buf=buf, W=W:
                                e.tensor_tensor(SC3[:, 0:W], SC3[:, 0:W],
                                                buf[:, 0:W], ALU.max),
                            after=[ab], tag=f"{tag}{hi}mx{c}"))
                tree = tree_ops[-1]

                red_ops = []
                for j in range(nblk):
                    w = min(P, W - j * P)
                    pb = PB[j % 2]
                    tr = sq.op("tensor",
                               lambda e, pb=pb, j=j, w=w:
                                   e.transpose(pb[0:w, 0:P],
                                               SC3[:, j * P:j * P + w],
                                               IDENT),
                               after=[tree, idld] + ([red_ops[j - 2]]
                                                     if j >= 2 else []),
                               tag=f"{tag}{hi}tp{j}")
                    red_ops.append(sq.op(
                        "vector",
                        lambda e, pb=pb, j=j, w=w:
                            e.tensor_reduce(AMTM[0:w, j:j + 1], pb[0:w, 0:P],
                                            AX.X, ALU.max),
                        after=[tr, amz], tag=f"{tag}{hi}rd{j}"))
                tr2 = sq.op("tensor",
                            lambda e: e.transpose(PB[2][0:16, 0:P],
                                                  AMTM, IDENT),
                            after=red_ops, tag=f"{tag}{hi}tp2")
                amev = sq.op("vector",
                             lambda e: e.tensor_copy(AMT2,
                                                     PB[2][0:16, 0:P]),
                             after=[tr2], tag=f"{tag}{hi}ame")
                bc_ev = None
                for i, (o_, w) in enumerate(slh):
                    jb = o_ // P
                    nfull = w // P
                    rem = w - nfull * P
                    adep = [amev] + ([bc_ev] if bc_ev else [])
                    ads = []
                    if nfull:
                        ads.append(sq.dma(
                            "gpsimd", a_row[0:1, 0:nfull * P],
                            AMT2[jb:jb + nfull, :], after=adep,
                            tag=f"{tag}{hi}ad{i}"))
                    if rem:
                        ads.append(sq.dma(
                            "gpsimd",
                            a_row[0:1, nfull * P:nfull * P + rem],
                            AMT2[jb + nfull:jb + nfull + 1, 0:rem],
                            after=adep, tag=f"{tag}{hi}ar{i}"))
                    mm = sq.op("tensor",
                               lambda e, w=w:
                                   e.matmul(PB[3][:, 0:w], ONESR,
                                            a_row[0:1, 0:w], start=True,
                                            stop=True),
                               after=ads + ([bc_ev] if bc_ev else []),
                               tag=f"{tag}{hi}am{i}")
                    bc_ev = sq.op("scalar",
                                  lambda e, o_=o_, w=w:
                                      e.copy(R_A[:, o_:o_ + w],
                                             PB[3][:, 0:w]),
                                  after=[mm], tag=f"{tag}{hi}ae{i}")

                def p_finrow(e, W=W, t0=t0):
                    e.tensor_tensor(R_T1[:, 0:W], R_RS[:, 0:W], R_RM[:, 0:W],
                                    ALU.mult)
                    e.tensor_tensor(R_A[:, 0:W], R_A[:, 0:W], R_T1[:, 0:W],
                                    ALU.mult)
                    e.tensor_single_scalar(R_A[:, 0:W], R_A[:, 0:W], A_EPS,
                                           ALU.add)
                    e.reciprocal(R_A[:, 0:W], R_A[:, 0:W])
                    e.tensor_single_scalar(R_A[:, 0:W], R_A[:, 0:W], 127.0,
                                           ALU.mult)
                    e.tensor_single_scalar(R_T2[:, 0:W], R_A[:, 0:W],
                                           WSC[:, 0:1], ALU.mult)
                    e.reciprocal(R_T2[:, 0:W], R_T2[:, 0:W])
                    e.tensor_single_scalar(QS[:, t0:t0 + W], R_T2[:, 0:W],
                                           0.5, ALU.mult)
                    return e.tensor_tensor(R_T1[:, 0:W], R_T1[:, 0:W],
                                           R_A[:, 0:W], ALU.mult)
                finrow = sq.op("vector", p_finrow, after=[bc_ev],
                               tag=f"{tag}{hi}fr")
                fin = None
                for c in range(C):
                    def p_fx(e, c=c, W=W, t0=t0):
                        e.tensor_tensor(hbc(c)[:, 0:W], hbc(c)[:, 0:W],
                                        R_T1[:, 0:W], ALU.mult)
                        return e.tensor_scalar(xic(c)[:, t0:t0 + W],
                                               hbc(c)[:, 0:W], MAGIC,
                                               MAGIC, ALU.add,
                                               ALU.subtract)
                    fin = sq.op("vector", p_fx, after=[finrow],
                                tag=f"{tag}{hi}fx{c}")
                fins.append(fin)
                prev = [fin]
            return fins

        if LOADTEST is not None and LOADTEST < 1:
            p1_fins = [ph0_done[0], ph0_done[0]]
        else:
            p1_fins = quant_phase(
                "p1", lambda c: x_h[c * P:(c + 1) * P, :],
                [(0, 1024), (1024, 1025)], ph0_done,
                sweep0_entry_fn=lambda c: ([] if c >= 8
                                           else list(ph0_done)),
                sweep0_order=list(range(8, 16)) + list(range(8)))
        if "xid" in tap:
            xid_h = nc.dram_tensor("xid", [D, CW], BF16,
                                   kind="ExternalOutput").ap()
            qsd_h = nc.dram_tensor("qsd", [P, CW], F32,
                                   kind="ExternalOutput").ap()
            for c in range(C):
                sq.dma("gpsimd", xid_h[c * P:(c + 1) * P, :], xic(c),
                       after=[p1_fins[-1]], tag=f"xid{c}")
            sq.dma("gpsimd", qsd_h, QS, after=[p1_fins[-1]], tag="qsd")

        def ffn_mm_phase(tag, tx, addsrc_fn, dst_fn, entryA, entryB,
                         fold=False, filler=None):
            slt = SL(tx)
            bsf = mk_bounds(0, [w for _, w in slt])
            ns = len(slt)
            ev_flat = []
            sp_ops = []
            mm_flat = []
            ev2_last = {}
            fsqr = {}
            ffmm = {}
            for o in range(C):
                par = o % 2
                wdep = list(entryA) if o < 2 else \
                    [mm_flat[(o - 2) * ns + ns - 1]]
                wld = sq.dma(
                    "sync",
                    WCOL[par].bitcast(BF16)[:, 0:C * P],
                    wiT_h[o * P:(o + 1) * P, :],
                    after=wdep, tag=f"{tag}w{o}")
                xdep = list(entryA) if o < 2 else [ev2_last[o - 2]]
                xld = sq.dma("sync", ROWA[par][:, 0:tx], addsrc_fn(o),
                             after=xdep, tag=f"{tag}x{o}")
                ev2 = None
                for i, (o_, w) in enumerate(slt):
                    g = o * ns + i
                    pb = PB[g % 2]

                    def p_mm(e, par=par, o_=o_, w=w, pb=pb):
                        mm = None
                        for d in range(C):
                            mm = e.matmul(
                                pb[:, 0:w],
                                WCOL[par].bitcast(BF16)[:,
                                                        d * P:(d + 1) * P],
                                xic(d)[:, o_:o_ + w],
                                start=(d == 0), stop=(d == C - 1))
                        return mm
                    mdep = [wld] + ([ev_flat[g - 2]] if g >= 2 else [])
                    if o == 0:
                        mdep += list(entryA) if (o_ + w <= 1024) else \
                            list(entryB)
                    mm = sq.op("tensor", p_mm, after=mdep,
                               tag=f"{tag}m{o}_{i}")
                    mm_flat.append(mm)

                    def p_ev(e, par=par, o_=o_, w=w, pb=pb):
                        return e.tensor_tensor(ROWB[par][:, o_:o_ + w],
                                               pb[:, 0:w],
                                               QS[:, o_:o_ + w], ALU.mult)
                    edep = [mm] + ([sp_ops[o - 2]]
                                   if (o >= 2 and i == 0) else [])
                    if fold and o >= 2 and i == 0:
                        edep += [fsqr[o - 2], ffmm.get(o - 2)]
                        edep = [d for d in edep if d is not None]
                    ev = sq.op("vector", p_ev, after=edep,
                               tag=f"{tag}e{o}_{i}")
                    ev_flat.append(ev)

                    def p_ev2(e, par=par, o_=o_, w=w):
                        return e.tensor_tensor(ROWB[par][:, o_:o_ + w],
                                               ROWB[par][:, o_:o_ + w],
                                               ROWA[par][:, o_:o_ + w],
                                               ALU.add)
                    ev2 = sq.op("gpsimd", p_ev2, after=[ev, xld],
                                tag=f"{tag}g{o}_{i}")
                ev2_last[o] = ev2
                dst = dst_fn(o)
                sp_ops.append(sq.dma("gpsimd", dst, ROWB[par][:, 0:tx],
                                     after=[ev2], tag=f"{tag}s{o}"))
                if fold:
                    if o > 0:
                        po = o - 1
                        ffmm[po] = stats_acc(
                            f"{tag}f{po}",
                            lambda o_, w, pp=po % 2:
                                ROWB[pp][:, o_:o_ + w],
                            lambda o_, w: SC12[:, o_:o_ + w],
                            bsf, po == 0, False, [fsqr[po]])
                    fsqr[o] = sq.op(
                        "scalar",
                        lambda e, par=par, tx=tx:
                            e.activation(SC12[:, 0:tx], ROWB[par][:, 0:tx],
                                         AF.Square),
                        after=[ev2] + ([ffmm[o - 1]] if o > 0 else []),
                        tag=f"{tag}q{o}")
                if filler is not None:
                    filler(6)
            fold_out = None
            if fold:
                ffmm[C - 1] = stats_acc(
                    f"{tag}f{C - 1}",
                    lambda o_, w, pp=(C - 1) % 2: ROWB[pp][:, o_:o_ + w],
                    lambda o_, w: SC12[:, o_:o_ + w],
                    bsf, False, True, [fsqr[C - 1]])
                fold_out = (bsf, [ffmm[C - 1]])
            if filler is not None:
                filler(1000)
            return sp_ops, mm_flat[-1], fold_out

        def p0b_gen():
            pw = {"h": None, "l": None, "s1": None, "s2": None}
            qs = ["sync", "scalar"]
            pi = 0
            for wname, w_src in (("wr", wr_h),):
                for c in range(C):
                    sps = []
                    for h in range(2):
                        p = pi % 2
                        dep = [x for x in (pw["h"], pw["l"])
                               if x is not None] or list(ph0_done)
                        wl = sq.dma(qs[p], W0STG[p],
                                    w_src[c * P:(c + 1) * P,
                                          h * 1024:(h + 1) * 1024],
                                    after=dep, tag=f"b{wname}l{c}_{h}")
                        hi = sq.op(
                            "scalar",
                            lambda e, p=p, h=h:
                                e.copy(W0HI[:, h * 1024:(h + 1) * 1024],
                                       W0STG[p]),
                            after=[x for x in
                                   (wl, pw["l"], pw["s1"])
                                   if x is not None],
                            tag=f"b{wname}h{c}_{h}")
                        lo = sq.op(
                            "vector",
                            lambda e, p=p, h=h:
                                e.tensor_tensor(
                                    W0LO[:, h * 1024:(h + 1) * 1024],
                                    W0STG[p],
                                    W0HI[:, h * 1024:(h + 1) * 1024],
                                    ALU.subtract),
                            after=[x for x in (hi, pw["s2"])
                                   if x is not None],
                            tag=f"b{wname}o{c}_{h}")
                        pw["h"] = hi
                        pw["l"] = lo
                        pi += 1
                        if h == 1:
                            s1 = sq.dma(
                                "gpsimd",
                                wshi_h[wname][c * P:(c + 1) * P, :],
                                W0HI, after=[pw["h"], hi],
                                tag=f"b{wname}s{c}")
                            s2 = sq.dma(
                                "gpsimd",
                                wslo_h[wname][c * P:(c + 1) * P, :],
                                W0LO[:, 0:2048], after=[lo],
                                tag=f"b{wname}t{c}")
                            pw["s1"] = s1
                            pw["s2"] = s2
                        yield
            wsplit_done["all"] = [x for x in (pw["h"], pw["l"],
                                              pw["s1"], pw["s2"])
                                  if x is not None]

        p0b_it = p0b_gen()

        def p0b_filler(k):
            for _ in range(k):
                if next(p0b_it, "END") == "END":
                    return

        p2_sp, p2_lastmm, p2_fold = ffn_mm_phase(
            "p2", TX,
            addsrc_fn=lambda o: x_h[o * P:(o + 1) * P, 0:TX],
            dst_fn=lambda o: x1_h[o * P:(o + 1) * P, 0:TX],
            entryA=[p1_fins[1]], entryB=[p1_fins[1]],
            fold=True, filler=p0b_filler)

        mix_spills = []
        xp_spills = []
        kxsp = {}
        vxsp = {}
        rxsp = {}
        _p3split = {}
        _p3sp1 = {}
        _p3sp2 = {}
        for _j in range(3):
            _p3sp1[(-1, _j)] = wsplit_done.get("all", [])
            _p3sp2[(-1, _j)] = wsplit_done.get("all", [])
        prev3 = sq.barrier()
        p2_bar = prev3
        for hi, (t0, W) in enumerate([(0, 1025), (1024, 1025)]
                                     if (LOADTEST is None or LOADTEST >= 3)
                                     else []):
            if hi == 1:
                for _j in range(3):
                    _p3sp1[(-1, _j)] = _p3sp1.get((C - 1, _j), [])
                    _p3sp2[(-1, _j)] = _p3sp2.get((C - 1, _j), [])
                _p3split[-1] = _p3split.get(C - 1, [])
            lds = {c: sq.dma("sync", hbc(c)[:, 0:W],
                             x1_h[c * P:(c + 1) * P, t0:t0 + W],
                             after=list(prev3), tag=f"p3{hi}ld{c}")
                   for c in range(C)}
            be3 = stats_rows(f"p3{hi}", p2_fold[0], t0, W,
                             list(p2_fold[1]) + (list(prev3) if hi
                                                 else []))
            rm2 = rowmath(f"p3{hi}", W, be3)
            s2 = second_rsqrt(f"p3{hi}", W, LN_EPS, rm2)
            rr = sq.op("vector",
                       lambda e, W=W: e.tensor_tensor(R_T1[:, 0:W],
                                                      R_RS[:, 0:W],
                                                      R_RM[:, 0:W],
                                                      ALU.mult),
                       after=[s2], tag=f"p3{hi}rr")

            xn_ops = []
            for c in range(C):
                xsp = SC1 if c % 2 == 0 else SC2

                def p_xp(e, c=c, W=W, xsp=xsp):
                    e.tensor_tensor(hbc(c)[:, 0:W], hbc(c)[:, 0:W],
                                    R_M[:, 0:W], ALU.subtract)
                    return e.tensor_tensor(xsp[:, 0:W], hbc(c)[:, 0:W],
                                           R_RS[:, 0:W], ALU.mult)
                xp = sq.op("gpsimd", p_xp,
                           after=[rr, lds[c]] + ([xp_spills[-2]]
                                                 if c >= 2 else []),
                           tag=f"p3{hi}xp{c}")
                xp_spills.append(sq.dma(
                    "gpsimd", xp_h[c * P:(c + 1) * P, t0:t0 + W],
                    xsp[:, 0:W], after=[xp], tag=f"p3{hi}xsp{c}"))

                def p_xn(e, c=c, W=W, hi=hi):
                    e.tensor_tensor(hbc(c)[:, 0:W], hbc(c)[:, 0:W],
                                    R_T1[:, 0:W], ALU.mult)
                    t = e.tensor_scalar(hbc(c)[:, 0:W], hbc(c)[:, 0:W],
                                        vt["lnsc"][:, c:c + 1],
                                        vt["lnb"][:, c:c + 1],
                                        ALU.mult, ALU.add)
                    if hi == 0:
                        t = e.tensor_single_scalar(hbc(c)[:, 0:1],
                                                   hbc(c)[:, 0:1],
                                                   PMASK[:, 0:1], ALU.mult)
                    return t
                xn_ops.append(sq.op("vector", p_xn, after=[xp],
                                    tag=f"p3{hi}xn{c}"))

            last_mix = xn_ops[-1]
            for c in range(C):
                def p_mix(e, c=c, W=W):
                    e.tensor_tensor(SC3[:, 0:W - 1], hbc(c)[:, 1:W],
                                    hbc(c)[:, 0:W - 1], ALU.subtract)
                    t = None
                    for j, mv in enumerate(("mk", "mv", "mr")):
                        t = e.scalar_tensor_tensor(
                            MIXO[j][:, 0:W - 1], SC3[:, 0:W - 1],
                            vt[mv][:, c:c + 1], hbc(c)[:, 0:W - 1],
                            ALU.mult, ALU.add)
                    return t
                mdep = [last_mix, xn_ops[c]] + _p3split.get(c - 1, [])
                mx = sq.op("vector", p_mix, after=mdep, tag=f"p3{hi}mx{c}")
                last_mix = mx
                _p3split[c] = []
                for j, (hih, loh, spd) in enumerate((
                        (kxhi_h, kxlo_h, kxsp), (vxhi_h, vxlo_h, vxsp),
                        (rxhi_h, rxlo_h, rxsp))):
                    hic = sq.op("scalar",
                                lambda e, j=j, W=W:
                                    e.copy(SPH[j][:, 0:W - 1],
                                           MIXO[j][:, 0:W - 1]),
                                after=[mx] + _p3sp1.get((c - 1, j), []),
                                tag=f"p3{hi}hc{c}_{j}")
                    loc = sq.op("vector",
                                lambda e, j=j, W=W:
                                    e.tensor_tensor(SPL[j][:, 0:W - 1],
                                                    MIXO[j][:, 0:W - 1],
                                                    SPH[j][:, 0:W - 1],
                                                    ALU.subtract),
                                after=[hic] + _p3sp2.get((c - 1, j), []),
                                tag=f"p3{hi}lc{c}_{j}")
                    sp1 = sq.dma(
                        "gpsimd", hih[c * P:(c + 1) * P, t0:t0 + W - 1],
                        SPH[j][:, 0:W - 1], after=[hic],
                        tag=f"p3{hi}s1{c}_{j}")
                    sp2 = sq.dma(
                        "gpsimd", loh[c * P:(c + 1) * P, t0:t0 + W - 1],
                        SPL[j][:, 0:W - 1], after=[loc],
                        tag=f"p3{hi}s2{c}_{j}")
                    mix_spills += [sp1, sp2]
                    _p3sp1[(c, j)] = [sp1]
                    _p3sp2[(c, j)] = [sp2]
                    _p3split[c] += [hic, loc]
                    spd[(hi, c)] = [sp1, sp2]
            prev3 = [last_mix] + mix_spills[-6:]
        p3_bar = sq.barrier()

        def proj_phase(tag, w_hbm, res_dep_fn, res_hbm, entry, make_evac,
                       post_o, split3=None, fold_cb=None, otf=False):
            slt = SL(T)
            ns = len(slt)
            otf_lo = {}
            mmh_last = {}
            res_half = {0: [], 1: []}
            for c in range(C):
                for half in range(2):
                    if split3 is None:
                        res_half[half].append(sq.dma(
                            "sync",
                            A1[:, c * CW + half * TH:
                               c * CW + (half + 1) * TH],
                            res_hbm[c * P:(c + 1) * P,
                                    half * TH:(half + 1) * TH],
                            after=res_dep_fn(c, half),
                            tag=f"{tag}r{c}_{half}"))
                    else:
                        _, _, reshi, reslo = split3
                        dep = res_dep_fn(c, half)
                        res_half[half].append(sq.dma(
                            "sync",
                            A1BF[:, c * 2 * CW + half * TH:
                                 c * 2 * CW + (half + 1) * TH],
                            reshi[c * P:(c + 1) * P,
                                  half * TH:(half + 1) * TH],
                            after=dep, tag=f"{tag}rh{c}_{half}"))
                        res_half[half].append(sq.dma(
                            "sync",
                            A1BF[:, c * 2 * CW + T + half * TH:
                                 c * 2 * CW + T + (half + 1) * TH],
                            reslo[c * P:(c + 1) * P,
                                  half * TH:(half + 1) * TH],
                            after=dep, tag=f"{tag}rl{c}_{half}"))
            evl = {}
            mm_last = {}
            for o in range(C):
                par = o % 2
                wdep = list(entry) if o < 2 else [mm_last[o - 2]]
                if otf:
                    wdep2 = list(entry) if o < 2 else \
                        [otf_lo[o - 2]]
                    wl = sq.dma(
                        "sync",
                        ROWC[par][:, 0:C * P].rearrange(
                            "p (dc oc) -> p dc oc", dc=C),
                        w_hbm.rearrange("(dc p) o -> p dc o", p=P)[
                            :, :, o * P:(o + 1) * P],
                        after=wdep2, tag=f"{tag}w{o}")
                    wcb = WCOL[par].bitcast(BF16)
                    hi = sq.op(
                        "gpsimd",
                        lambda e, par=par, wcb=wcb:
                            e.tensor_copy(wcb[:, 0:C * P],
                                          ROWC[par][:, 0:C * P]),
                        after=[wl] + ([mmh_last[o - 2]] if o >= 2 else []),
                        tag=f"{tag}wh{o}")
                    lo = sq.op(
                        "gpsimd",
                        lambda e, par=par, wcb=wcb:
                            e.tensor_tensor(wcb[:, C * P:2 * C * P],
                                            ROWC[par][:, 0:C * P],
                                            wcb[:, 0:C * P],
                                            ALU.subtract),
                        after=[hi] + ([mm_last[o - 2]] if o >= 2 else []),
                        tag=f"{tag}wo{o}")
                    otf_lo[o] = lo
                    wlds = [hi, lo]
                elif split3 is None:
                    wlds = [sq.dma(
                        "sync",
                        WCOL[par][:, 0:C * P].rearrange(
                            "p (dc oc) -> p dc oc", dc=C),
                        w_hbm.rearrange("(dc p) o -> p dc o", p=P)[
                            :, :, o * P:(o + 1) * P],
                        after=wdep, tag=f"{tag}w{o}")]
                else:
                    whi, wlo = split3[0], split3[1]
                    wcb = WCOL[par].bitcast(BF16)
                    wlds = [
                        sq.dma("sync",
                               wcb[:, 0:C * P].rearrange(
                                   "p (dc oc) -> p dc oc", dc=C),
                               whi.rearrange("(dc p) o -> p dc o", p=P)[
                                   :, :, o * P:(o + 1) * P],
                               after=wdep, tag=f"{tag}wh{o}"),
                        sq.dma("sync",
                               wcb[:, C * P:2 * C * P].rearrange(
                                   "p (dc oc) -> p dc oc", dc=C),
                               wlo.rearrange("(dc p) o -> p dc o", p=P)[
                                   :, :, o * P:(o + 1) * P],
                               after=wdep, tag=f"{tag}wl{o}")]
                for i, (o_, w) in enumerate(slt):
                    g = o * ns + i
                    pb = PB[g % 2]

                    if split3 is None:
                        def p_mm(e, par=par, o_=o_, w=w, pb=pb):
                            mm = None
                            for d in range(C):
                                mm = e.matmul(
                                    pb[:, 0:w],
                                    WCOL[par][:, d * P:(d + 1) * P],
                                    A1[:, d * CW + o_:d * CW + o_ + w],
                                    start=(d == 0), stop=(d == C - 1))
                            return mm
                    else:
                        def p_mm_parts(e, par=par, o_=o_, w=w, pb=pb,
                                       passes=None):
                            wcb = WCOL[par].bitcast(BF16)
                            mm = None
                            for lb, rb, first, last in passes:
                                for d in range(C):
                                    mm = e.matmul(
                                        pb[:, 0:w],
                                        wcb[:, lb + d * P:lb + (d + 1) * P],
                                        A1BF[:, d * 2 * CW + rb + o_:
                                             d * 2 * CW + rb + o_ + w],
                                        start=(first and d == 0),
                                        stop=(last and d == C - 1))
                            return mm
                    if split3 is not None and otf:
                        base = [evl[g - 2]] if g >= 2 else []
                        if o == 0:
                            base = base + res_half[0 if i < 2 else 1]
                        mmh = sq.op(
                            "tensor",
                            lambda e, f=p_mm_parts:
                                f(e, passes=((0, 0, True, False),
                                             (0, T, False, False))),
                            after=[wlds[0]] + base, tag=f"{tag}mh{o}_{i}")
                        mm = sq.op(
                            "tensor",
                            lambda e, f=p_mm_parts:
                                f(e, passes=((C * P, 0, False, True),)),
                            after=[wlds[1], mmh], tag=f"{tag}m{o}_{i}")
                        mmh_last[o] = mmh
                    else:
                        if split3 is not None:
                            def p_mm(e, f=p_mm_parts):
                                return f(e, passes=((0, 0, True, False),
                                                    (0, T, False, False),
                                                    (C * P, 0, False,
                                                     True)))
                        mdep = list(wlds) + ([evl[g - 2]] if g >= 2 else [])
                        if o == 0:
                            mdep += res_half[0 if i < 2 else 1]
                        mm = sq.op("tensor", p_mm, after=mdep,
                                   tag=f"{tag}m{o}_{i}")
                    mm_last[o] = mm
                    evl[g] = make_evac(o, par, i, (o_, w), mm, pb)
                post_o(o, par)
                if fold_cb is not None:
                    fold_cb(o, par)
            return mm_last[C - 1]

        p4_entry = p3_bar
        p4_spills = []

        def p4_evac(o, par, i, sl, mm, pb):
            o_, w = sl
            dep = [mm] + ([p4_spills[o - 2]] if (o >= 2 and i == 0) else [])
            return sq.op("scalar",
                         lambda e, par=par, o_=o_, w=w, pb=pb:
                             e.activation(ROWA[par][:, o_:o_ + w],
                                          pb[:, 0:w], AF.Exp),
                         after=dep, tag=f"p4e{o}_{i}")

        p4_pend = {}

        def p4_post(o, par):
            p4_pend[o] = (sq.last["scalar"], par)
            if o > 0:
                dep, ppar = p4_pend.pop(o - 1)
                p4_spills.append(sq.dma(
                    "gpsimd", ek_h[(o - 1) * P:o * P, 0:T],
                    ROWA[ppar][:, 0:T], after=[dep], tag=f"p4sp{o - 1}"))

        def p4_flush():
            dep, ppar = p4_pend.pop(C - 1)
            p4_spills.append(sq.dma(
                "gpsimd", ek_h[(C - 1) * P:C * P, 0:T],
                ROWA[ppar][:, 0:T], after=[dep], tag=f"p4sp{C - 1}"))

        def p4_res_dep(c, half):
            d = list(kxsp.get((half, c), []))
            d += p2_bar if c < 8 else p3_bar
            return [x for x in d if x is not None]

        p4_lastmm = None
        if LOADTEST is None or LOADTEST >= 4:
            p4_lastmm = proj_phase(
                "p4", wk_h, p4_res_dep, None, p4_entry, p4_evac, p4_post,
                split3=(None, None, kxhi_h, kxlo_h), otf=True)
            p4_flush()

        p5_entry = sq.barrier()
        p5_spills = []

        def p5_res_dep(c, half):
            d = list(vxsp.get((half, c), [])) + [p4_lastmm]
            return [x for x in d if x is not None]
        p5_post = []
        p5_pend = {}
        p5_ekld = {}

        def p5_evac(o, par, i, sl, mm, pb):
            o_, w = sl
            if i == 0:
                p5_ekld[o] = sq.dma(
                    "sync", ROWA[par][:, 0:T], ek_h[o * P:(o + 1) * P, 0:T],
                    after=[p4_spills[o]] + (p5_entry if o < 2
                                            else [p5_post[o - 2]]),
                    tag=f"p5ek{o}")
            dep = [mm] + ([p5_post[o - 2]] if (o >= 2 and i == 0) else [])
            return sq.op("scalar",
                         lambda e, par=par, o_=o_, w=w, pb=pb:
                             e.copy(ROWB[par][:, o_:o_ + w], pb[:, 0:w]),
                         after=dep, tag=f"p5e{o}_{i}")

        def p5_post_fn(o, par):
            def p_ekv(e, par=par):
                return e.tensor_tensor(ROWB[par][:, 0:T], ROWA[par][:, 0:T],
                                       ROWB[par][:, 0:T], ALU.mult)
            ekv = sq.op("vector", p_ekv,
                        after=[p5_ekld[o], sq.last["scalar"]] +
                              ([p5_spills[o - 2]] if o >= 2 else []),
                        tag=f"p5ekv{o}")
            p5_pend[o] = (ekv, par)
            if o > 0:
                pekv, ppar = p5_pend.pop(o - 1)
                p5_spills.append(sq.dma(
                    "gpsimd", ekv_h[(o - 1) * P:o * P, 0:T],
                    ROWB[ppar][:, 0:T], after=[pekv], tag=f"p5sp{o - 1}"))

            lamb = vt["lam"][:, o:o + 1].to_broadcast([P, TH])

            def p_scan0(e, par=par, lamb=lamb):
                e.tensor_tensor_scan(SA[:, 1:TH + 1], lamb,
                                     ROWB[par][:, 0:TH], 0.0,
                                     ALU.mult, ALU.add)
                return e.tensor_tensor_scan(SBt[:, 1:TH + 1], lamb,
                                            ROWA[par][:, 0:TH], 0.0,
                                            ALU.mult, ALU.add)
            s0 = sq.op("vector", p_scan0, after=[ekv], tag=f"p5s0{o}")

            def p_cp01(e):
                e.tensor_copy(SA[:, 0:1], SA[:, TH:TH + 1])
                return e.tensor_copy(SBt[:, 0:1], SBt[:, TH:TH + 1])
            cp01 = sq.op("vector", p_cp01, strict=[s0], tag=f"p5cp{o}")

            def p_scan1(e, par=par, lamb=lamb):
                e.tensor_tensor_scan(SA[:, 1:TH + 1], lamb,
                                     ROWB[par][:, TH:T], SA[:, 0:1],
                                     ALU.mult, ALU.add)
                return e.tensor_tensor_scan(SBt[:, 1:TH + 1], lamb,
                                            ROWA[par][:, TH:T], SBt[:, 0:1],
                                            ALU.mult, ALU.add)
            s1 = sq.op("vector", p_scan1, strict=[cp01], tag=f"p5s1{o}")

            def p_car(e, o=o):
                e.tensor_copy(CARRY[:, o:o + 1], SA[:, TH:TH + 1])
                return e.tensor_copy(CARRY[:, C + o:C + o + 1],
                                     SBt[:, TH:TH + 1])
            p5_post.append(sq.op("vector", p_car, strict=[s1],
                                 tag=f"p5sc{o}"))

        p5_lastmm = None
        if LOADTEST is None or LOADTEST >= 5:
            p5_lastmm = proj_phase(
                "p5", wv_h, p5_res_dep, None, p5_entry, p5_evac, p5_post_fn,
                split3=(None, None, vxhi_h, vxlo_h), otf=True)
            _pekv, _ppar = p5_pend.pop(C - 1)
            p5_spills.append(sq.dma(
                "gpsimd", ekv_h[(C - 1) * P:C * P, 0:T],
                ROWB[_ppar][:, 0:T], after=[_pekv], tag=f"p5sp{C - 1}"))

        p6_entry = sq.barrier()
        cin_sp = sq.dma("gpsimd", cin_h, CARRY[:, :], after=p6_entry,
                        tag="p6in")
        selld = sq.dma("sync", SELM, selmask_h, after=p6_entry, tag="p6sel")

        def p_cc(e):
            return e.collective_compute(
                "AllGather", ALU.bypass,
                replica_groups=[list(range(NCORE))],
                ins=[cin_h.opt()], outs=[cout_h.opt()])
        if NO_CC:
            cc = sq.dma("gpsimd", cout_h[0:P, :], cin_h, after=[cin_sp],
                        tag="p6cc")
        else:
            cc = sq.op("gpsimd", p_cc, after=[cin_sp], tag="p6cc")
        gld = sq.dma("gpsimd",
                     GATH.rearrange("p (r f) -> p r f", r=NCORE),
                     cout_h.rearrange("(r p) f -> p r f", p=P),
                     strict=[cc], tag="p6g")

        f0 = sq.op("vector",
                   lambda e: e.tensor_tensor(GATH, GATH, SELM, ALU.mult),
                   after=[gld, selld], tag="p6f0")
        f1 = sq.op("vector",
                   lambda e: e.tensor_tensor(GATH[:, 0:128], GATH[:, 0:128],
                                             GATH[:, 128:256], ALU.add),
                   strict=[f0], tag="p6f1")
        f2 = sq.op("vector",
                   lambda e: e.tensor_tensor(GATH[:, 0:64], GATH[:, 0:64],
                                             GATH[:, 64:128], ALU.add),
                   strict=[f1], tag="p6f2")
        f3 = sq.op("vector",
                   lambda e: e.tensor_tensor(GATH[:, 0:32], GATH[:, 0:32],
                                             GATH[:, 32:64], ALU.add),
                   strict=[f2], tag="p6f3")
        csel = sq.op("vector",
                     lambda e: e.tensor_copy(CARRY[:, :], GATH[:, 0:32]),
                     strict=[f3], tag="p6csel")
        CIN = CARRY

        if STOP_AT <= 6 or LOADTEST is not None:
            sq.op("gpsimd", lambda e: None, after=sq.barrier(), tag="final")
            sq.finalize()
            with nc.Block() as block:
                @block.sync
                def _(eng):
                    sq.run_engine("sync", eng)

                @block.tensor
                def _(eng):
                    sq.run_engine("tensor", eng)

                @block.vector
                def _(eng):
                    sq.run_engine("vector", eng)

                @block.scalar
                def _(eng):
                    sq.run_engine("scalar", eng)

                @block.gpsimd
                def _(eng):
                    sq.run_engine("gpsimd", eng)
            return nc
        p7_entry = sq.barrier()
        rw_spills = []

        def p7_res_dep(c, half):
            d = list(rxsp.get((half, c), [])) + [p5_lastmm]
            return [x for x in d if x is not None]
        p7_post = []
        p7_lds = {}

        def p7_evac(o, par, i, sl, mm, pb):
            o_, w = sl
            if i == 0:
                ekvld = sq.dma(
                    "sync", ROWA[par][:, 0:T], ekv_h[o * P:(o + 1) * P, 0:T],
                    after=[p5_spills[o]] + (p7_entry if o < 2
                                            else [p7_post[o - 2]]),
                    tag=f"p7ekv{o}")
                ekld = sq.dma(
                    "sync", ROWB[par][:, 0:T], ek_h[o * P:(o + 1) * P, 0:T],
                    after=(p7_entry if o < 2 else [p7_post[o - 2]]),
                    tag=f"p7ek{o}")
                p7_lds[o] = (ekvld, ekld)
            dep = [mm] + ([rw_spills[o - 2]] if (o >= 2 and i == 0) else [])
            return sq.op("scalar",
                         lambda e, par=par, o_=o_, w=w, pb=pb:
                             e.activation(ROWC[par][:, o_:o_ + w],
                                          pb[:, 0:w], AF.Sigmoid),
                         after=dep, tag=f"p7sg{o}_{i}")

        def p7_post_fn(o, par):
            ekvld, ekld = p7_lds[o]
            sgdone = sq.last["scalar"]
            lamb = vt["lam"][:, o:o + 1].to_broadcast([P, TH])
            prev_half = None
            for th in range(2):
                sl = slice(th * TH, (th + 1) * TH)

                def p_cp(e, o=o, th=th):
                    if th == 0:
                        e.tensor_copy(SA[:, 0:1], CIN[:, o:o + 1])
                        return e.tensor_copy(SBt[:, 0:1],
                                             CIN[:, C + o:C + o + 1])
                    e.tensor_copy(SA[:, 0:1], SA[:, TH:TH + 1])
                    return e.tensor_copy(SBt[:, 0:1], SBt[:, TH:TH + 1])
                cpi = sq.op("vector", p_cp,
                            after=[ekvld, ekld, csel],
                            strict=[prev_half], tag=f"p7cp{o}_{th}")

                def p_scan2(e, par=par, sl=sl, lamb=lamb):
                    e.tensor_tensor_scan(SA[:, 1:TH + 1], lamb,
                                         ROWA[par][:, sl], SA[:, 0:1],
                                         ALU.mult, ALU.add)
                    return e.tensor_tensor_scan(SBt[:, 1:TH + 1], lamb,
                                                ROWB[par][:, sl],
                                                SBt[:, 0:1],
                                                ALU.mult, ALU.add)
                sc2 = sq.op("vector", p_scan2, strict=[cpi],
                            tag=f"p7sc{o}_{th}")

                def p_eu(e, par=par, o=o, sl=sl):
                    e.mul(ROWA[par][:, sl], ROWA[par][:, sl],
                          vt["eu"][:, o:o + 1])
                    return e.mul(ROWB[par][:, sl], ROWB[par][:, sl],
                                 vt["eu"][:, o:o + 1])
                eu = sq.op("scalar", p_eu, after=[sc2], tag=f"p7eu{o}_{th}")

                def p_num(e, par=par, sl=sl):
                    return e.tensor_tensor(ROWA[par][:, sl],
                                           ROWA[par][:, sl],
                                           SA[:, 0:TH], ALU.add)
                gnum = sq.op("gpsimd", p_num, after=[eu],
                             tag=f"p7nm{o}_{th}")

                def p_den(e, par=par, sl=sl):
                    e.tensor_tensor(ROWB[par][:, sl], ROWB[par][:, sl],
                                    SBt[:, 0:TH], ALU.add)
                    return e.reciprocal(ROWB[par][:, sl], ROWB[par][:, sl])
                vden = sq.op("vector", p_den, after=[eu],
                             tag=f"p7dn{o}_{th}")

                def p_wkv(e, par=par, sl=sl):
                    e.tensor_tensor(ROWA[par][:, sl], ROWA[par][:, sl],
                                    ROWB[par][:, sl], ALU.mult)
                    return e.tensor_tensor(ROWC[par][:, sl],
                                           ROWC[par][:, sl],
                                           ROWA[par][:, sl], ALU.mult)
                prev_half = sq.op("gpsimd", p_wkv,
                                  after=[gnum, vden, sgdone],
                                  tag=f"p7wk{o}_{th}")
            p7_post.append(prev_half)
            rw_spills.append(sq.dma("gpsimd", rw_h[o * P:(o + 1) * P, 0:T],
                                    ROWC[par][:, 0:T], after=[prev_half],
                                    tag=f"p7sp{o}"))

        p7_lastmm = proj_phase(
            "p7", wr_h, p7_res_dep, None, p7_entry, p7_evac, p7_post_fn,
            split3=(wshi_h["wr"], wslo_h["wr"], rxhi_h, rxlo_h))

        p9_entry = sq.barrier()
        a2_spills = []
        p9_evs = {}
        p9_lds = {}

        def p9_evac(o, par, i, sl, mm, pb):
            o_, w = sl
            if i == 0:
                x1ld = sq.dma(
                    "sync", ROWA[par][:, 0:TX],
                    x1_h[o * P:(o + 1) * P, 0:TX],
                    after=(p9_entry if o < 2 else [p9_evs[o - 2]]),
                    tag=f"p9x1{o}")
                xpld = sq.dma(
                    "sync", ROWC[par][:, 0:TX],
                    xp_h[o * P:(o + 1) * P, 0:TX],
                    after=(p9_entry if o < 2 else [p9_evs[o - 2]]),
                    tag=f"p9xp{o}")
                p9_lds[o] = (x1ld, xpld)

            def p_ev(e, par=par, o_=o_, w=w, pb=pb):
                return e.tensor_tensor(ROWB[par][:, o_:o_ + w], pb[:, 0:w],
                                       ROWA[par][:, 1 + o_:1 + o_ + w],
                                       ALU.add)
            dep = [mm, p9_lds[o][0]] + ([a2_spills[o - 2]]
                                        if (o >= 2 and i == 0) else [])
            if o >= 2 and i == 0:
                dep += [d for d in (p9_fold_sqr.get(o - 2),
                                    p9_fold_mm.get(o - 2))
                        if d is not None]
            ev = sq.op("vector", p_ev, after=dep, tag=f"p9e{o}_{i}")

            def p_ev2(e, par=par, o_=o_, w=w):
                return e.tensor_tensor(ROWB[par][:, o_:o_ + w],
                                       ROWB[par][:, o_:o_ + w],
                                       ROWC[par][:, 1 + o_:1 + o_ + w],
                                       ALU.add)
            ev2 = sq.op("gpsimd", p_ev2, after=[ev, p9_lds[o][1]],
                        tag=f"p9g{o}_{i}")
            if i == 3:
                p9_evs[o] = ev2
            return ev

        def p9_post(o, par):
            a2_spills.append(sq.dma("gpsimd", a2_h[o * P:(o + 1) * P, 0:T],
                                    ROWB[par][:, 0:T],
                                    after=[p9_evs[o]], tag=f"p9sp{o}"))

        def p9_res_dep(c, half):
            d = [rw_spills[c] if c < len(rw_spills) else None, p7_lastmm]
            return [x for x in d if x is not None]

        p9_fold_sqr = {}
        p9_fold_mm = {}
        bsT = mk_bounds(0, [w for _, w in SL(T)])

        def p9_fold_cb(o, par):
            if o > 0:
                po = o - 1
                p9_fold_mm[po] = stats_acc(
                    f"p9f{po}",
                    lambda o_, w, pp=po % 2: ROWB[pp][:, o_:o_ + w],
                    lambda o_, w: RB[:, o_:o_ + w],
                    bsT, po == 0, False, [p9_fold_sqr[po]])
            p9_fold_sqr[o] = sq.op(
                "scalar",
                lambda e, par=par: e.activation(RB[:, 0:T],
                                                ROWB[par][:, 0:T],
                                                AF.Square),
                after=[p9_evs[o]] + ([p9_fold_mm[o - 1]] if o > 0 else []),
                tag=f"p9q{o}")

        proj_phase("p9", wo_h, p9_res_dep, rw_h, p9_entry, p9_evac,
                   p9_post, fold_cb=p9_fold_cb)
        p9_fold_mm[C - 1] = stats_acc(
            f"p9f{C - 1}",
            lambda o_, w, pp=(C - 1) % 2: ROWB[pp][:, o_:o_ + w],
            lambda o_, w: RB[:, o_:o_ + w],
            bsT, False, True, [p9_fold_sqr[C - 1]])

        p10_entry = sq.barrier()
        p10_fins = quant_phase("pa", lambda c: a2_h[c * P:(c + 1) * P, :],
                               [(0, 1024), (1024, 1024)], p10_entry,
                               packed=(bsT, [p9_fold_mm[C - 1]]))

        y_spills, _, _ = ffn_mm_phase(
            "pb", T,
            addsrc_fn=lambda o: a2_h[o * P:(o + 1) * P, 0:T],
            dst_fn=lambda o: y_h[o * P:(o + 1) * P, 0:T],
            entryA=[p10_fins[1]], entryB=[p10_fins[1]])

        sq.op("gpsimd", lambda e: None, after=y_spills, tag="final")

        sq.finalize()

        with nc.Block() as block:
            @block.sync
            def _(eng):
                sq.run_engine("sync", eng)

            @block.tensor
            def _(eng):
                sq.run_engine("tensor", eng)

            @block.vector
            def _(eng):
                sq.run_engine("vector", eng)

            @block.scalar
            def _(eng):
                sq.run_engine("scalar", eng)

            @block.gpsimd
            def _(eng):
                sq.run_engine("gpsimd", eng)

    return nc


def _vecfmt(v):
    return np.ascontiguousarray(v.reshape(C, P).T.astype(np.float32))


def prepare_in_maps(**inputs):
    x = np.asarray(inputs["L0_attn_x"], dtype=np.float32)
    Wk = np.ascontiguousarray(np.asarray(inputs["L0_attn_Wk"], np.float32))
    Wv = np.ascontiguousarray(np.asarray(inputs["L0_attn_Wv"], np.float32))
    Wr = np.ascontiguousarray(np.asarray(inputs["L0_attn_Wr"], np.float32))
    Wo = np.ascontiguousarray(np.asarray(inputs["L0_attn_Wo"], np.float32))
    Wt = np.ascontiguousarray(np.asarray(inputs["L0_ffn_W"], np.float32).T)
    vec = {
        "lnsc": _vecfmt(np.asarray(inputs["L0_attn_ln_scale"], np.float32)),
        "lnb": _vecfmt(np.asarray(inputs["L0_attn_ln_bias"], np.float32)),
        "mk": _vecfmt(np.asarray(inputs["L0_attn_mix_k"], np.float32)),
        "mv": _vecfmt(np.asarray(inputs["L0_attn_mix_v"], np.float32)),
        "mr": _vecfmt(np.asarray(inputs["L0_attn_mix_r"], np.float32)),
        "tf": _vecfmt(np.asarray(inputs["L0_attn_time_first"], np.float32)),
        "td": _vecfmt(np.asarray(inputs["L0_attn_time_decay"], np.float32)),
        "gam": _vecfmt(np.asarray(inputs["L0_ffn_gamma"], np.float32)),
    }
    ident = np.eye(P, dtype=np.float32)
    onesrow = np.ones((1, P), np.float32)
    onescol = np.ones((P, 1), np.float32)

    in_maps = []
    for core in range(NCORE):
        b, h = core // 2, core % 2
        t0 = h * T
        xf = np.zeros((D, CW), np.float32)
        xf[:, 1:TX] = x[b, t0:t0 + T, :].T
        if h == 1:
            xf[:, 0] = x[b, t0 - 1, :]
        pmask = np.full((P, 1), float(h), np.float32)
        selmask = np.zeros((P, NCORE, 32), np.float32)
        if h == 1:
            selmask[:, core - 1, :] = 1.0
        m = dict(x=np.ascontiguousarray(xf), wk=Wk, wv=Wv, wr=Wr, wo=Wo,
                 wt=Wt, ident=ident, onesrow=onesrow, onescol=onescol,
                 pmask=pmask,
                 selmask=np.ascontiguousarray(selmask.reshape(P,
                                                              NCORE * 32)))
        m.update(vec)
        in_maps.append(m)
    return in_maps


def kernel(**inputs):
    in_maps = prepare_in_maps(**inputs)
    nc = build_nc()
    res = run_bass_kernel_spmd(nc, in_maps, core_ids=list(range(NCORE)))
    out = np.empty((4, 4096, D), np.float32)
    for core in range(NCORE):
        b, h = core // 2, core % 2
        out[b, h * T:(h + 1) * T, :] = res.results[core]["y"].T
    return out


# version 30 (final)
# speedup vs baseline: 1.0182x; 1.0019x over previous
import numpy as np
from contextlib import ExitStack

import concourse.bass as bass
import concourse.mybir as mybir
from concourse.bass_utils import run_bass_kernel_spmd

F32 = mybir.dt.float32
BF16 = mybir.dt.bfloat16
AF = mybir.ActivationFunctionType
ALU = mybir.AluOpType
AX = mybir.AxisListType

D = 2048
C = 16
P = 128
T = 2048
TX = 2049
CW = 2052
HW = 1028
TH = 1024
MAGIC = float(3 << 22)
NCORE = 8

LN_EPS = 1e-5
RMS_EPS = 1e-6
A_EPS = 1e-5
W_EPS = 1e-5

ENGS = ("sync", "tensor", "vector", "scalar", "gpsimd")
DMA_ENGS = ("sync", "scalar", "gpsimd")
DMA_POOL = 6

TAP = ()
NO_CC = False
STOP_AT = 99
LOADTEST = None


def SL(total, start=0, maxw=512):
    out = []
    o = 0
    while o < total:
        w = min(maxw, total - o)
        out.append([start + o, w])
        o += w
    if len(out) >= 2 and out[-1][1] < 64:
        spare = out[-2][1] + out[-1][1]
        out[-2][1] = spare // 2
        out[-1] = [out[-2][0] + spare // 2, spare - spare // 2]
    return [tuple(x) for x in out]


class Op:
    __slots__ = ("eng", "emit", "after", "dma", "ms", "sem", "val", "prev",
                 "tag")

    def __init__(self, eng, emit, after, dma, tag, strict=()):
        self.eng, self.emit, self.dma, self.tag = eng, emit, dma, tag
        self.after = [a for a in after
                      if a is not None and (a.eng != eng or a.dma)]
        self.after += [a for a in strict if a is not None]
        self.ms = False
        self.sem = None
        self.val = None
        self.prev = 0


class Seq:
    def __init__(self, nc, est):
        self.nc = nc
        self.ops = []
        self.last = {e: None for e in ENGS}
        self.isem = {e: est.enter_context(nc.semaphore(f"i_{e}"))
                     for e in ENGS}
        self.icnt = {e: 0 for e in ENGS}
        self.dsem = {e: [est.enter_context(nc.semaphore(f"d_{e}{i}"))
                         for i in range(DMA_POOL)] for e in DMA_ENGS}
        self.dcnt = {e: [0] * DMA_POOL for e in DMA_ENGS}
        self.duse = {e: 0 for e in DMA_ENGS}
        self.dlog = {e: [] for e in DMA_ENGS}

    def op(self, eng, emit, after=(), tag="", strict=()):
        o = Op(eng, emit, after, False, tag, strict)
        self.ops.append(o)
        self.last[eng] = o
        return o

    def dma(self, eng, out, in_, after=(), tag="", strict=()):
        def emit(e, out=out, in_=in_):
            return e.dma_start(out=out, in_=in_)
        o = Op(eng, emit, after, True, tag, strict)
        self.ops.append(o)
        self.last[eng] = o
        self.dlog[eng].append(o)
        return o

    def barrier(self):
        snap = [self.last[e] for e in ENGS if self.last[e] is not None]
        for e in DMA_ENGS:
            snap.extend(self.dlog[e][-DMA_POOL:])
        seen, out = set(), []
        for o in snap:
            if id(o) not in seen:
                seen.add(id(o))
                out.append(o)
        return out

    def finalize(self):
        for o in self.ops:
            for a in o.after:
                a.ms = True
        for o in self.ops:
            e = o.eng
            if o.dma:
                slot = self.duse[e] % DMA_POOL
                self.duse[e] += 1
                o.sem = self.dsem[e][slot]
                o.prev = self.dcnt[e][slot]
                self.dcnt[e][slot] += 16
                o.val = self.dcnt[e][slot]
                o.ms = True
            elif o.ms:
                self.icnt[e] += 1
                o.sem = self.isem[e]
                o.val = self.icnt[e]

    def run_engine(self, eng_name, handle):
        for o in self.ops:
            if o.eng != eng_name:
                continue
            for a in o.after:
                handle.wait_ge(a.sem, a.val)
            if o.dma and o.prev:
                handle.wait_ge(o.sem, o.prev)
            inst = o.emit(handle)
            if o.ms:
                if inst is None:
                    raise RuntimeError(f"milestoned op '{o.tag}' returned "
                                       "no instruction")
                inst.then_inc(o.sem, 16 if o.dma else 1)


def build_nc(tap=TAP):
    nc = bass.Bass("TRN2", target_bir_lowering=False)

    def din(name, shape, dt=F32):
        return nc.dram_tensor(name, shape, dt, kind="ExternalInput").ap()

    def dscr(name, shape, dt=F32):
        if name in tap:
            return nc.dram_tensor(name, shape, dt, kind="ExternalOutput").ap()
        return nc.dram_tensor(name, shape, dt).ap()

    x_h = din("x", [D, CW])
    wk_h = din("wk", [D, D])
    wv_h = din("wv", [D, D])
    wr_h = din("wr", [D, D])
    wo_h = din("wo", [D, D])
    wt_h = din("wt", [D, D])
    vecs_h = {n: din(n, [P, C]) for n in
              ("lnsc", "lnb", "mk", "mv", "mr", "tf", "td", "gam")}
    ident_h = din("ident", [P, P])
    onesrow_h = din("onesrow", [1, P])
    onescol_h = din("onescol", [P, 1])
    pmask_h = din("pmask", [P, 1])
    selmask_h = din("selmask", [P, 256])

    y_h = nc.dram_tensor("y", [D, T], F32, kind="ExternalOutput").ap()

    wiT_h = dscr("wiT", [D, D], BF16)
    x1_h = dscr("x1s", [D, CW])
    xp_h = dscr("xps", [D, CW])
    kxhi_h = dscr("kxhi", [D, T], BF16)
    kxlo_h = dscr("kxlo", [D, T], BF16)
    vxhi_h = dscr("vxhi", [D, T], BF16)
    vxlo_h = dscr("vxlo", [D, T], BF16)
    rxhi_h = dscr("rxhi", [D, T], BF16)
    rxlo_h = dscr("rxlo", [D, T], BF16)
    wshi_h = {w: dscr(w + "hi", [D, D], BF16) for w in ("wk", "wv", "wr")}
    wslo_h = {w: dscr(w + "lo", [D, D], BF16) for w in ("wk", "wv", "wr")}
    ek_h = dscr("eks", [D, T])
    ekv_h = dscr("ekvs", [D, T])
    rw_h = dscr("rws", [D, T])
    a2_h = dscr("a2s", [D, T])
    cin_h = nc.dram_tensor("cin_b", [P, 32], F32).ap()
    cout_h = nc.dram_tensor("cout_b", [NCORE * P, 32], F32).ap()

    est = ExitStack()
    sbt = lambda n, s, dt=F32: est.enter_context(nc.sbuf_tensor(n, s, dt))

    with est:
        A1 = sbt("A1", [P, C * CW])
        HB = sbt("HB", [P, C * HW])
        RB = sbt("RB", [P, 2 * 1028])
        XI = A1[:, 0:(C * CW) // 2].bitcast(BF16)
        xic = lambda c: XI[:, c * CW:(c + 1) * CW]
        SH = 8 * CW

        def shslot(i):
            return A1[:, SH + i * HW:SH + (i + 1) * HW]
        R_S, R_Q, R_M, R_T1, R_T2, R_RS, R_RM, R_A = (shslot(i)
                                                      for i in range(8))
        SC1, SC2 = shslot(8), shslot(9)
        SC12 = A1[:, SH + 8 * HW:SH + 10 * HW]
        SC3 = R_T2
        MIXO = [R_S, R_Q, R_A]
        QS = A1[:, SH + 10 * HW:SH + 10 * HW + CW]
        IDENT = A1[:, SH + 10 * HW + CW:SH + 10 * HW + CW + P]
        SB2 = SH + 10 * HW + CW + P
        W0STG = [A1[:, SB2:SB2 + 1024], A1[:, SB2 + 1024:SB2 + 2048]]
        W0HI = A1[:, SB2 + 2048:SB2 + 3072].bitcast(BF16)
        W0LO = RB[:, 1028:2056].bitcast(BF16)
        SPH = [A1[:, SB2 + j * 512:SB2 + (j + 1) * 512
                  ].bitcast(BF16)[:, 0:TH] for j in range(3)]
        SPL = [A1[:, SB2 + 1536 + j * 512:SB2 + 1536 + (j + 1) * 512
                  ].bitcast(BF16)[:, 0:TH] for j in range(3)]
        A1BF = A1[:, :].bitcast(BF16)
        hbc = lambda c: HB[:, c * HW:(c + 1) * HW]

        def hrow(i):
            return HB[:, i * CW:(i + 1) * CW]
        ROWA = [hrow(0), hrow(1)]
        ROWB = [hrow(2), hrow(3)]
        ROWC = [hrow(4), hrow(5)]
        WCOL = [hrow(6), hrow(7)]
        SA = RB[:, 0:1028]
        SBt = RB[:, 1028:2056]

        VEC = sbt("vec", [P, C * 11])
        vt = {}
        for i, n in enumerate(("lnsc", "lnb", "mk", "mv", "mr", "tf", "td",
                               "gam", "lam", "eu", "scr")):
            vt[n] = VEC[:, i * C:(i + 1) * C]
        ONESC = sbt("onesc", [P, 1])
        PMASK = sbt("pmaskt", [P, 1])
        CARRY = sbt("carry", [P, 32])
        WSC = sbt("wsc", [P, 1])
        RROW = sbt("rrow", [1, 1152])
        s_row = RROW[0:1, 0:512]
        q_row = RROW[0:1, 512:1024]
        a_row = RROW[0:1, 0:512]
        ONESR = RROW[0:1, 1024:1152]
        AMT2 = RB[0:16, 0:P]
        AMTM = RB[:, 128:144]
        GATH = RB[:, 256:512]
        SELM = RB[:, 512:768]

        PB = [est.enter_context(nc.psum_tensor(f"pb{i}", [P, 512], F32))
              for i in range(8)]

        sq = Seq(nc, est)

        ld_vec = []
        for ap_in, tile in [
            (vecs_h["lnsc"], vt["lnsc"]), (vecs_h["lnb"], vt["lnb"]),
            (vecs_h["mk"], vt["mk"]), (vecs_h["mv"], vt["mv"]),
            (vecs_h["mr"], vt["mr"]), (vecs_h["tf"], vt["tf"]),
            (vecs_h["td"], vt["td"]), (vecs_h["gam"], vt["gam"]),
            (onesrow_h, ONESR), (onescol_h, ONESC[:, :]),
            (pmask_h, PMASK[:, :]),
        ]:
            ld_vec.append(sq.dma("sync", tile, ap_in, tag="ld_vec"))
        ld_wt = [sq.dma("sync", A1[:, c * CW:c * CW + D],
                        wt_h[c * P:(c + 1) * P, :], tag=f"ld_wt{c}")
                 for c in range(C)]

        sq.op("scalar", lambda e: e.activation(vt["scr"], vt["td"], AF.Exp),
              after=ld_vec, tag="p0exp1")
        sq.op("scalar", lambda e: e.activation(vt["lam"], vt["scr"], AF.Exp,
                                               scale=-1.0), tag="p0lam")
        lam_done = sq.op("scalar",
                         lambda e: e.activation(vt["eu"], vt["tf"], AF.Exp),
                         tag="p0eu")
        def p0_abs(e):
            t = None
            for c in range(C):
                t = e.tensor_reduce(vt["scr"][:, c:c + 1],
                                    A1[:, c * CW:c * CW + D],
                                    AX.X, ALU.add, apply_absolute_value=True)
            return t
        wabs0 = sq.op("vector", p0_abs, after=ld_wt + [lam_done],
                      tag="p0abs0")
        wabs = sq.op("vector",
                     lambda e: e.tensor_reduce(WSC[:, 0:1], vt["scr"],
                                               AX.X, ALU.add),
                     strict=[wabs0], tag="p0abs")

        wsum = sq.op("tensor",
                     lambda e: e.matmul(PB[0][0:1, 0:1], WSC[:, 0:1],
                                        ONESC[:, :], start=True, stop=True),
                     after=[wabs], tag="p0sum")

        w0 = sq.op("vector",
                   lambda e: e.tensor_copy(RROW[0:1, 0:1], PB[0][0:1, 0:1]),
                   after=[wsum], tag="p0w0")
        w1 = sq.op("vector",
                   lambda e: e.tensor_single_scalar(RROW[0:1, 1:2],
                                                    RROW[0:1, 0:1],
                                                    1.0 / (D * D), ALU.mult),
                   strict=[w0], tag="p0w1")
        w2 = sq.op("vector",
                   lambda e: e.tensor_single_scalar(RROW[0:1, 2:3],
                                                    RROW[0:1, 1:2], W_EPS,
                                                    ALU.add),
                   strict=[w1], tag="p0w2")
        wsrow = sq.op("vector",
                      lambda e: e.reciprocal(RROW[0:1, 3:4], RROW[0:1, 2:3]),
                      strict=[w2], tag="p0wsrow")

        wsbc = sq.op("tensor",
                     lambda e: e.matmul(PB[1][:, 0:1], ONESR,
                                        RROW[0:1, 3:4], start=True,
                                        stop=True),
                     after=[wsrow], tag="p0wsbc")
        wsev = sq.op("vector",
                     lambda e: e.tensor_copy(WSC[:, 0:1], PB[1][:, 0:1]),
                     after=[wsbc], tag="p0wsev")

        wq_sp = {}
        for c in range(C):
            par = c % 2
            buf = hrow(par).bitcast(BF16)[:, 0:D]
            scr = hrow(2)[:, 0:D] if par == 0 else hrow(3)[:, 0:D]

            def p0_q(e, c=c, buf=buf, scr=scr):
                e.tensor_single_scalar(scr, A1[:, c * CW:c * CW + D],
                                       WSC[:, 0:1], ALU.mult)
                e.tensor_scalar(scr, scr, MAGIC, MAGIC, ALU.add,
                                ALU.subtract)
                return e.tensor_scalar(buf, scr, 1.0, -1.0, ALU.min, ALU.max)
            qo = sq.op("vector", p0_q,
                       after=[wq_sp[c - 2]] if c >= 2 else [],
                       strict=[wsev] if c < 2 else [],
                       tag=f"p0q{c}")
            wq_sp[c] = sq.dma(
                "gpsimd",
                wiT_h.rearrange("(ob p) (dc ol) -> p ob dc ol",
                                p=P, ol=P)[:, :, c, :],
                buf.rearrange("p (ob ol) -> p ob ol", ol=P),
                after=[qo], tag=f"p0qsp{c}")
        ph0_done = sq.barrier()

        wsplit_done = {}


        def st_sum(i):
            return (PB[6], 32 * i) if i < 3 else (PB[7], 32 * (i - 3))

        def st_sq(i):
            return (PB[4], 32 * i) if i < 3 else (PB[5], 32 * (i - 3))

        def mk_bounds(t0, widths):
            bs = [t0]
            for w in widths:
                bs.append(bs[-1] + w)
            return bs

        def bpieces(bs, a, b):
            out = []
            g = a
            while g < b:
                i = max(j for j in range(len(bs) - 1) if bs[j] <= g)
                end = min(b, bs[i + 1])
                out.append((i, g - bs[i], g - a, end - g))
                g = end
            return out

        def stats_acc(tag, src_fn, sq_fn, bs, first, last, dep):
            def p_mm(e):
                mm = None
                for i in range(len(bs) - 1):
                    o_, w = bs[i], bs[i + 1] - bs[i]
                    bank, base = st_sum(i)
                    e.matmul(bank[base:base + 1, 0:w], ONESC[:, :],
                             src_fn(o_, w), start=first, stop=last)
                    bank, base = st_sq(i)
                    mm = e.matmul(bank[base:base + 1, 0:w], ONESC[:, :],
                                  sq_fn(o_, w), start=first, stop=last)
                return mm
            return sq.op("tensor", p_mm, after=dep, tag=tag)

        def stats_sweep_cls(tag, src_row_fn, t0, W, entry_fn, order=None):
            bs = mk_bounds(t0, [w for _, w in SL(W)])
            order = order if order is not None else list(range(C))
            lds = {}
            mm_ops = []
            for idx, c in enumerate(order):
                lds[c] = sq.dma("sync", hbc(c)[:, 0:W],
                                src_row_fn(c)[:, t0:t0 + W],
                                after=entry_fn(c), tag=f"{tag}ld{c}")
                xsq = SC1 if idx % 2 == 0 else SC2
                so = sq.op("scalar",
                           lambda e, c=c, xsq=xsq, W=W:
                               e.activation(xsq[:, 0:W], hbc(c)[:, 0:W],
                                            AF.Square),
                           after=[lds[c]] + ([mm_ops[idx - 2]]
                                             if idx >= 2 else []),
                           tag=f"{tag}sq{c}")
                mm_ops.append(stats_acc(
                    f"{tag}mm{c}",
                    lambda o_, w, c=c: hbc(c)[:, o_ - t0:o_ - t0 + w],
                    lambda o_, w, xsq=xsq, t0=t0:
                        xsq[:, o_ - t0:o_ - t0 + w],
                    bs, idx == 0, idx == C - 1, [so]))
            return lds, mm_ops[-1], bs

        def stats_rows(tag, bs, t0, W, entry):
            bc_prev = None
            for i, (o_, w) in enumerate(SL(W)):
                pcs = bpieces(bs, t0 + o_, t0 + o_ + w)

                def p_cp(e, pcs=pcs):
                    t = None
                    for (sl, off, dst, ln) in pcs:
                        bank, base = st_sum(sl)
                        e.copy(s_row[0:1, dst:dst + ln],
                               bank[base:base + 1, off:off + ln])
                        bank, base = st_sq(sl)
                        t = e.copy(q_row[0:1, dst:dst + ln],
                                   bank[base:base + 1, off:off + ln])
                    return t
                cp = sq.op("scalar", p_cp,
                           after=list(entry) + ([bc_prev] if bc_prev
                                                else []),
                           tag=f"{tag}rc{i}")
                mm = sq.op("tensor",
                           lambda e, w=w:
                               (e.matmul(PB[2][:, 0:w], ONESR,
                                         s_row[0:1, 0:w], start=True,
                                         stop=True),
                                e.matmul(PB[3][:, 0:w], ONESR,
                                         q_row[0:1, 0:w], start=True,
                                         stop=True))[-1],
                           after=[cp], tag=f"{tag}bc{i}")
                bc_prev = sq.op("scalar",
                                lambda e, o_=o_, w=w:
                                    (e.copy(R_S[:, o_:o_ + w],
                                            PB[2][:, 0:w]),
                                     e.copy(R_Q[:, o_:o_ + w],
                                            PB[3][:, 0:w]))[-1],
                                after=[mm], tag=f"{tag}be{i}")
            return bc_prev

        def rowmath(tag, W, entry):
            def p_rm1(e, W=W):
                inv_d = 1.0 / D
                e.tensor_single_scalar(R_M[:, 0:W], R_S[:, 0:W], inv_d,
                                       ALU.mult)
                e.tensor_single_scalar(R_Q[:, 0:W], R_Q[:, 0:W], inv_d,
                                       ALU.mult)
                e.tensor_tensor(R_A[:, 0:W], R_M[:, 0:W], R_M[:, 0:W],
                                ALU.mult)
                e.tensor_tensor(R_Q[:, 0:W], R_Q[:, 0:W], R_A[:, 0:W],
                                ALU.subtract)
                return e.tensor_single_scalar(R_T1[:, 0:W], R_Q[:, 0:W],
                                              LN_EPS, ALU.add)
            rm1 = sq.op("vector", p_rm1, after=[entry], tag=f"{tag}rm1")
            sq1 = sq.op("scalar",
                        lambda e, W=W: e.activation(R_T2[:, 0:W],
                                                    R_T1[:, 0:W], AF.Sqrt),
                        after=[rm1], tag=f"{tag}s1")

            def p_rm2(e, W=W):
                e.reciprocal(R_RS[:, 0:W], R_T2[:, 0:W])
                e.tensor_tensor(R_T2[:, 0:W], R_RS[:, 0:W], R_RS[:, 0:W],
                                ALU.mult)
                e.tensor_tensor(R_T2[:, 0:W], R_T2[:, 0:W], R_T1[:, 0:W],
                                ALU.mult)
                e.tensor_scalar(R_T2[:, 0:W], R_T2[:, 0:W], -0.5, 1.5,
                                ALU.mult, ALU.add)
                return e.tensor_tensor(R_RS[:, 0:W], R_RS[:, 0:W],
                                       R_T2[:, 0:W], ALU.mult)
            return sq.op("vector", p_rm2, after=[sq1], tag=f"{tag}rm2")

        def second_rsqrt(tag, W, eps, entry):
            def p_a(e):
                e.reciprocal(R_RM[:, 0:W], R_T1[:, 0:W])
                e.tensor_tensor(R_RM[:, 0:W], R_RM[:, 0:W], R_Q[:, 0:W],
                                ALU.mult)
                return e.tensor_single_scalar(R_T1[:, 0:W], R_RM[:, 0:W],
                                              eps, ALU.add)
            a = sq.op("vector", p_a, after=[entry], tag=f"{tag}r2a")
            s = sq.op("scalar",
                      lambda e: e.activation(R_T2[:, 0:W], R_T1[:, 0:W],
                                             AF.Sqrt),
                      after=[a], tag=f"{tag}r2s")

            def p_b(e):
                e.reciprocal(R_RM[:, 0:W], R_T2[:, 0:W])
                e.tensor_tensor(R_T2[:, 0:W], R_RM[:, 0:W], R_RM[:, 0:W],
                                ALU.mult)
                e.tensor_tensor(R_T2[:, 0:W], R_T2[:, 0:W], R_T1[:, 0:W],
                                ALU.mult)
                e.tensor_scalar(R_T2[:, 0:W], R_T2[:, 0:W], -0.5, 1.5,
                                ALU.mult, ALU.add)
                return e.tensor_tensor(R_RM[:, 0:W], R_RM[:, 0:W],
                                       R_T2[:, 0:W], ALU.mult)
            return sq.op("vector", p_b, after=[s], tag=f"{tag}r2b")

        fxmaps = {}

        def quant_phase(tag, src_row_fn, halves, entry,
                        sweep0_entry_fn=None, sweep0_order=None,
                        packed=None):
            idld = sq.dma("sync", IDENT, ident_h, after=entry,
                          tag=f"{tag}id")
            amz = sq.op("vector", lambda e: e.memset(AMTM, 0.0),
                        after=entry, tag=f"{tag}amz")
            prev = entry
            fins = []
            for hi, (t0, W) in enumerate(halves):
                slh = SL(W)
                nblk = (W + P - 1) // P
                if hi == 0 and sweep0_entry_fn is not None:
                    efn = sweep0_entry_fn
                    order = sweep0_order
                else:
                    efn = lambda c, prev=prev: list(prev)
                    order = None
                if packed is None:
                    lds, lastmm, bs = stats_sweep_cls(f"{tag}{hi}",
                                                      src_row_fn, t0, W,
                                                      efn, order)
                    acc = [lastmm]
                else:
                    bs, acc = packed
                    lds = {c: sq.dma("sync", hbc(c)[:, 0:W],
                                     src_row_fn(c)[:, t0:t0 + W],
                                     after=efn(c),
                                     tag=f"{tag}{hi}ld{c}")
                           for c in range(C)}
                be = stats_rows(f"{tag}{hi}", bs, t0, W,
                                list(acc) + (list(prev) if hi else []))
                rm2 = rowmath(f"{tag}{hi}", W, be)
                rmm = second_rsqrt(f"{tag}{hi}", W, RMS_EPS, rm2)

                subg_ops = []
                for c in range(C):
                    eng = "vector"

                    def p_sg(e, c=c, W=W):
                        e.tensor_tensor(hbc(c)[:, 0:W], hbc(c)[:, 0:W],
                                        R_M[:, 0:W], ALU.subtract)
                        return e.tensor_single_scalar(hbc(c)[:, 0:W],
                                                      hbc(c)[:, 0:W],
                                                      vt["gam"][:, c:c + 1],
                                                      ALU.mult)
                    subg_ops.append(sq.op(eng, p_sg, after=[rmm, lds[c]],
                                          tag=f"{tag}{hi}sg{c}"))
                tree_ops = []
                for c in range(C):
                    buf = SC3 if c == 0 else (SC1 if c % 2 == 1 else SC2)
                    adep = [subg_ops[c]] + ([tree_ops[c - 2]]
                                            if c >= 3 else [])
                    ab = sq.op("scalar",
                               lambda e, c=c, buf=buf, W=W:
                                   e.activation(buf[:, 0:W], hbc(c)[:, 0:W],
                                                AF.Abs),
                               after=adep, tag=f"{tag}{hi}ab{c}")
                    if c == 0:
                        tree_ops.append(ab)
                    else:
                        tree_ops.append(sq.op(
                            "vector",
                            lambda e, buf=buf, W=W:
                                e.tensor_tensor(SC3[:, 0:W], SC3[:, 0:W],
                                                buf[:, 0:W], ALU.max),
                            after=[ab], tag=f"{tag}{hi}mx{c}"))
                tree = tree_ops[-1]

                red_ops = []
                for j in range(nblk):
                    w = min(P, W - j * P)
                    pb = PB[j % 2]
                    tr = sq.op("tensor",
                               lambda e, pb=pb, j=j, w=w:
                                   e.transpose(pb[0:w, 0:P],
                                               SC3[:, j * P:j * P + w],
                                               IDENT),
                               after=[tree, idld] + ([red_ops[j - 2]]
                                                     if j >= 2 else []),
                               tag=f"{tag}{hi}tp{j}")
                    red_ops.append(sq.op(
                        "vector",
                        lambda e, pb=pb, j=j, w=w:
                            e.tensor_reduce(AMTM[0:w, j:j + 1], pb[0:w, 0:P],
                                            AX.X, ALU.max),
                        after=[tr, amz], tag=f"{tag}{hi}rd{j}"))
                tr2 = sq.op("tensor",
                            lambda e: e.transpose(PB[2][0:16, 0:P],
                                                  AMTM, IDENT),
                            after=red_ops, tag=f"{tag}{hi}tp2")
                amev = sq.op("vector",
                             lambda e: e.tensor_copy(AMT2,
                                                     PB[2][0:16, 0:P]),
                             after=[tr2], tag=f"{tag}{hi}ame")
                bc_ev = None
                for i, (o_, w) in enumerate(slh):
                    jb = o_ // P
                    nfull = w // P
                    rem = w - nfull * P
                    adep = [amev] + ([bc_ev] if bc_ev else [])
                    ads = []
                    if nfull:
                        ads.append(sq.dma(
                            "gpsimd", a_row[0:1, 0:nfull * P],
                            AMT2[jb:jb + nfull, :], after=adep,
                            tag=f"{tag}{hi}ad{i}"))
                    if rem:
                        ads.append(sq.dma(
                            "gpsimd",
                            a_row[0:1, nfull * P:nfull * P + rem],
                            AMT2[jb + nfull:jb + nfull + 1, 0:rem],
                            after=adep, tag=f"{tag}{hi}ar{i}"))
                    mm = sq.op("tensor",
                               lambda e, w=w:
                                   e.matmul(PB[3][:, 0:w], ONESR,
                                            a_row[0:1, 0:w], start=True,
                                            stop=True),
                               after=ads + ([bc_ev] if bc_ev else []),
                               tag=f"{tag}{hi}am{i}")
                    bc_ev = sq.op("scalar",
                                  lambda e, o_=o_, w=w:
                                      e.copy(R_A[:, o_:o_ + w],
                                             PB[3][:, 0:w]),
                                  after=[mm], tag=f"{tag}{hi}ae{i}")

                def p_finrow(e, W=W, t0=t0):
                    e.tensor_tensor(R_T1[:, 0:W], R_RS[:, 0:W], R_RM[:, 0:W],
                                    ALU.mult)
                    e.tensor_tensor(R_A[:, 0:W], R_A[:, 0:W], R_T1[:, 0:W],
                                    ALU.mult)
                    e.tensor_single_scalar(R_A[:, 0:W], R_A[:, 0:W], A_EPS,
                                           ALU.add)
                    e.reciprocal(R_A[:, 0:W], R_A[:, 0:W])
                    e.tensor_single_scalar(R_A[:, 0:W], R_A[:, 0:W], 127.0,
                                           ALU.mult)
                    e.tensor_single_scalar(R_T2[:, 0:W], R_A[:, 0:W],
                                           WSC[:, 0:1], ALU.mult)
                    e.reciprocal(R_T2[:, 0:W], R_T2[:, 0:W])
                    e.tensor_single_scalar(QS[:, t0:t0 + W], R_T2[:, 0:W],
                                           0.5, ALU.mult)
                    return e.tensor_tensor(R_T1[:, 0:W], R_T1[:, 0:W],
                                           R_A[:, 0:W], ALU.mult)
                finrow = sq.op("vector", p_finrow, after=[bc_ev],
                               tag=f"{tag}{hi}fr")
                fin = None
                fxmap = fxmaps.setdefault(tag, {})
                for c in range(C):
                    def p_fx(e, c=c, W=W, t0=t0):
                        e.tensor_tensor(hbc(c)[:, 0:W], hbc(c)[:, 0:W],
                                        R_T1[:, 0:W], ALU.mult)
                        return e.tensor_scalar(xic(c)[:, t0:t0 + W],
                                               hbc(c)[:, 0:W], MAGIC,
                                               MAGIC, ALU.add,
                                               ALU.subtract)
                    fin = sq.op("vector", p_fx, after=[finrow],
                                tag=f"{tag}{hi}fx{c}")
                    fxmap[(hi, c)] = fin
                fins.append(fin)
                prev = [fin]
            return fins

        if LOADTEST is not None and LOADTEST < 1:
            p1_fins = [ph0_done[0], ph0_done[0]]
        else:
            p1_fins = quant_phase(
                "p1", lambda c: x_h[c * P:(c + 1) * P, :],
                [(0, 1024), (1024, 1025)], ph0_done,
                sweep0_entry_fn=lambda c: ([] if c >= 8
                                           else list(ph0_done)),
                sweep0_order=list(range(8, 16)) + list(range(8)))
        if "xid" in tap:
            xid_h = nc.dram_tensor("xid", [D, CW], BF16,
                                   kind="ExternalOutput").ap()
            qsd_h = nc.dram_tensor("qsd", [P, CW], F32,
                                   kind="ExternalOutput").ap()
            for c in range(C):
                sq.dma("gpsimd", xid_h[c * P:(c + 1) * P, :], xic(c),
                       after=[p1_fins[-1]], tag=f"xid{c}")
            sq.dma("gpsimd", qsd_h, QS, after=[p1_fins[-1]], tag="qsd")

        def ffn_mm_phase(tag, tx, addsrc_fn, dst_fn, entryA, entryB,
                         fold=False, filler=None, pre_w=None, pre_x=None):
            slt = SL(tx)
            bsf = mk_bounds(0, [w for _, w in slt])
            ns = len(slt)
            ev_flat = []
            sp_ops = []
            mm_flat = []
            ev2_last = {}
            fsqr = {}
            ffmm = {}
            for o in range(C):
                par = o % 2
                if o < 2:
                    wdep = pre_w(o) if pre_w is not None else list(entryA)
                else:
                    wdep = [mm_flat[(o - 2) * ns + ns - 1]]
                wld = sq.dma(
                    "sync",
                    WCOL[par].bitcast(BF16)[:, 0:C * P],
                    wiT_h[o * P:(o + 1) * P, :],
                    after=wdep, tag=f"{tag}w{o}")
                if o < 2:
                    xdep = pre_x(o) if pre_x is not None else list(entryA)
                else:
                    xdep = [ev2_last[o - 2]]
                xld = sq.dma("sync", ROWA[par][:, 0:tx], addsrc_fn(o),
                             after=xdep, tag=f"{tag}x{o}")
                ev2 = None
                for i, (o_, w) in enumerate(slt):
                    g = o * ns + i
                    pb = PB[g % 2]

                    def p_mm(e, par=par, o_=o_, w=w, pb=pb):
                        mm = None
                        for d in range(C):
                            mm = e.matmul(
                                pb[:, 0:w],
                                WCOL[par].bitcast(BF16)[:,
                                                        d * P:(d + 1) * P],
                                xic(d)[:, o_:o_ + w],
                                start=(d == 0), stop=(d == C - 1))
                        return mm
                    mdep = [wld] + ([ev_flat[g - 2]] if g >= 2 else [])
                    if o == 0:
                        mdep += list(entryA) if (o_ + w <= 1024) else \
                            list(entryB)
                    mm = sq.op("tensor", p_mm, after=mdep,
                               tag=f"{tag}m{o}_{i}")
                    mm_flat.append(mm)

                    def p_ev(e, par=par, o_=o_, w=w, pb=pb):
                        return e.tensor_tensor(ROWB[par][:, o_:o_ + w],
                                               pb[:, 0:w],
                                               QS[:, o_:o_ + w], ALU.mult)
                    edep = [mm] + ([sp_ops[o - 2]]
                                   if (o >= 2 and i == 0) else [])
                    if fold and o >= 2 and i == 0:
                        edep += [fsqr[o - 2], ffmm.get(o - 2)]
                        edep = [d for d in edep if d is not None]
                    ev = sq.op("vector", p_ev, after=edep,
                               tag=f"{tag}e{o}_{i}")
                    ev_flat.append(ev)

                    def p_ev2(e, par=par, o_=o_, w=w):
                        return e.tensor_tensor(ROWB[par][:, o_:o_ + w],
                                               ROWB[par][:, o_:o_ + w],
                                               ROWA[par][:, o_:o_ + w],
                                               ALU.add)
                    ev2 = sq.op("gpsimd", p_ev2, after=[ev, xld],
                                tag=f"{tag}g{o}_{i}")
                ev2_last[o] = ev2
                dst = dst_fn(o)
                sp_ops.append(sq.dma("gpsimd", dst, ROWB[par][:, 0:tx],
                                     after=[ev2], tag=f"{tag}s{o}"))
                if fold:
                    if o > 0:
                        po = o - 1
                        ffmm[po] = stats_acc(
                            f"{tag}f{po}",
                            lambda o_, w, pp=po % 2:
                                ROWB[pp][:, o_:o_ + w],
                            lambda o_, w: SC12[:, o_:o_ + w],
                            bsf, po == 0, False, [fsqr[po]])
                    fsqr[o] = sq.op(
                        "scalar",
                        lambda e, par=par, tx=tx:
                            e.activation(SC12[:, 0:tx], ROWB[par][:, 0:tx],
                                         AF.Square),
                        after=[ev2] + ([ffmm[o - 1]] if o > 0 else []),
                        tag=f"{tag}q{o}")
                if filler is not None:
                    filler(6)
            fold_out = None
            if fold:
                ffmm[C - 1] = stats_acc(
                    f"{tag}f{C - 1}",
                    lambda o_, w, pp=(C - 1) % 2: ROWB[pp][:, o_:o_ + w],
                    lambda o_, w: SC12[:, o_:o_ + w],
                    bsf, False, True, [fsqr[C - 1]])
                fold_out = (bsf, [ffmm[C - 1]])
            if filler is not None:
                filler(1000)
            return sp_ops, mm_flat[-1], fold_out

        def p0b_gen():
            pw = {"h": None, "l": None, "s1": None, "s2": None}
            qs = ["sync", "scalar"]
            pi = 0
            for wname, w_src in (("wr", wr_h),):
                for c in range(C):
                    sps = []
                    for h in range(2):
                        p = pi % 2
                        dep = [x for x in (pw["h"], pw["l"])
                               if x is not None] or list(ph0_done)
                        wl = sq.dma(qs[p], W0STG[p],
                                    w_src[c * P:(c + 1) * P,
                                          h * 1024:(h + 1) * 1024],
                                    after=dep, tag=f"b{wname}l{c}_{h}")
                        hi = sq.op(
                            "scalar",
                            lambda e, p=p, h=h:
                                e.copy(W0HI[:, h * 1024:(h + 1) * 1024],
                                       W0STG[p]),
                            after=[x for x in
                                   (wl, pw["l"], pw["s1"])
                                   if x is not None],
                            tag=f"b{wname}h{c}_{h}")
                        lo = sq.op(
                            "vector",
                            lambda e, p=p, h=h:
                                e.tensor_tensor(
                                    W0LO[:, h * 1024:(h + 1) * 1024],
                                    W0STG[p],
                                    W0HI[:, h * 1024:(h + 1) * 1024],
                                    ALU.subtract),
                            after=[x for x in (hi, pw["s2"])
                                   if x is not None],
                            tag=f"b{wname}o{c}_{h}")
                        pw["h"] = hi
                        pw["l"] = lo
                        pi += 1
                        if h == 1:
                            s1 = sq.dma(
                                "gpsimd",
                                wshi_h[wname][c * P:(c + 1) * P, :],
                                W0HI, after=[pw["h"], hi],
                                tag=f"b{wname}s{c}")
                            s2 = sq.dma(
                                "gpsimd",
                                wslo_h[wname][c * P:(c + 1) * P, :],
                                W0LO[:, 0:2048], after=[lo],
                                tag=f"b{wname}t{c}")
                            pw["s1"] = s1
                            pw["s2"] = s2
                        yield
            wsplit_done["all"] = [x for x in (pw["h"], pw["l"],
                                              pw["s1"], pw["s2"])
                                  if x is not None]

        p0b_it = p0b_gen()

        def p0b_filler(k):
            for _ in range(k):
                if next(p0b_it, "END") == "END":
                    return

        _fx1 = fxmaps["p1"]

        def p2_pre_w(o):
            return ([_fx1[(1, 11)], _fx1[(1, 12)], _fx1[(1, 13)]]
                    if o == 0 else
                    [_fx1[(1, 13)], _fx1[(1, 14)], _fx1[(1, 15)]])

        def p2_pre_x(o):
            return ([_fx1[(1, 0)], _fx1[(1, 1)]] if o == 0 else
                    [_fx1[(1, 1)], _fx1[(1, 2)], _fx1[(1, 3)]])
        p2_sp, p2_lastmm, p2_fold = ffn_mm_phase(
            "p2", TX,
            addsrc_fn=lambda o: x_h[o * P:(o + 1) * P, 0:TX],
            dst_fn=lambda o: x1_h[o * P:(o + 1) * P, 0:TX],
            entryA=[p1_fins[1]], entryB=[p1_fins[1]],
            fold=True, filler=p0b_filler,
            pre_w=p2_pre_w, pre_x=p2_pre_x)

        mix_spills = []
        xp_spills = []
        kxsp = {}
        vxsp = {}
        rxsp = {}
        _p3split = {}
        _p3sp1 = {}
        _p3sp2 = {}
        for _j in range(3):
            _p3sp1[(-1, _j)] = wsplit_done.get("all", [])
            _p3sp2[(-1, _j)] = wsplit_done.get("all", [])
        prev3 = sq.barrier()
        p2_bar = prev3
        for hi, (t0, W) in enumerate([(0, 1025), (1024, 1025)]
                                     if (LOADTEST is None or LOADTEST >= 3)
                                     else []):
            if hi == 1:
                for _j in range(3):
                    _p3sp1[(-1, _j)] = _p3sp1.get((C - 1, _j), [])
                    _p3sp2[(-1, _j)] = _p3sp2.get((C - 1, _j), [])
                _p3split[-1] = _p3split.get(C - 1, [])
            lds = {c: sq.dma("sync", hbc(c)[:, 0:W],
                             x1_h[c * P:(c + 1) * P, t0:t0 + W],
                             after=list(prev3), tag=f"p3{hi}ld{c}")
                   for c in range(C)}
            be3 = stats_rows(f"p3{hi}", p2_fold[0], t0, W,
                             list(p2_fold[1]) + (list(prev3) if hi
                                                 else []))
            rm2 = rowmath(f"p3{hi}", W, be3)
            s2 = second_rsqrt(f"p3{hi}", W, LN_EPS, rm2)
            rr = sq.op("vector",
                       lambda e, W=W: e.tensor_tensor(R_T1[:, 0:W],
                                                      R_RS[:, 0:W],
                                                      R_RM[:, 0:W],
                                                      ALU.mult),
                       after=[s2], tag=f"p3{hi}rr")

            xn_ops = []
            for c in range(C):
                xsp = SC1 if c % 2 == 0 else SC2

                def p_xp(e, c=c, W=W, xsp=xsp):
                    e.tensor_tensor(hbc(c)[:, 0:W], hbc(c)[:, 0:W],
                                    R_M[:, 0:W], ALU.subtract)
                    return e.tensor_tensor(xsp[:, 0:W], hbc(c)[:, 0:W],
                                           R_RS[:, 0:W], ALU.mult)
                xp = sq.op("gpsimd", p_xp,
                           after=[rr, lds[c]] + ([xp_spills[-2]]
                                                 if c >= 2 else []),
                           tag=f"p3{hi}xp{c}")
                xp_spills.append(sq.dma(
                    "gpsimd", xp_h[c * P:(c + 1) * P, t0:t0 + W],
                    xsp[:, 0:W], after=[xp], tag=f"p3{hi}xsp{c}"))

                def p_xn(e, c=c, W=W, hi=hi):
                    e.tensor_tensor(hbc(c)[:, 0:W], hbc(c)[:, 0:W],
                                    R_T1[:, 0:W], ALU.mult)
                    t = e.tensor_scalar(hbc(c)[:, 0:W], hbc(c)[:, 0:W],
                                        vt["lnsc"][:, c:c + 1],
                                        vt["lnb"][:, c:c + 1],
                                        ALU.mult, ALU.add)
                    if hi == 0:
                        t = e.tensor_single_scalar(hbc(c)[:, 0:1],
                                                   hbc(c)[:, 0:1],
                                                   PMASK[:, 0:1], ALU.mult)
                    return t
                xn_ops.append(sq.op("vector", p_xn, after=[xp],
                                    tag=f"p3{hi}xn{c}"))

            last_mix = xn_ops[-1]
            for c in range(C):
                def p_mix(e, c=c, W=W):
                    e.tensor_tensor(SC3[:, 0:W - 1], hbc(c)[:, 1:W],
                                    hbc(c)[:, 0:W - 1], ALU.subtract)
                    t = None
                    for j, mv in enumerate(("mk", "mv", "mr")):
                        t = e.scalar_tensor_tensor(
                            MIXO[j][:, 0:W - 1], SC3[:, 0:W - 1],
                            vt[mv][:, c:c + 1], hbc(c)[:, 0:W - 1],
                            ALU.mult, ALU.add)
                    return t
                mdep = [last_mix, xn_ops[c]] + _p3split.get(c - 1, [])
                mx = sq.op("vector", p_mix, after=mdep, tag=f"p3{hi}mx{c}")
                last_mix = mx
                _p3split[c] = []
                for j, (hih, loh, spd) in enumerate((
                        (kxhi_h, kxlo_h, kxsp), (vxhi_h, vxlo_h, vxsp),
                        (rxhi_h, rxlo_h, rxsp))):
                    hic = sq.op("scalar",
                                lambda e, j=j, W=W:
                                    e.copy(SPH[j][:, 0:W - 1],
                                           MIXO[j][:, 0:W - 1]),
                                after=[mx] + _p3sp1.get((c - 1, j), []),
                                tag=f"p3{hi}hc{c}_{j}")
                    loc = sq.op("vector",
                                lambda e, j=j, W=W:
                                    e.tensor_tensor(SPL[j][:, 0:W - 1],
                                                    MIXO[j][:, 0:W - 1],
                                                    SPH[j][:, 0:W - 1],
                                                    ALU.subtract),
                                after=[hic] + _p3sp2.get((c - 1, j), []),
                                tag=f"p3{hi}lc{c}_{j}")
                    sp1 = sq.dma(
                        "gpsimd", hih[c * P:(c + 1) * P, t0:t0 + W - 1],
                        SPH[j][:, 0:W - 1], after=[hic],
                        tag=f"p3{hi}s1{c}_{j}")
                    sp2 = sq.dma(
                        "gpsimd", loh[c * P:(c + 1) * P, t0:t0 + W - 1],
                        SPL[j][:, 0:W - 1], after=[loc],
                        tag=f"p3{hi}s2{c}_{j}")
                    mix_spills += [sp1, sp2]
                    _p3sp1[(c, j)] = [sp1]
                    _p3sp2[(c, j)] = [sp2]
                    _p3split[c] += [hic, loc]
                    spd[(hi, c)] = [sp1, sp2]
            prev3 = [last_mix] + mix_spills[-6:]
        p3_bar = sq.barrier()

        def proj_phase(tag, w_hbm, res_dep_fn, res_hbm, entry, make_evac,
                       post_o, split3=None, fold_cb=None, otf=False):
            slt = SL(T)
            ns = len(slt)
            otf_lo = {}
            mmh_last = {}
            res_half = {0: [], 1: []}
            for c in range(C):
                for half in range(2):
                    if split3 is None:
                        res_half[half].append(sq.dma(
                            "sync",
                            A1[:, c * CW + half * TH:
                               c * CW + (half + 1) * TH],
                            res_hbm[c * P:(c + 1) * P,
                                    half * TH:(half + 1) * TH],
                            after=res_dep_fn(c, half),
                            tag=f"{tag}r{c}_{half}"))
                    else:
                        _, _, reshi, reslo = split3
                        dep = res_dep_fn(c, half)
                        res_half[half].append(sq.dma(
                            "sync",
                            A1BF[:, c * 2 * CW + half * TH:
                                 c * 2 * CW + (half + 1) * TH],
                            reshi[c * P:(c + 1) * P,
                                  half * TH:(half + 1) * TH],
                            after=dep, tag=f"{tag}rh{c}_{half}"))
                        res_half[half].append(sq.dma(
                            "sync",
                            A1BF[:, c * 2 * CW + T + half * TH:
                                 c * 2 * CW + T + (half + 1) * TH],
                            reslo[c * P:(c + 1) * P,
                                  half * TH:(half + 1) * TH],
                            after=dep, tag=f"{tag}rl{c}_{half}"))
            evl = {}
            mm_last = {}
            for o in range(C):
                par = o % 2
                wdep = list(entry) if o < 2 else [mm_last[o - 2]]
                if otf:
                    wdep2 = list(entry) if o < 2 else \
                        [otf_lo[o - 2]]
                    wl = sq.dma(
                        "sync",
                        ROWC[par][:, 0:C * P].rearrange(
                            "p (dc oc) -> p dc oc", dc=C),
                        w_hbm.rearrange("(dc p) o -> p dc o", p=P)[
                            :, :, o * P:(o + 1) * P],
                        after=wdep2, tag=f"{tag}w{o}")
                    wcb = WCOL[par].bitcast(BF16)
                    hi = sq.op(
                        "gpsimd",
                        lambda e, par=par, wcb=wcb:
                            e.tensor_copy(wcb[:, 0:C * P],
                                          ROWC[par][:, 0:C * P]),
                        after=[wl] + ([mmh_last[o - 2]] if o >= 2 else []),
                        tag=f"{tag}wh{o}")
                    lo = sq.op(
                        "gpsimd",
                        lambda e, par=par, wcb=wcb:
                            e.tensor_tensor(wcb[:, C * P:2 * C * P],
                                            ROWC[par][:, 0:C * P],
                                            wcb[:, 0:C * P],
                                            ALU.subtract),
                        after=[hi] + ([mm_last[o - 2]] if o >= 2 else []),
                        tag=f"{tag}wo{o}")
                    otf_lo[o] = lo
                    wlds = [hi, lo]
                elif split3 is None:
                    wlds = [sq.dma(
                        "sync",
                        WCOL[par][:, 0:C * P].rearrange(
                            "p (dc oc) -> p dc oc", dc=C),
                        w_hbm.rearrange("(dc p) o -> p dc o", p=P)[
                            :, :, o * P:(o + 1) * P],
                        after=wdep, tag=f"{tag}w{o}")]
                else:
                    whi, wlo = split3[0], split3[1]
                    wcb = WCOL[par].bitcast(BF16)
                    wlds = [
                        sq.dma("sync",
                               wcb[:, 0:C * P].rearrange(
                                   "p (dc oc) -> p dc oc", dc=C),
                               whi.rearrange("(dc p) o -> p dc o", p=P)[
                                   :, :, o * P:(o + 1) * P],
                               after=wdep, tag=f"{tag}wh{o}"),
                        sq.dma("sync",
                               wcb[:, C * P:2 * C * P].rearrange(
                                   "p (dc oc) -> p dc oc", dc=C),
                               wlo.rearrange("(dc p) o -> p dc o", p=P)[
                                   :, :, o * P:(o + 1) * P],
                               after=wdep, tag=f"{tag}wl{o}")]
                for i, (o_, w) in enumerate(slt):
                    g = o * ns + i
                    pb = PB[g % 2]

                    if split3 is None:
                        def p_mm(e, par=par, o_=o_, w=w, pb=pb):
                            mm = None
                            for d in range(C):
                                mm = e.matmul(
                                    pb[:, 0:w],
                                    WCOL[par][:, d * P:(d + 1) * P],
                                    A1[:, d * CW + o_:d * CW + o_ + w],
                                    start=(d == 0), stop=(d == C - 1))
                            return mm
                    else:
                        def p_mm_parts(e, par=par, o_=o_, w=w, pb=pb,
                                       passes=None):
                            wcb = WCOL[par].bitcast(BF16)
                            mm = None
                            for lb, rb, first, last in passes:
                                for d in range(C):
                                    mm = e.matmul(
                                        pb[:, 0:w],
                                        wcb[:, lb + d * P:lb + (d + 1) * P],
                                        A1BF[:, d * 2 * CW + rb + o_:
                                             d * 2 * CW + rb + o_ + w],
                                        start=(first and d == 0),
                                        stop=(last and d == C - 1))
                            return mm
                    if split3 is not None and otf:
                        base = [evl[g - 2]] if g >= 2 else []
                        if o == 0:
                            base = base + res_half[0 if i < 2 else 1]
                        mmh = sq.op(
                            "tensor",
                            lambda e, f=p_mm_parts:
                                f(e, passes=((0, 0, True, False),
                                             (0, T, False, False))),
                            after=[wlds[0]] + base, tag=f"{tag}mh{o}_{i}")
                        mm = sq.op(
                            "tensor",
                            lambda e, f=p_mm_parts:
                                f(e, passes=((C * P, 0, False, True),)),
                            after=[wlds[1], mmh], tag=f"{tag}m{o}_{i}")
                        mmh_last[o] = mmh
                    else:
                        if split3 is not None:
                            def p_mm(e, f=p_mm_parts):
                                return f(e, passes=((0, 0, True, False),
                                                    (0, T, False, False),
                                                    (C * P, 0, False,
                                                     True)))
                        mdep = list(wlds) + ([evl[g - 2]] if g >= 2 else [])
                        if o == 0:
                            mdep += res_half[0 if i < 2 else 1]
                        mm = sq.op("tensor", p_mm, after=mdep,
                                   tag=f"{tag}m{o}_{i}")
                    mm_last[o] = mm
                    evl[g] = make_evac(o, par, i, (o_, w), mm, pb)
                post_o(o, par)
                if fold_cb is not None:
                    fold_cb(o, par)
            return mm_last[C - 1]

        p4_entry = p3_bar
        p4_spills = []

        def p4_evac(o, par, i, sl, mm, pb):
            o_, w = sl
            dep = [mm] + ([p4_spills[o - 2]] if (o >= 2 and i == 0) else [])
            return sq.op("scalar",
                         lambda e, par=par, o_=o_, w=w, pb=pb:
                             e.activation(ROWA[par][:, o_:o_ + w],
                                          pb[:, 0:w], AF.Exp),
                         after=dep, tag=f"p4e{o}_{i}")

        p4_pend = {}

        def p4_post(o, par):
            p4_pend[o] = (sq.last["scalar"], par)
            if o > 0:
                dep, ppar = p4_pend.pop(o - 1)
                p4_spills.append(sq.dma(
                    "gpsimd", ek_h[(o - 1) * P:o * P, 0:T],
                    ROWA[ppar][:, 0:T], after=[dep], tag=f"p4sp{o - 1}"))

        def p4_flush():
            dep, ppar = p4_pend.pop(C - 1)
            p4_spills.append(sq.dma(
                "gpsimd", ek_h[(C - 1) * P:C * P, 0:T],
                ROWA[ppar][:, 0:T], after=[dep], tag=f"p4sp{C - 1}"))

        def p4_res_dep(c, half):
            d = list(kxsp.get((half, c), []))
            d += p2_bar if c < 8 else p3_bar
            return [x for x in d if x is not None]

        p4_lastmm = None
        if LOADTEST is None or LOADTEST >= 4:
            p4_lastmm = proj_phase(
                "p4", wk_h, p4_res_dep, None, p4_entry, p4_evac, p4_post,
                split3=(None, None, kxhi_h, kxlo_h), otf=True)
            p4_flush()

        p5_entry = sq.barrier()
        p5_spills = []

        def p5_res_dep(c, half):
            d = list(vxsp.get((half, c), [])) + [p4_lastmm]
            return [x for x in d if x is not None]
        p5_post = []
        p5_pend = {}
        p5_ekld = {}

        def p5_evac(o, par, i, sl, mm, pb):
            o_, w = sl
            if i == 0:
                p5_ekld[o] = sq.dma(
                    "sync", ROWA[par][:, 0:T], ek_h[o * P:(o + 1) * P, 0:T],
                    after=[p4_spills[o]] + (p5_entry if o < 2
                                            else [p5_post[o - 2]]),
                    tag=f"p5ek{o}")
            dep = [mm] + ([p5_post[o - 2]] if (o >= 2 and i == 0) else [])
            return sq.op("scalar",
                         lambda e, par=par, o_=o_, w=w, pb=pb:
                             e.copy(ROWB[par][:, o_:o_ + w], pb[:, 0:w]),
                         after=dep, tag=f"p5e{o}_{i}")

        def p5_post_fn(o, par):
            def p_ekv(e, par=par):
                return e.tensor_tensor(ROWB[par][:, 0:T], ROWA[par][:, 0:T],
                                       ROWB[par][:, 0:T], ALU.mult)
            ekv = sq.op("vector", p_ekv,
                        after=[p5_ekld[o], sq.last["scalar"]] +
                              ([p5_spills[o - 2]] if o >= 2 else []),
                        tag=f"p5ekv{o}")
            p5_pend[o] = (ekv, par)
            if o > 0:
                pekv, ppar = p5_pend.pop(o - 1)
                p5_spills.append(sq.dma(
                    "gpsimd", ekv_h[(o - 1) * P:o * P, 0:T],
                    ROWB[ppar][:, 0:T], after=[pekv], tag=f"p5sp{o - 1}"))

            lamb = vt["lam"][:, o:o + 1].to_broadcast([P, TH])

            def p_scan0(e, par=par, lamb=lamb):
                e.tensor_tensor_scan(SA[:, 1:TH + 1], lamb,
                                     ROWB[par][:, 0:TH], 0.0,
                                     ALU.mult, ALU.add)
                return e.tensor_tensor_scan(SBt[:, 1:TH + 1], lamb,
                                            ROWA[par][:, 0:TH], 0.0,
                                            ALU.mult, ALU.add)
            s0 = sq.op("vector", p_scan0, after=[ekv], tag=f"p5s0{o}")

            def p_cp01(e):
                e.tensor_copy(SA[:, 0:1], SA[:, TH:TH + 1])
                return e.tensor_copy(SBt[:, 0:1], SBt[:, TH:TH + 1])
            cp01 = sq.op("vector", p_cp01, strict=[s0], tag=f"p5cp{o}")

            def p_scan1(e, par=par, lamb=lamb):
                e.tensor_tensor_scan(SA[:, 1:TH + 1], lamb,
                                     ROWB[par][:, TH:T], SA[:, 0:1],
                                     ALU.mult, ALU.add)
                return e.tensor_tensor_scan(SBt[:, 1:TH + 1], lamb,
                                            ROWA[par][:, TH:T], SBt[:, 0:1],
                                            ALU.mult, ALU.add)
            s1 = sq.op("vector", p_scan1, strict=[cp01], tag=f"p5s1{o}")

            def p_car(e, o=o):
                e.tensor_copy(CARRY[:, o:o + 1], SA[:, TH:TH + 1])
                return e.tensor_copy(CARRY[:, C + o:C + o + 1],
                                     SBt[:, TH:TH + 1])
            p5_post.append(sq.op("vector", p_car, strict=[s1],
                                 tag=f"p5sc{o}"))

        p5_lastmm = None
        if LOADTEST is None or LOADTEST >= 5:
            p5_lastmm = proj_phase(
                "p5", wv_h, p5_res_dep, None, p5_entry, p5_evac, p5_post_fn,
                split3=(None, None, vxhi_h, vxlo_h), otf=True)
            _pekv, _ppar = p5_pend.pop(C - 1)
            p5_spills.append(sq.dma(
                "gpsimd", ekv_h[(C - 1) * P:C * P, 0:T],
                ROWB[_ppar][:, 0:T], after=[_pekv], tag=f"p5sp{C - 1}"))

        p6_entry = sq.barrier()
        cin_sp = sq.dma("gpsimd", cin_h, CARRY[:, :], after=p6_entry,
                        tag="p6in")
        selld = sq.dma("sync", SELM, selmask_h, after=p6_entry, tag="p6sel")

        def p_cc(e):
            return e.collective_compute(
                "AllGather", ALU.bypass,
                replica_groups=[list(range(NCORE))],
                ins=[cin_h.opt()], outs=[cout_h.opt()])
        if NO_CC:
            cc = sq.dma("gpsimd", cout_h[0:P, :], cin_h, after=[cin_sp],
                        tag="p6cc")
        else:
            cc = sq.op("gpsimd", p_cc, after=[cin_sp], tag="p6cc")
        gld = sq.dma("gpsimd",
                     GATH.rearrange("p (r f) -> p r f", r=NCORE),
                     cout_h.rearrange("(r p) f -> p r f", p=P),
                     strict=[cc], tag="p6g")

        f0 = sq.op("vector",
                   lambda e: e.tensor_tensor(GATH, GATH, SELM, ALU.mult),
                   after=[gld, selld], tag="p6f0")
        f1 = sq.op("vector",
                   lambda e: e.tensor_tensor(GATH[:, 0:128], GATH[:, 0:128],
                                             GATH[:, 128:256], ALU.add),
                   strict=[f0], tag="p6f1")
        f2 = sq.op("vector",
                   lambda e: e.tensor_tensor(GATH[:, 0:64], GATH[:, 0:64],
                                             GATH[:, 64:128], ALU.add),
                   strict=[f1], tag="p6f2")
        f3 = sq.op("vector",
                   lambda e: e.tensor_tensor(GATH[:, 0:32], GATH[:, 0:32],
                                             GATH[:, 32:64], ALU.add),
                   strict=[f2], tag="p6f3")
        csel = sq.op("vector",
                     lambda e: e.tensor_copy(CARRY[:, :], GATH[:, 0:32]),
                     strict=[f3], tag="p6csel")
        CIN = CARRY

        if STOP_AT <= 6 or LOADTEST is not None:
            sq.op("gpsimd", lambda e: None, after=sq.barrier(), tag="final")
            sq.finalize()
            with nc.Block() as block:
                @block.sync
                def _(eng):
                    sq.run_engine("sync", eng)

                @block.tensor
                def _(eng):
                    sq.run_engine("tensor", eng)

                @block.vector
                def _(eng):
                    sq.run_engine("vector", eng)

                @block.scalar
                def _(eng):
                    sq.run_engine("scalar", eng)

                @block.gpsimd
                def _(eng):
                    sq.run_engine("gpsimd", eng)
            return nc
        p7_entry = sq.barrier()
        rw_spills = []

        def p7_res_dep(c, half):
            d = list(rxsp.get((half, c), [])) + [p5_lastmm]
            return [x for x in d if x is not None]
        p7_post = []
        p7_lds = {}

        def p7_evac(o, par, i, sl, mm, pb):
            o_, w = sl
            if i == 0:
                ekvld = sq.dma(
                    "sync", ROWA[par][:, 0:T], ekv_h[o * P:(o + 1) * P, 0:T],
                    after=[p5_spills[o]] + (p7_entry if o < 2
                                            else [p7_post[o - 2]]),
                    tag=f"p7ekv{o}")
                ekld = sq.dma(
                    "sync", ROWB[par][:, 0:T], ek_h[o * P:(o + 1) * P, 0:T],
                    after=(p7_entry if o < 2 else [p7_post[o - 2]]),
                    tag=f"p7ek{o}")
                p7_lds[o] = (ekvld, ekld)
            dep = [mm] + ([rw_spills[o - 2]] if (o >= 2 and i == 0) else [])
            return sq.op("scalar",
                         lambda e, par=par, o_=o_, w=w, pb=pb:
                             e.activation(ROWC[par][:, o_:o_ + w],
                                          pb[:, 0:w], AF.Sigmoid),
                         after=dep, tag=f"p7sg{o}_{i}")

        def p7_post_fn(o, par):
            ekvld, ekld = p7_lds[o]
            sgdone = sq.last["scalar"]
            lamb = vt["lam"][:, o:o + 1].to_broadcast([P, TH])
            prev_half = None
            for th in range(2):
                sl = slice(th * TH, (th + 1) * TH)

                def p_cp(e, o=o, th=th):
                    if th == 0:
                        e.tensor_copy(SA[:, 0:1], CIN[:, o:o + 1])
                        return e.tensor_copy(SBt[:, 0:1],
                                             CIN[:, C + o:C + o + 1])
                    e.tensor_copy(SA[:, 0:1], SA[:, TH:TH + 1])
                    return e.tensor_copy(SBt[:, 0:1], SBt[:, TH:TH + 1])
                cpi = sq.op("vector", p_cp,
                            after=[ekvld, ekld, csel],
                            strict=[prev_half], tag=f"p7cp{o}_{th}")

                def p_scan2(e, par=par, sl=sl, lamb=lamb):
                    e.tensor_tensor_scan(SA[:, 1:TH + 1], lamb,
                                         ROWA[par][:, sl], SA[:, 0:1],
                                         ALU.mult, ALU.add)
                    return e.tensor_tensor_scan(SBt[:, 1:TH + 1], lamb,
                                                ROWB[par][:, sl],
                                                SBt[:, 0:1],
                                                ALU.mult, ALU.add)
                sc2 = sq.op("vector", p_scan2, strict=[cpi],
                            tag=f"p7sc{o}_{th}")

                def p_eu(e, par=par, o=o, sl=sl):
                    e.mul(ROWA[par][:, sl], ROWA[par][:, sl],
                          vt["eu"][:, o:o + 1])
                    return e.mul(ROWB[par][:, sl], ROWB[par][:, sl],
                                 vt["eu"][:, o:o + 1])
                eu = sq.op("scalar", p_eu, after=[sc2], tag=f"p7eu{o}_{th}")

                def p_wkv(e, par=par, sl=sl):
                    e.tensor_tensor(ROWA[par][:, sl], ROWA[par][:, sl],
                                    SA[:, 0:TH], ALU.add)
                    e.tensor_tensor(ROWB[par][:, sl], ROWB[par][:, sl],
                                    SBt[:, 0:TH], ALU.add)
                    e.reciprocal(ROWB[par][:, sl], ROWB[par][:, sl])
                    e.tensor_tensor(ROWA[par][:, sl], ROWA[par][:, sl],
                                    ROWB[par][:, sl], ALU.mult)
                    return e.tensor_tensor(ROWC[par][:, sl],
                                           ROWC[par][:, sl],
                                           ROWA[par][:, sl], ALU.mult)
                prev_half = sq.op("vector", p_wkv, after=[eu, sgdone],
                                  tag=f"p7wk{o}_{th}")
            p7_post.append(prev_half)
            rw_spills.append(sq.dma("gpsimd", rw_h[o * P:(o + 1) * P, 0:T],
                                    ROWC[par][:, 0:T], after=[prev_half],
                                    tag=f"p7sp{o}"))

        p7_lastmm = proj_phase(
            "p7", wr_h, p7_res_dep, None, p7_entry, p7_evac, p7_post_fn,
            split3=(wshi_h["wr"], wslo_h["wr"], rxhi_h, rxlo_h))

        p9_entry = sq.barrier()
        a2_spills = []
        p9_evs = {}
        p9_lds = {}

        def p9_evac(o, par, i, sl, mm, pb):
            o_, w = sl
            if i == 0:
                x1ld = sq.dma(
                    "sync", ROWA[par][:, 0:TX],
                    x1_h[o * P:(o + 1) * P, 0:TX],
                    after=(p9_entry if o < 2 else [p9_evs[o - 2]]),
                    tag=f"p9x1{o}")
                xpld = sq.dma(
                    "sync", ROWC[par][:, 0:TX],
                    xp_h[o * P:(o + 1) * P, 0:TX],
                    after=(p9_entry if o < 2 else [p9_evs[o - 2]]),
                    tag=f"p9xp{o}")
                p9_lds[o] = (x1ld, xpld)

            def p_ev(e, par=par, o_=o_, w=w, pb=pb):
                return e.tensor_tensor(ROWB[par][:, o_:o_ + w], pb[:, 0:w],
                                       ROWA[par][:, 1 + o_:1 + o_ + w],
                                       ALU.add)
            dep = [mm, p9_lds[o][0]] + ([a2_spills[o - 2]]
                                        if (o >= 2 and i == 0) else [])
            if o >= 2 and i == 0:
                dep += [d for d in (p9_fold_sqr.get(o - 2),
                                    p9_fold_mm.get(o - 2))
                        if d is not None]
            ev = sq.op("vector", p_ev, after=dep, tag=f"p9e{o}_{i}")

            def p_ev2(e, par=par, o_=o_, w=w):
                return e.tensor_tensor(ROWB[par][:, o_:o_ + w],
                                       ROWB[par][:, o_:o_ + w],
                                       ROWC[par][:, 1 + o_:1 + o_ + w],
                                       ALU.add)
            ev2 = sq.op("gpsimd", p_ev2, after=[ev, p9_lds[o][1]],
                        tag=f"p9g{o}_{i}")
            if i == 3:
                p9_evs[o] = ev2
            return ev

        def p9_post(o, par):
            a2_spills.append(sq.dma("gpsimd", a2_h[o * P:(o + 1) * P, 0:T],
                                    ROWB[par][:, 0:T],
                                    after=[p9_evs[o]], tag=f"p9sp{o}"))

        def p9_res_dep(c, half):
            d = [rw_spills[c] if c < len(rw_spills) else None, p7_lastmm]
            return [x for x in d if x is not None]

        p9_fold_sqr = {}
        p9_fold_mm = {}
        bsT = mk_bounds(0, [w for _, w in SL(T)])

        def p9_fold_cb(o, par):
            if o > 0:
                po = o - 1
                p9_fold_mm[po] = stats_acc(
                    f"p9f{po}",
                    lambda o_, w, pp=po % 2: ROWB[pp][:, o_:o_ + w],
                    lambda o_, w: RB[:, o_:o_ + w],
                    bsT, po == 0, False, [p9_fold_sqr[po]])
            p9_fold_sqr[o] = sq.op(
                "scalar",
                lambda e, par=par: e.activation(RB[:, 0:T],
                                                ROWB[par][:, 0:T],
                                                AF.Square),
                after=[p9_evs[o]] + ([p9_fold_mm[o - 1]] if o > 0 else []),
                tag=f"p9q{o}")

        proj_phase("p9", wo_h, p9_res_dep, rw_h, p9_entry, p9_evac,
                   p9_post, fold_cb=p9_fold_cb)
        p9_fold_mm[C - 1] = stats_acc(
            f"p9f{C - 1}",
            lambda o_, w, pp=(C - 1) % 2: ROWB[pp][:, o_:o_ + w],
            lambda o_, w: RB[:, o_:o_ + w],
            bsT, False, True, [p9_fold_sqr[C - 1]])

        p10_entry = sq.barrier()
        p10_fins = quant_phase("pa", lambda c: a2_h[c * P:(c + 1) * P, :],
                               [(0, 1024), (1024, 1024)], p10_entry,
                               packed=(bsT, [p9_fold_mm[C - 1]]))

        y_spills, _, _ = ffn_mm_phase(
            "pb", T,
            addsrc_fn=lambda o: a2_h[o * P:(o + 1) * P, 0:T],
            dst_fn=lambda o: y_h[o * P:(o + 1) * P, 0:T],
            entryA=[p10_fins[1]], entryB=[p10_fins[1]])

        sq.op("gpsimd", lambda e: None, after=y_spills, tag="final")

        sq.finalize()

        with nc.Block() as block:
            @block.sync
            def _(eng):
                sq.run_engine("sync", eng)

            @block.tensor
            def _(eng):
                sq.run_engine("tensor", eng)

            @block.vector
            def _(eng):
                sq.run_engine("vector", eng)

            @block.scalar
            def _(eng):
                sq.run_engine("scalar", eng)

            @block.gpsimd
            def _(eng):
                sq.run_engine("gpsimd", eng)

    return nc


def _vecfmt(v):
    return np.ascontiguousarray(v.reshape(C, P).T.astype(np.float32))


def prepare_in_maps(**inputs):
    x = np.asarray(inputs["L0_attn_x"], dtype=np.float32)
    Wk = np.ascontiguousarray(np.asarray(inputs["L0_attn_Wk"], np.float32))
    Wv = np.ascontiguousarray(np.asarray(inputs["L0_attn_Wv"], np.float32))
    Wr = np.ascontiguousarray(np.asarray(inputs["L0_attn_Wr"], np.float32))
    Wo = np.ascontiguousarray(np.asarray(inputs["L0_attn_Wo"], np.float32))
    Wt = np.ascontiguousarray(np.asarray(inputs["L0_ffn_W"], np.float32).T)
    vec = {
        "lnsc": _vecfmt(np.asarray(inputs["L0_attn_ln_scale"], np.float32)),
        "lnb": _vecfmt(np.asarray(inputs["L0_attn_ln_bias"], np.float32)),
        "mk": _vecfmt(np.asarray(inputs["L0_attn_mix_k"], np.float32)),
        "mv": _vecfmt(np.asarray(inputs["L0_attn_mix_v"], np.float32)),
        "mr": _vecfmt(np.asarray(inputs["L0_attn_mix_r"], np.float32)),
        "tf": _vecfmt(np.asarray(inputs["L0_attn_time_first"], np.float32)),
        "td": _vecfmt(np.asarray(inputs["L0_attn_time_decay"], np.float32)),
        "gam": _vecfmt(np.asarray(inputs["L0_ffn_gamma"], np.float32)),
    }
    ident = np.eye(P, dtype=np.float32)
    onesrow = np.ones((1, P), np.float32)
    onescol = np.ones((P, 1), np.float32)

    in_maps = []
    for core in range(NCORE):
        b, h = core // 2, core % 2
        t0 = h * T
        xf = np.zeros((D, CW), np.float32)
        xf[:, 1:TX] = x[b, t0:t0 + T, :].T
        if h == 1:
            xf[:, 0] = x[b, t0 - 1, :]
        pmask = np.full((P, 1), float(h), np.float32)
        selmask = np.zeros((P, NCORE, 32), np.float32)
        if h == 1:
            selmask[:, core - 1, :] = 1.0
        m = dict(x=np.ascontiguousarray(xf), wk=Wk, wv=Wv, wr=Wr, wo=Wo,
                 wt=Wt, ident=ident, onesrow=onesrow, onescol=onescol,
                 pmask=pmask,
                 selmask=np.ascontiguousarray(selmask.reshape(P,
                                                              NCORE * 32)))
        m.update(vec)
        in_maps.append(m)
    return in_maps


def kernel(**inputs):
    in_maps = prepare_in_maps(**inputs)
    nc = build_nc()
    res = run_bass_kernel_spmd(nc, in_maps, core_ids=list(range(NCORE)))
    out = np.empty((4, 4096, D), np.float32)
    for core in range(NCORE):
        b, h = core // 2, core % 2
        out[b, h * T:(h + 1) * T, :] = res.results[core]["y"].T
    return out


# version 31
# speedup vs baseline: 1.0194x; 1.0011x over previous
import numpy as np
from contextlib import ExitStack

import concourse.bass as bass
import concourse.mybir as mybir
from concourse.bass_utils import run_bass_kernel_spmd

F32 = mybir.dt.float32
BF16 = mybir.dt.bfloat16
AF = mybir.ActivationFunctionType
ALU = mybir.AluOpType
AX = mybir.AxisListType

D = 2048
C = 16
P = 128
T = 2048
TX = 2049
CW = 2052
HW = 1028
TH = 1024
MAGIC = float(3 << 22)
NCORE = 8

LN_EPS = 1e-5
RMS_EPS = 1e-6
A_EPS = 1e-5
W_EPS = 1e-5

ENGS = ("sync", "tensor", "vector", "scalar", "gpsimd")
DMA_ENGS = ("sync", "scalar", "gpsimd")
DMA_POOL = 6

TAP = ()
NO_CC = False
STOP_AT = 99
LOADTEST = None


def SL(total, start=0, maxw=512):
    out = []
    o = 0
    while o < total:
        w = min(maxw, total - o)
        out.append([start + o, w])
        o += w
    if len(out) >= 2 and out[-1][1] < 64:
        spare = out[-2][1] + out[-1][1]
        out[-2][1] = spare // 2
        out[-1] = [out[-2][0] + spare // 2, spare - spare // 2]
    return [tuple(x) for x in out]


class Op:
    __slots__ = ("eng", "emit", "after", "dma", "ms", "sem", "val", "prev",
                 "tag")

    def __init__(self, eng, emit, after, dma, tag, strict=()):
        self.eng, self.emit, self.dma, self.tag = eng, emit, dma, tag
        self.after = [a for a in after
                      if a is not None and (a.eng != eng or a.dma)]
        self.after += [a for a in strict if a is not None]
        self.ms = False
        self.sem = None
        self.val = None
        self.prev = 0


class Seq:
    def __init__(self, nc, est):
        self.nc = nc
        self.ops = []
        self.last = {e: None for e in ENGS}
        self.isem = {e: est.enter_context(nc.semaphore(f"i_{e}"))
                     for e in ENGS}
        self.icnt = {e: 0 for e in ENGS}
        self.dsem = {e: [est.enter_context(nc.semaphore(f"d_{e}{i}"))
                         for i in range(DMA_POOL)] for e in DMA_ENGS}
        self.dcnt = {e: [0] * DMA_POOL for e in DMA_ENGS}
        self.duse = {e: 0 for e in DMA_ENGS}
        self.dlog = {e: [] for e in DMA_ENGS}

    def op(self, eng, emit, after=(), tag="", strict=()):
        o = Op(eng, emit, after, False, tag, strict)
        self.ops.append(o)
        self.last[eng] = o
        return o

    def dma(self, eng, out, in_, after=(), tag="", strict=()):
        def emit(e, out=out, in_=in_):
            return e.dma_start(out=out, in_=in_)
        o = Op(eng, emit, after, True, tag, strict)
        self.ops.append(o)
        self.last[eng] = o
        self.dlog[eng].append(o)
        return o

    def barrier(self):
        snap = [self.last[e] for e in ENGS if self.last[e] is not None]
        for e in DMA_ENGS:
            snap.extend(self.dlog[e][-DMA_POOL:])
        seen, out = set(), []
        for o in snap:
            if id(o) not in seen:
                seen.add(id(o))
                out.append(o)
        return out

    def finalize(self):
        for o in self.ops:
            for a in o.after:
                a.ms = True
        for o in self.ops:
            e = o.eng
            if o.dma:
                slot = self.duse[e] % DMA_POOL
                self.duse[e] += 1
                o.sem = self.dsem[e][slot]
                o.prev = self.dcnt[e][slot]
                self.dcnt[e][slot] += 16
                o.val = self.dcnt[e][slot]
                o.ms = True
            elif o.ms:
                self.icnt[e] += 1
                o.sem = self.isem[e]
                o.val = self.icnt[e]

    def run_engine(self, eng_name, handle):
        for o in self.ops:
            if o.eng != eng_name:
                continue
            for a in o.after:
                handle.wait_ge(a.sem, a.val)
            if o.dma and o.prev:
                handle.wait_ge(o.sem, o.prev)
            inst = o.emit(handle)
            if o.ms:
                if inst is None:
                    raise RuntimeError(f"milestoned op '{o.tag}' returned "
                                       "no instruction")
                inst.then_inc(o.sem, 16 if o.dma else 1)


def build_nc(tap=TAP):
    nc = bass.Bass("TRN2", target_bir_lowering=False)

    def din(name, shape, dt=F32):
        return nc.dram_tensor(name, shape, dt, kind="ExternalInput").ap()

    def dscr(name, shape, dt=F32):
        if name in tap:
            return nc.dram_tensor(name, shape, dt, kind="ExternalOutput").ap()
        return nc.dram_tensor(name, shape, dt).ap()

    x_h = din("x", [D, CW])
    wk_h = din("wk", [D, D])
    wv_h = din("wv", [D, D])
    wr_h = din("wr", [D, D])
    wo_h = din("wo", [D, D])
    wt_h = din("wt", [D, D])
    vecs_h = {n: din(n, [P, C]) for n in
              ("lnsc", "lnb", "mk", "mv", "mr", "tf", "td", "gam")}
    ident_h = din("ident", [P, P])
    onesrow_h = din("onesrow", [1, P])
    onescol_h = din("onescol", [P, 1])
    pmask_h = din("pmask", [P, 1])
    selmask_h = din("selmask", [P, 256])

    y_h = nc.dram_tensor("y", [D, T], F32, kind="ExternalOutput").ap()

    wiT_h = dscr("wiT", [D, D], BF16)
    x1_h = dscr("x1s", [D, CW])
    xp_h = dscr("xps", [D, CW])
    kxhi_h = dscr("kxhi", [D, T], BF16)
    kxlo_h = dscr("kxlo", [D, T], BF16)
    vxhi_h = dscr("vxhi", [D, T], BF16)
    vxlo_h = dscr("vxlo", [D, T], BF16)
    rxhi_h = dscr("rxhi", [D, T], BF16)
    rxlo_h = dscr("rxlo", [D, T], BF16)
    wshi_h = {w: dscr(w + "hi", [D, D], BF16) for w in ("wk", "wv", "wr")}
    wslo_h = {w: dscr(w + "lo", [D, D], BF16) for w in ("wk", "wv", "wr")}
    ek_h = dscr("eks", [D, T])
    ekv_h = dscr("ekvs", [D, T])
    rw_h = dscr("rws", [D, T])
    a2_h = dscr("a2s", [D, T])
    cin_h = nc.dram_tensor("cin_b", [P, 32], F32).ap()
    cout_h = nc.dram_tensor("cout_b", [NCORE * P, 32], F32).ap()

    est = ExitStack()
    sbt = lambda n, s, dt=F32: est.enter_context(nc.sbuf_tensor(n, s, dt))

    with est:
        A1 = sbt("A1", [P, C * CW])
        HB = sbt("HB", [P, C * HW])
        RB = sbt("RB", [P, 2 * 1028])
        XI = A1[:, 0:(C * CW) // 2].bitcast(BF16)
        xic = lambda c: XI[:, c * CW:(c + 1) * CW]
        SH = 8 * CW

        def shslot(i):
            return A1[:, SH + i * HW:SH + (i + 1) * HW]
        R_S, R_Q, R_M, R_T1, R_T2, R_RS, R_RM, R_A = (shslot(i)
                                                      for i in range(8))
        SC1, SC2 = shslot(8), shslot(9)
        SC12 = A1[:, SH + 8 * HW:SH + 10 * HW]
        SC3 = R_T2
        MIXO = [R_S, R_Q, R_A]
        QS = A1[:, SH + 10 * HW:SH + 10 * HW + CW]
        IDENT = A1[:, SH + 10 * HW + CW:SH + 10 * HW + CW + P]
        SB2 = SH + 10 * HW + CW + P
        W0STG = [A1[:, SB2:SB2 + 1024], A1[:, SB2 + 1024:SB2 + 2048]]
        W0HI = A1[:, SB2 + 2048:SB2 + 3072].bitcast(BF16)
        W0LO = RB[:, 1028:2056].bitcast(BF16)
        SPH = [A1[:, SB2 + j * 512:SB2 + (j + 1) * 512
                  ].bitcast(BF16)[:, 0:TH] for j in range(3)]
        SPL = [A1[:, SB2 + 1536 + j * 512:SB2 + 1536 + (j + 1) * 512
                  ].bitcast(BF16)[:, 0:TH] for j in range(3)]
        A1BF = A1[:, :].bitcast(BF16)
        hbc = lambda c: HB[:, c * HW:(c + 1) * HW]

        def hrow(i):
            return HB[:, i * CW:(i + 1) * CW]
        ROWA = [hrow(0), hrow(1)]
        ROWB = [hrow(2), hrow(3)]
        ROWC = [hrow(4), hrow(5)]
        WCOL = [hrow(6), hrow(7)]
        SA = RB[:, 0:1028]
        SBt = RB[:, 1028:2056]

        VEC = sbt("vec", [P, C * 11])
        vt = {}
        for i, n in enumerate(("lnsc", "lnb", "mk", "mv", "mr", "tf", "td",
                               "gam", "lam", "eu", "scr")):
            vt[n] = VEC[:, i * C:(i + 1) * C]
        ONESC = sbt("onesc", [P, 1])
        PMASK = sbt("pmaskt", [P, 1])
        CARRY = sbt("carry", [P, 32])
        WSC = sbt("wsc", [P, 1])
        RROW = sbt("rrow", [1, 1152])
        s_row = RROW[0:1, 0:512]
        q_row = RROW[0:1, 512:1024]
        a_row = RROW[0:1, 0:512]
        ONESR = RROW[0:1, 1024:1152]
        AMT2 = RB[0:16, 0:P]
        AMTM = RB[:, 128:144]
        GATH = RB[:, 256:512]
        SELM = RB[:, 512:768]

        PB = [est.enter_context(nc.psum_tensor(f"pb{i}", [P, 512], F32))
              for i in range(8)]

        sq = Seq(nc, est)

        ld_vec = []
        for ap_in, tile in [
            (vecs_h["lnsc"], vt["lnsc"]), (vecs_h["lnb"], vt["lnb"]),
            (vecs_h["mk"], vt["mk"]), (vecs_h["mv"], vt["mv"]),
            (vecs_h["mr"], vt["mr"]), (vecs_h["tf"], vt["tf"]),
            (vecs_h["td"], vt["td"]), (vecs_h["gam"], vt["gam"]),
            (onesrow_h, ONESR), (onescol_h, ONESC[:, :]),
            (pmask_h, PMASK[:, :]),
        ]:
            ld_vec.append(sq.dma("sync", tile, ap_in, tag="ld_vec"))
        ld_wt = [sq.dma("sync", A1[:, c * CW:c * CW + D],
                        wt_h[c * P:(c + 1) * P, :], tag=f"ld_wt{c}")
                 for c in range(C)]

        sq.op("scalar", lambda e: e.activation(vt["scr"], vt["td"], AF.Exp),
              after=ld_vec, tag="p0exp1")
        sq.op("scalar", lambda e: e.activation(vt["lam"], vt["scr"], AF.Exp,
                                               scale=-1.0), tag="p0lam")
        lam_done = sq.op("scalar",
                         lambda e: e.activation(vt["eu"], vt["tf"], AF.Exp),
                         tag="p0eu")
        def p0_abs(e):
            t = None
            for c in range(C):
                t = e.tensor_reduce(vt["scr"][:, c:c + 1],
                                    A1[:, c * CW:c * CW + D],
                                    AX.X, ALU.add, apply_absolute_value=True)
            return t
        wabs0 = sq.op("vector", p0_abs, after=ld_wt + [lam_done],
                      tag="p0abs0")
        wabs = sq.op("vector",
                     lambda e: e.tensor_reduce(WSC[:, 0:1], vt["scr"],
                                               AX.X, ALU.add),
                     strict=[wabs0], tag="p0abs")

        wsum = sq.op("tensor",
                     lambda e: e.matmul(PB[0][0:1, 0:1], WSC[:, 0:1],
                                        ONESC[:, :], start=True, stop=True),
                     after=[wabs], tag="p0sum")

        w0 = sq.op("vector",
                   lambda e: e.tensor_copy(RROW[0:1, 0:1], PB[0][0:1, 0:1]),
                   after=[wsum], tag="p0w0")
        w1 = sq.op("vector",
                   lambda e: e.tensor_single_scalar(RROW[0:1, 1:2],
                                                    RROW[0:1, 0:1],
                                                    1.0 / (D * D), ALU.mult),
                   strict=[w0], tag="p0w1")
        w2 = sq.op("vector",
                   lambda e: e.tensor_single_scalar(RROW[0:1, 2:3],
                                                    RROW[0:1, 1:2], W_EPS,
                                                    ALU.add),
                   strict=[w1], tag="p0w2")
        wsrow = sq.op("vector",
                      lambda e: e.reciprocal(RROW[0:1, 3:4], RROW[0:1, 2:3]),
                      strict=[w2], tag="p0wsrow")

        wsbc = sq.op("tensor",
                     lambda e: e.matmul(PB[1][:, 0:1], ONESR,
                                        RROW[0:1, 3:4], start=True,
                                        stop=True),
                     after=[wsrow], tag="p0wsbc")
        wsev = sq.op("vector",
                     lambda e: e.tensor_copy(WSC[:, 0:1], PB[1][:, 0:1]),
                     after=[wsbc], tag="p0wsev")

        wq_sp = {}
        for c in range(C):
            par = c % 2
            buf = hrow(par).bitcast(BF16)[:, 0:D]
            scr = hrow(2)[:, 0:D] if par == 0 else hrow(3)[:, 0:D]

            def p0_q(e, c=c, buf=buf, scr=scr):
                e.tensor_single_scalar(scr, A1[:, c * CW:c * CW + D],
                                       WSC[:, 0:1], ALU.mult)
                e.tensor_scalar(scr, scr, MAGIC, MAGIC, ALU.add,
                                ALU.subtract)
                return e.tensor_scalar(buf, scr, 1.0, -1.0, ALU.min, ALU.max)
            qo = sq.op("vector", p0_q,
                       after=[wq_sp[c - 2]] if c >= 2 else [],
                       strict=[wsev] if c < 2 else [],
                       tag=f"p0q{c}")
            wq_sp[c] = sq.dma(
                "gpsimd",
                wiT_h.rearrange("(ob p) (dc ol) -> p ob dc ol",
                                p=P, ol=P)[:, :, c, :],
                buf.rearrange("p (ob ol) -> p ob ol", ol=P),
                after=[qo], tag=f"p0qsp{c}")
        ph0_done = sq.barrier()

        wsplit_done = {}


        def st_sum(i):
            return (PB[6], 32 * i) if i < 3 else (PB[7], 32 * (i - 3))

        def st_sq(i):
            return (PB[4], 32 * i) if i < 3 else (PB[5], 32 * (i - 3))

        def mk_bounds(t0, widths):
            bs = [t0]
            for w in widths:
                bs.append(bs[-1] + w)
            return bs

        def bpieces(bs, a, b):
            out = []
            g = a
            while g < b:
                i = max(j for j in range(len(bs) - 1) if bs[j] <= g)
                end = min(b, bs[i + 1])
                out.append((i, g - bs[i], g - a, end - g))
                g = end
            return out

        def stats_acc(tag, src_fn, sq_fn, bs, first, last, dep):
            def p_mm(e):
                mm = None
                for i in range(len(bs) - 1):
                    o_, w = bs[i], bs[i + 1] - bs[i]
                    bank, base = st_sum(i)
                    e.matmul(bank[base:base + 1, 0:w], ONESC[:, :],
                             src_fn(o_, w), start=first, stop=last)
                    bank, base = st_sq(i)
                    mm = e.matmul(bank[base:base + 1, 0:w], ONESC[:, :],
                                  sq_fn(o_, w), start=first, stop=last)
                return mm
            return sq.op("tensor", p_mm, after=dep, tag=tag)

        def stats_sweep_cls(tag, src_row_fn, t0, W, entry_fn, order=None,
                            sq_guard=()):
            bs = mk_bounds(t0, [w for _, w in SL(W)])
            order = order if order is not None else list(range(C))
            lds = {}
            mm_ops = []
            for idx, c in enumerate(order):
                lds[c] = sq.dma("sync", hbc(c)[:, 0:W],
                                src_row_fn(c)[:, t0:t0 + W],
                                after=entry_fn(c), tag=f"{tag}ld{c}")
                xsq = SC1 if idx % 2 == 0 else SC2
                so = sq.op("scalar",
                           lambda e, c=c, xsq=xsq, W=W:
                               e.activation(xsq[:, 0:W], hbc(c)[:, 0:W],
                                            AF.Square),
                           after=[lds[c]] + list(sq_guard) +
                                 ([mm_ops[idx - 2]] if idx >= 2 else []),
                           tag=f"{tag}sq{c}")
                mm_ops.append(stats_acc(
                    f"{tag}mm{c}",
                    lambda o_, w, c=c: hbc(c)[:, o_ - t0:o_ - t0 + w],
                    lambda o_, w, xsq=xsq, t0=t0:
                        xsq[:, o_ - t0:o_ - t0 + w],
                    bs, idx == 0, idx == C - 1, [so]))
            return lds, mm_ops[-1], bs

        def stats_rows(tag, bs, t0, W, entry):
            bc_prev = None
            for i, (o_, w) in enumerate(SL(W)):
                pcs = bpieces(bs, t0 + o_, t0 + o_ + w)

                def p_cp(e, pcs=pcs):
                    t = None
                    for (sl, off, dst, ln) in pcs:
                        bank, base = st_sum(sl)
                        e.copy(s_row[0:1, dst:dst + ln],
                               bank[base:base + 1, off:off + ln])
                        bank, base = st_sq(sl)
                        t = e.copy(q_row[0:1, dst:dst + ln],
                                   bank[base:base + 1, off:off + ln])
                    return t
                cp = sq.op("scalar", p_cp,
                           after=list(entry) + ([bc_prev] if bc_prev
                                                else []),
                           tag=f"{tag}rc{i}")
                mm = sq.op("tensor",
                           lambda e, w=w:
                               (e.matmul(PB[2][:, 0:w], ONESR,
                                         s_row[0:1, 0:w], start=True,
                                         stop=True),
                                e.matmul(PB[3][:, 0:w], ONESR,
                                         q_row[0:1, 0:w], start=True,
                                         stop=True))[-1],
                           after=[cp], tag=f"{tag}bc{i}")
                bc_prev = sq.op("scalar",
                                lambda e, o_=o_, w=w:
                                    (e.copy(R_S[:, o_:o_ + w],
                                            PB[2][:, 0:w]),
                                     e.copy(R_Q[:, o_:o_ + w],
                                            PB[3][:, 0:w]))[-1],
                                after=[mm], tag=f"{tag}be{i}")
            return bc_prev

        def rowmath(tag, W, entry):
            def p_rm1(e, W=W):
                inv_d = 1.0 / D
                e.tensor_single_scalar(R_M[:, 0:W], R_S[:, 0:W], inv_d,
                                       ALU.mult)
                e.tensor_single_scalar(R_Q[:, 0:W], R_Q[:, 0:W], inv_d,
                                       ALU.mult)
                e.tensor_tensor(R_A[:, 0:W], R_M[:, 0:W], R_M[:, 0:W],
                                ALU.mult)
                e.tensor_tensor(R_Q[:, 0:W], R_Q[:, 0:W], R_A[:, 0:W],
                                ALU.subtract)
                return e.tensor_single_scalar(R_T1[:, 0:W], R_Q[:, 0:W],
                                              LN_EPS, ALU.add)
            rm1 = sq.op("vector", p_rm1, after=[entry], tag=f"{tag}rm1")
            sq1 = sq.op("scalar",
                        lambda e, W=W: e.activation(R_T2[:, 0:W],
                                                    R_T1[:, 0:W], AF.Sqrt),
                        after=[rm1], tag=f"{tag}s1")

            def p_rm2(e, W=W):
                e.reciprocal(R_RS[:, 0:W], R_T2[:, 0:W])
                e.tensor_tensor(R_T2[:, 0:W], R_RS[:, 0:W], R_RS[:, 0:W],
                                ALU.mult)
                e.tensor_tensor(R_T2[:, 0:W], R_T2[:, 0:W], R_T1[:, 0:W],
                                ALU.mult)
                e.tensor_scalar(R_T2[:, 0:W], R_T2[:, 0:W], -0.5, 1.5,
                                ALU.mult, ALU.add)
                return e.tensor_tensor(R_RS[:, 0:W], R_RS[:, 0:W],
                                       R_T2[:, 0:W], ALU.mult)
            return sq.op("vector", p_rm2, after=[sq1], tag=f"{tag}rm2")

        def second_rsqrt(tag, W, eps, entry):
            def p_a(e):
                e.reciprocal(R_RM[:, 0:W], R_T1[:, 0:W])
                e.tensor_tensor(R_RM[:, 0:W], R_RM[:, 0:W], R_Q[:, 0:W],
                                ALU.mult)
                return e.tensor_single_scalar(R_T1[:, 0:W], R_RM[:, 0:W],
                                              eps, ALU.add)
            a = sq.op("vector", p_a, after=[entry], tag=f"{tag}r2a")
            s = sq.op("scalar",
                      lambda e: e.activation(R_T2[:, 0:W], R_T1[:, 0:W],
                                             AF.Sqrt),
                      after=[a], tag=f"{tag}r2s")

            def p_b(e):
                e.reciprocal(R_RM[:, 0:W], R_T2[:, 0:W])
                e.tensor_tensor(R_T2[:, 0:W], R_RM[:, 0:W], R_RM[:, 0:W],
                                ALU.mult)
                e.tensor_tensor(R_T2[:, 0:W], R_T2[:, 0:W], R_T1[:, 0:W],
                                ALU.mult)
                e.tensor_scalar(R_T2[:, 0:W], R_T2[:, 0:W], -0.5, 1.5,
                                ALU.mult, ALU.add)
                return e.tensor_tensor(R_RM[:, 0:W], R_RM[:, 0:W],
                                       R_T2[:, 0:W], ALU.mult)
            return sq.op("vector", p_b, after=[s], tag=f"{tag}r2b")

        fxmaps = {}

        def quant_phase(tag, src_row_fn, halves, entry,
                        sweep0_entry_fn=None, sweep0_order=None,
                        packed=None):
            idld = sq.dma("sync", IDENT, ident_h, after=entry,
                          tag=f"{tag}id")
            amz = sq.op("vector", lambda e: e.memset(AMTM, 0.0),
                        after=entry, tag=f"{tag}amz")
            prev = entry
            fins = []
            prev_finrow = None
            _fxm = fxmaps.setdefault(tag, {})
            for hi, (t0, W) in enumerate(halves):
                slh = SL(W)
                nblk = (W + P - 1) // P
                if hi == 0 and sweep0_entry_fn is not None:
                    efn = sweep0_entry_fn
                    order = sweep0_order
                else:
                    efn = lambda c, prev=prev, hi=hi: \
                        ([_fxm[(hi - 1, c)]] if (hi - 1, c) in _fxm
                         else list(prev))
                    order = None
                if packed is None:
                    lds, lastmm, bs = stats_sweep_cls(
                        f"{tag}{hi}", src_row_fn, t0, W, efn, order,
                        sq_guard=([prev_finrow] if (hi > 0 and prev_finrow
                                                    is not None) else ()))
                    acc = [lastmm]
                else:
                    bs, acc = packed
                    lds = {c: sq.dma("sync", hbc(c)[:, 0:W],
                                     src_row_fn(c)[:, t0:t0 + W],
                                     after=efn(c),
                                     tag=f"{tag}{hi}ld{c}")
                           for c in range(C)}
                be = stats_rows(f"{tag}{hi}", bs, t0, W,
                                list(acc) + (list(prev) if hi else []))
                rm2 = rowmath(f"{tag}{hi}", W, be)
                rmm = second_rsqrt(f"{tag}{hi}", W, RMS_EPS, rm2)

                subg_ops = []
                for c in range(C):
                    eng = "vector"

                    def p_sg(e, c=c, W=W):
                        e.tensor_tensor(hbc(c)[:, 0:W], hbc(c)[:, 0:W],
                                        R_M[:, 0:W], ALU.subtract)
                        return e.tensor_single_scalar(hbc(c)[:, 0:W],
                                                      hbc(c)[:, 0:W],
                                                      vt["gam"][:, c:c + 1],
                                                      ALU.mult)
                    subg_ops.append(sq.op(eng, p_sg, after=[rmm, lds[c]],
                                          tag=f"{tag}{hi}sg{c}"))
                tree_ops = []
                for c in range(C):
                    buf = SC3 if c == 0 else (SC1 if c % 2 == 1 else SC2)
                    adep = [subg_ops[c]] + ([tree_ops[c - 2]]
                                            if c >= 3 else [])
                    ab = sq.op("scalar",
                               lambda e, c=c, buf=buf, W=W:
                                   e.activation(buf[:, 0:W], hbc(c)[:, 0:W],
                                                AF.Abs),
                               after=adep, tag=f"{tag}{hi}ab{c}")
                    if c == 0:
                        tree_ops.append(ab)
                    else:
                        tree_ops.append(sq.op(
                            "vector",
                            lambda e, buf=buf, W=W:
                                e.tensor_tensor(SC3[:, 0:W], SC3[:, 0:W],
                                                buf[:, 0:W], ALU.max),
                            after=[ab], tag=f"{tag}{hi}mx{c}"))
                tree = tree_ops[-1]

                red_ops = []
                for j in range(nblk):
                    w = min(P, W - j * P)
                    pb = PB[j % 2]
                    tr = sq.op("tensor",
                               lambda e, pb=pb, j=j, w=w:
                                   e.transpose(pb[0:w, 0:P],
                                               SC3[:, j * P:j * P + w],
                                               IDENT),
                               after=[tree, idld] + ([red_ops[j - 2]]
                                                     if j >= 2 else []),
                               tag=f"{tag}{hi}tp{j}")
                    red_ops.append(sq.op(
                        "vector",
                        lambda e, pb=pb, j=j, w=w:
                            e.tensor_reduce(AMTM[0:w, j:j + 1], pb[0:w, 0:P],
                                            AX.X, ALU.max),
                        after=[tr, amz], tag=f"{tag}{hi}rd{j}"))
                tr2 = sq.op("tensor",
                            lambda e: e.transpose(PB[2][0:16, 0:P],
                                                  AMTM, IDENT),
                            after=red_ops, tag=f"{tag}{hi}tp2")
                amev = sq.op("vector",
                             lambda e: e.tensor_copy(AMT2,
                                                     PB[2][0:16, 0:P]),
                             after=[tr2], tag=f"{tag}{hi}ame")
                bc_ev = None
                for i, (o_, w) in enumerate(slh):
                    jb = o_ // P
                    nfull = w // P
                    rem = w - nfull * P
                    adep = [amev] + ([bc_ev] if bc_ev else [])
                    ads = []
                    if nfull:
                        ads.append(sq.dma(
                            "gpsimd", a_row[0:1, 0:nfull * P],
                            AMT2[jb:jb + nfull, :], after=adep,
                            tag=f"{tag}{hi}ad{i}"))
                    if rem:
                        ads.append(sq.dma(
                            "gpsimd",
                            a_row[0:1, nfull * P:nfull * P + rem],
                            AMT2[jb + nfull:jb + nfull + 1, 0:rem],
                            after=adep, tag=f"{tag}{hi}ar{i}"))
                    mm = sq.op("tensor",
                               lambda e, w=w:
                                   e.matmul(PB[3][:, 0:w], ONESR,
                                            a_row[0:1, 0:w], start=True,
                                            stop=True),
                               after=ads + ([bc_ev] if bc_ev else []),
                               tag=f"{tag}{hi}am{i}")
                    bc_ev = sq.op("scalar",
                                  lambda e, o_=o_, w=w:
                                      e.copy(R_A[:, o_:o_ + w],
                                             PB[3][:, 0:w]),
                                  after=[mm], tag=f"{tag}{hi}ae{i}")

                def p_finrow(e, W=W, t0=t0):
                    e.tensor_tensor(R_T1[:, 0:W], R_RS[:, 0:W], R_RM[:, 0:W],
                                    ALU.mult)
                    e.tensor_tensor(R_A[:, 0:W], R_A[:, 0:W], R_T1[:, 0:W],
                                    ALU.mult)
                    e.tensor_single_scalar(R_A[:, 0:W], R_A[:, 0:W], A_EPS,
                                           ALU.add)
                    e.reciprocal(R_A[:, 0:W], R_A[:, 0:W])
                    e.tensor_single_scalar(R_A[:, 0:W], R_A[:, 0:W], 127.0,
                                           ALU.mult)
                    e.tensor_single_scalar(R_T2[:, 0:W], R_A[:, 0:W],
                                           WSC[:, 0:1], ALU.mult)
                    e.reciprocal(R_T2[:, 0:W], R_T2[:, 0:W])
                    e.tensor_single_scalar(QS[:, t0:t0 + W], R_T2[:, 0:W],
                                           0.5, ALU.mult)
                    return e.tensor_tensor(R_T1[:, 0:W], R_T1[:, 0:W],
                                           R_A[:, 0:W], ALU.mult)
                finrow = sq.op("vector", p_finrow, after=[bc_ev],
                               tag=f"{tag}{hi}fr")
                fin = None
                fxmap = _fxm
                for c in range(C):
                    def p_fx(e, c=c, W=W, t0=t0):
                        e.tensor_tensor(hbc(c)[:, 0:W], hbc(c)[:, 0:W],
                                        R_T1[:, 0:W], ALU.mult)
                        return e.tensor_scalar(xic(c)[:, t0:t0 + W],
                                               hbc(c)[:, 0:W], MAGIC,
                                               MAGIC, ALU.add,
                                               ALU.subtract)
                    fin = sq.op("vector", p_fx, after=[finrow],
                                tag=f"{tag}{hi}fx{c}")
                    fxmap[(hi, c)] = fin
                fins.append(fin)
                prev = [fin]
                prev_finrow = finrow
            return fins

        if LOADTEST is not None and LOADTEST < 1:
            p1_fins = [ph0_done[0], ph0_done[0]]
        else:
            p1_fins = quant_phase(
                "p1", lambda c: x_h[c * P:(c + 1) * P, :],
                [(0, 1024), (1024, 1025)], ph0_done,
                sweep0_entry_fn=lambda c: ([] if c >= 8
                                           else list(ph0_done)),
                sweep0_order=list(range(8, 16)) + list(range(8)))
        if "xid" in tap:
            xid_h = nc.dram_tensor("xid", [D, CW], BF16,
                                   kind="ExternalOutput").ap()
            qsd_h = nc.dram_tensor("qsd", [P, CW], F32,
                                   kind="ExternalOutput").ap()
            for c in range(C):
                sq.dma("gpsimd", xid_h[c * P:(c + 1) * P, :], xic(c),
                       after=[p1_fins[-1]], tag=f"xid{c}")
            sq.dma("gpsimd", qsd_h, QS, after=[p1_fins[-1]], tag="qsd")

        def ffn_mm_phase(tag, tx, addsrc_fn, dst_fn, entryA, entryB,
                         fold=False, filler=None, pre_w=None, pre_x=None):
            slt = SL(tx)
            bsf = mk_bounds(0, [w for _, w in slt])
            ns = len(slt)
            ev_flat = []
            sp_ops = []
            mm_flat = []
            ev2_last = {}
            fsqr = {}
            ffmm = {}
            for o in range(C):
                par = o % 2
                if o < 2:
                    wdep = pre_w(o) if pre_w is not None else list(entryA)
                else:
                    wdep = [mm_flat[(o - 2) * ns + ns - 1]]
                wld = sq.dma(
                    "sync",
                    WCOL[par].bitcast(BF16)[:, 0:C * P],
                    wiT_h[o * P:(o + 1) * P, :],
                    after=wdep, tag=f"{tag}w{o}")
                if o < 2:
                    xdep = pre_x(o) if pre_x is not None else list(entryA)
                else:
                    xdep = [ev2_last[o - 2]]
                xld = sq.dma("sync", ROWA[par][:, 0:tx], addsrc_fn(o),
                             after=xdep, tag=f"{tag}x{o}")
                ev2 = None
                for i, (o_, w) in enumerate(slt):
                    g = o * ns + i
                    pb = PB[g % 2]

                    def p_mm(e, par=par, o_=o_, w=w, pb=pb):
                        mm = None
                        for d in range(C):
                            mm = e.matmul(
                                pb[:, 0:w],
                                WCOL[par].bitcast(BF16)[:,
                                                        d * P:(d + 1) * P],
                                xic(d)[:, o_:o_ + w],
                                start=(d == 0), stop=(d == C - 1))
                        return mm
                    mdep = [wld] + ([ev_flat[g - 2]] if g >= 2 else [])
                    if o == 0:
                        mdep += list(entryA) if (o_ + w <= 1024) else \
                            list(entryB)
                    mm = sq.op("tensor", p_mm, after=mdep,
                               tag=f"{tag}m{o}_{i}")
                    mm_flat.append(mm)

                    def p_ev(e, par=par, o_=o_, w=w, pb=pb):
                        return e.tensor_tensor(ROWB[par][:, o_:o_ + w],
                                               pb[:, 0:w],
                                               QS[:, o_:o_ + w], ALU.mult)
                    edep = [mm] + ([sp_ops[o - 2]]
                                   if (o >= 2 and i == 0) else [])
                    if fold and o >= 2 and i == 0:
                        edep += [fsqr[o - 2], ffmm.get(o - 2)]
                        edep = [d for d in edep if d is not None]
                    ev = sq.op("vector", p_ev, after=edep,
                               tag=f"{tag}e{o}_{i}")
                    ev_flat.append(ev)

                    def p_ev2(e, par=par, o_=o_, w=w):
                        return e.tensor_tensor(ROWB[par][:, o_:o_ + w],
                                               ROWB[par][:, o_:o_ + w],
                                               ROWA[par][:, o_:o_ + w],
                                               ALU.add)
                    ev2 = sq.op("gpsimd", p_ev2, after=[ev, xld],
                                tag=f"{tag}g{o}_{i}")
                ev2_last[o] = ev2
                dst = dst_fn(o)
                sp_ops.append(sq.dma("gpsimd", dst, ROWB[par][:, 0:tx],
                                     after=[ev2], tag=f"{tag}s{o}"))
                if fold:
                    if o > 0:
                        po = o - 1
                        ffmm[po] = stats_acc(
                            f"{tag}f{po}",
                            lambda o_, w, pp=po % 2:
                                ROWB[pp][:, o_:o_ + w],
                            lambda o_, w: SC12[:, o_:o_ + w],
                            bsf, po == 0, False, [fsqr[po]])
                    fsqr[o] = sq.op(
                        "scalar",
                        lambda e, par=par, tx=tx:
                            e.activation(SC12[:, 0:tx], ROWB[par][:, 0:tx],
                                         AF.Square),
                        after=[ev2] + ([ffmm[o - 1]] if o > 0 else []),
                        tag=f"{tag}q{o}")
                if filler is not None:
                    filler(6)
            fold_out = None
            if fold:
                ffmm[C - 1] = stats_acc(
                    f"{tag}f{C - 1}",
                    lambda o_, w, pp=(C - 1) % 2: ROWB[pp][:, o_:o_ + w],
                    lambda o_, w: SC12[:, o_:o_ + w],
                    bsf, False, True, [fsqr[C - 1]])
                fold_out = (bsf, [ffmm[C - 1]])
            if filler is not None:
                filler(1000)
            return sp_ops, mm_flat[-1], fold_out

        def p0b_gen():
            pw = {"h": None, "l": None, "s1": None, "s2": None}
            qs = ["sync", "scalar"]
            pi = 0
            for wname, w_src in (("wr", wr_h),):
                for c in range(C):
                    sps = []
                    for h in range(2):
                        p = pi % 2
                        dep = [x for x in (pw["h"], pw["l"])
                               if x is not None] or list(ph0_done)
                        wl = sq.dma(qs[p], W0STG[p],
                                    w_src[c * P:(c + 1) * P,
                                          h * 1024:(h + 1) * 1024],
                                    after=dep, tag=f"b{wname}l{c}_{h}")
                        hi = sq.op(
                            "scalar",
                            lambda e, p=p, h=h:
                                e.copy(W0HI[:, h * 1024:(h + 1) * 1024],
                                       W0STG[p]),
                            after=[x for x in
                                   (wl, pw["l"], pw["s1"])
                                   if x is not None],
                            tag=f"b{wname}h{c}_{h}")
                        lo = sq.op(
                            "vector",
                            lambda e, p=p, h=h:
                                e.tensor_tensor(
                                    W0LO[:, h * 1024:(h + 1) * 1024],
                                    W0STG[p],
                                    W0HI[:, h * 1024:(h + 1) * 1024],
                                    ALU.subtract),
                            after=[x for x in (hi, pw["s2"])
                                   if x is not None],
                            tag=f"b{wname}o{c}_{h}")
                        pw["h"] = hi
                        pw["l"] = lo
                        pi += 1
                        if h == 1:
                            s1 = sq.dma(
                                "gpsimd",
                                wshi_h[wname][c * P:(c + 1) * P, :],
                                W0HI, after=[pw["h"], hi],
                                tag=f"b{wname}s{c}")
                            s2 = sq.dma(
                                "gpsimd",
                                wslo_h[wname][c * P:(c + 1) * P, :],
                                W0LO[:, 0:2048], after=[lo],
                                tag=f"b{wname}t{c}")
                            pw["s1"] = s1
                            pw["s2"] = s2
                        yield
            wsplit_done["all"] = [x for x in (pw["h"], pw["l"],
                                              pw["s1"], pw["s2"])
                                  if x is not None]

        p0b_it = p0b_gen()

        def p0b_filler(k):
            for _ in range(k):
                if next(p0b_it, "END") == "END":
                    return

        _fx1 = fxmaps["p1"]

        def p2_pre_w(o):
            return ([_fx1[(1, 11)], _fx1[(1, 12)], _fx1[(1, 13)]]
                    if o == 0 else
                    [_fx1[(1, 13)], _fx1[(1, 14)], _fx1[(1, 15)]])

        def p2_pre_x(o):
            return ([_fx1[(1, 0)], _fx1[(1, 1)]] if o == 0 else
                    [_fx1[(1, 1)], _fx1[(1, 2)], _fx1[(1, 3)]])
        p2_sp, p2_lastmm, p2_fold = ffn_mm_phase(
            "p2", TX,
            addsrc_fn=lambda o: x_h[o * P:(o + 1) * P, 0:TX],
            dst_fn=lambda o: x1_h[o * P:(o + 1) * P, 0:TX],
            entryA=[p1_fins[1]], entryB=[p1_fins[1]],
            fold=True, filler=p0b_filler,
            pre_w=p2_pre_w, pre_x=p2_pre_x)

        mix_spills = []
        xp_spills = []
        kxsp = {}
        vxsp = {}
        rxsp = {}
        _p3split = {}
        _p3sp1 = {}
        _p3sp2 = {}
        for _j in range(3):
            _p3sp1[(-1, _j)] = wsplit_done.get("all", [])
            _p3sp2[(-1, _j)] = wsplit_done.get("all", [])
        prev3 = sq.barrier()
        p2_bar = prev3
        for hi, (t0, W) in enumerate([(0, 1025), (1024, 1025)]
                                     if (LOADTEST is None or LOADTEST >= 3)
                                     else []):
            if hi == 1:
                for _j in range(3):
                    _p3sp1[(-1, _j)] = _p3sp1.get((C - 1, _j), [])
                    _p3sp2[(-1, _j)] = _p3sp2.get((C - 1, _j), [])
                _p3split[-1] = _p3split.get(C - 1, [])
            lds = {c: sq.dma("sync", hbc(c)[:, 0:W],
                             x1_h[c * P:(c + 1) * P, t0:t0 + W],
                             after=list(prev3), tag=f"p3{hi}ld{c}")
                   for c in range(C)}
            be3 = stats_rows(f"p3{hi}", p2_fold[0], t0, W,
                             list(p2_fold[1]) + (list(prev3) if hi
                                                 else []))
            rm2 = rowmath(f"p3{hi}", W, be3)
            s2 = second_rsqrt(f"p3{hi}", W, LN_EPS, rm2)
            rr = sq.op("vector",
                       lambda e, W=W: e.tensor_tensor(R_T1[:, 0:W],
                                                      R_RS[:, 0:W],
                                                      R_RM[:, 0:W],
                                                      ALU.mult),
                       after=[s2], tag=f"p3{hi}rr")

            xn_ops = []
            for c in range(C):
                xsp = SC1 if c % 2 == 0 else SC2

                def p_xp(e, c=c, W=W, xsp=xsp):
                    e.tensor_tensor(hbc(c)[:, 0:W], hbc(c)[:, 0:W],
                                    R_M[:, 0:W], ALU.subtract)
                    return e.tensor_tensor(xsp[:, 0:W], hbc(c)[:, 0:W],
                                           R_RS[:, 0:W], ALU.mult)
                xp = sq.op("gpsimd", p_xp,
                           after=[rr, lds[c]] + ([xp_spills[-2]]
                                                 if c >= 2 else []),
                           tag=f"p3{hi}xp{c}")
                xp_spills.append(sq.dma(
                    "gpsimd", xp_h[c * P:(c + 1) * P, t0:t0 + W],
                    xsp[:, 0:W], after=[xp], tag=f"p3{hi}xsp{c}"))

                def p_xn(e, c=c, W=W, hi=hi):
                    e.tensor_tensor(hbc(c)[:, 0:W], hbc(c)[:, 0:W],
                                    R_T1[:, 0:W], ALU.mult)
                    t = e.tensor_scalar(hbc(c)[:, 0:W], hbc(c)[:, 0:W],
                                        vt["lnsc"][:, c:c + 1],
                                        vt["lnb"][:, c:c + 1],
                                        ALU.mult, ALU.add)
                    if hi == 0:
                        t = e.tensor_single_scalar(hbc(c)[:, 0:1],
                                                   hbc(c)[:, 0:1],
                                                   PMASK[:, 0:1], ALU.mult)
                    return t
                xn_ops.append(sq.op("vector", p_xn, after=[xp],
                                    tag=f"p3{hi}xn{c}"))

            last_mix = xn_ops[-1]
            for c in range(C):
                def p_mix(e, c=c, W=W):
                    e.tensor_tensor(SC3[:, 0:W - 1], hbc(c)[:, 1:W],
                                    hbc(c)[:, 0:W - 1], ALU.subtract)
                    t = None
                    for j, mv in enumerate(("mk", "mv", "mr")):
                        t = e.scalar_tensor_tensor(
                            MIXO[j][:, 0:W - 1], SC3[:, 0:W - 1],
                            vt[mv][:, c:c + 1], hbc(c)[:, 0:W - 1],
                            ALU.mult, ALU.add)
                    return t
                mdep = [last_mix, xn_ops[c]] + _p3split.get(c - 1, [])
                mx = sq.op("vector", p_mix, after=mdep, tag=f"p3{hi}mx{c}")
                last_mix = mx
                _p3split[c] = []
                for j, (hih, loh, spd) in enumerate((
                        (kxhi_h, kxlo_h, kxsp), (vxhi_h, vxlo_h, vxsp),
                        (rxhi_h, rxlo_h, rxsp))):
                    hic = sq.op("scalar",
                                lambda e, j=j, W=W:
                                    e.copy(SPH[j][:, 0:W - 1],
                                           MIXO[j][:, 0:W - 1]),
                                after=[mx] + _p3sp1.get((c - 1, j), []),
                                tag=f"p3{hi}hc{c}_{j}")
                    loc = sq.op("vector",
                                lambda e, j=j, W=W:
                                    e.tensor_tensor(SPL[j][:, 0:W - 1],
                                                    MIXO[j][:, 0:W - 1],
                                                    SPH[j][:, 0:W - 1],
                                                    ALU.subtract),
                                after=[hic] + _p3sp2.get((c - 1, j), []),
                                tag=f"p3{hi}lc{c}_{j}")
                    sp1 = sq.dma(
                        "gpsimd", hih[c * P:(c + 1) * P, t0:t0 + W - 1],
                        SPH[j][:, 0:W - 1], after=[hic],
                        tag=f"p3{hi}s1{c}_{j}")
                    sp2 = sq.dma(
                        "gpsimd", loh[c * P:(c + 1) * P, t0:t0 + W - 1],
                        SPL[j][:, 0:W - 1], after=[loc],
                        tag=f"p3{hi}s2{c}_{j}")
                    mix_spills += [sp1, sp2]
                    _p3sp1[(c, j)] = [sp1]
                    _p3sp2[(c, j)] = [sp2]
                    _p3split[c] += [hic, loc]
                    spd[(hi, c)] = [sp1, sp2]
            prev3 = [last_mix] + mix_spills[-6:]
        p3_bar = sq.barrier()

        def proj_phase(tag, w_hbm, res_dep_fn, res_hbm, entry, make_evac,
                       post_o, split3=None, fold_cb=None, otf=False):
            slt = SL(T)
            ns = len(slt)
            otf_lo = {}
            mmh_last = {}
            res_half = {0: [], 1: []}
            for c in range(C):
                for half in range(2):
                    if split3 is None:
                        res_half[half].append(sq.dma(
                            "sync",
                            A1[:, c * CW + half * TH:
                               c * CW + (half + 1) * TH],
                            res_hbm[c * P:(c + 1) * P,
                                    half * TH:(half + 1) * TH],
                            after=res_dep_fn(c, half),
                            tag=f"{tag}r{c}_{half}"))
                    else:
                        _, _, reshi, reslo = split3
                        dep = res_dep_fn(c, half)
                        res_half[half].append(sq.dma(
                            "sync",
                            A1BF[:, c * 2 * CW + half * TH:
                                 c * 2 * CW + (half + 1) * TH],
                            reshi[c * P:(c + 1) * P,
                                  half * TH:(half + 1) * TH],
                            after=dep, tag=f"{tag}rh{c}_{half}"))
                        res_half[half].append(sq.dma(
                            "sync",
                            A1BF[:, c * 2 * CW + T + half * TH:
                                 c * 2 * CW + T + (half + 1) * TH],
                            reslo[c * P:(c + 1) * P,
                                  half * TH:(half + 1) * TH],
                            after=dep, tag=f"{tag}rl{c}_{half}"))
            evl = {}
            mm_last = {}
            for o in range(C):
                par = o % 2
                wdep = list(entry) if o < 2 else [mm_last[o - 2]]
                if otf:
                    wdep2 = list(entry) if o < 2 else \
                        [otf_lo[o - 2]]
                    wl = sq.dma(
                        "sync",
                        ROWC[par][:, 0:C * P].rearrange(
                            "p (dc oc) -> p dc oc", dc=C),
                        w_hbm.rearrange("(dc p) o -> p dc o", p=P)[
                            :, :, o * P:(o + 1) * P],
                        after=wdep2, tag=f"{tag}w{o}")
                    wcb = WCOL[par].bitcast(BF16)
                    hi = sq.op(
                        "gpsimd",
                        lambda e, par=par, wcb=wcb:
                            e.tensor_copy(wcb[:, 0:C * P],
                                          ROWC[par][:, 0:C * P]),
                        after=[wl] + ([mmh_last[o - 2]] if o >= 2 else []),
                        tag=f"{tag}wh{o}")
                    lo = sq.op(
                        "gpsimd",
                        lambda e, par=par, wcb=wcb:
                            e.tensor_tensor(wcb[:, C * P:2 * C * P],
                                            ROWC[par][:, 0:C * P],
                                            wcb[:, 0:C * P],
                                            ALU.subtract),
                        after=[hi] + ([mm_last[o - 2]] if o >= 2 else []),
                        tag=f"{tag}wo{o}")
                    otf_lo[o] = lo
                    wlds = [hi, lo]
                elif split3 is None:
                    wlds = [sq.dma(
                        "sync",
                        WCOL[par][:, 0:C * P].rearrange(
                            "p (dc oc) -> p dc oc", dc=C),
                        w_hbm.rearrange("(dc p) o -> p dc o", p=P)[
                            :, :, o * P:(o + 1) * P],
                        after=wdep, tag=f"{tag}w{o}")]
                else:
                    whi, wlo = split3[0], split3[1]
                    wcb = WCOL[par].bitcast(BF16)
                    wlds = [
                        sq.dma("sync",
                               wcb[:, 0:C * P].rearrange(
                                   "p (dc oc) -> p dc oc", dc=C),
                               whi.rearrange("(dc p) o -> p dc o", p=P)[
                                   :, :, o * P:(o + 1) * P],
                               after=wdep, tag=f"{tag}wh{o}"),
                        sq.dma("sync",
                               wcb[:, C * P:2 * C * P].rearrange(
                                   "p (dc oc) -> p dc oc", dc=C),
                               wlo.rearrange("(dc p) o -> p dc o", p=P)[
                                   :, :, o * P:(o + 1) * P],
                               after=wdep, tag=f"{tag}wl{o}")]
                for i, (o_, w) in enumerate(slt):
                    g = o * ns + i
                    pb = PB[g % 2]

                    if split3 is None:
                        def p_mm(e, par=par, o_=o_, w=w, pb=pb):
                            mm = None
                            for d in range(C):
                                mm = e.matmul(
                                    pb[:, 0:w],
                                    WCOL[par][:, d * P:(d + 1) * P],
                                    A1[:, d * CW + o_:d * CW + o_ + w],
                                    start=(d == 0), stop=(d == C - 1))
                            return mm
                    else:
                        def p_mm_parts(e, par=par, o_=o_, w=w, pb=pb,
                                       passes=None):
                            wcb = WCOL[par].bitcast(BF16)
                            mm = None
                            for lb, rb, first, last in passes:
                                for d in range(C):
                                    mm = e.matmul(
                                        pb[:, 0:w],
                                        wcb[:, lb + d * P:lb + (d + 1) * P],
                                        A1BF[:, d * 2 * CW + rb + o_:
                                             d * 2 * CW + rb + o_ + w],
                                        start=(first and d == 0),
                                        stop=(last and d == C - 1))
                            return mm
                    if split3 is not None and otf:
                        base = [evl[g - 2]] if g >= 2 else []
                        if o == 0:
                            base = base + res_half[0 if i < 2 else 1]
                        mmh = sq.op(
                            "tensor",
                            lambda e, f=p_mm_parts:
                                f(e, passes=((0, 0, True, False),
                                             (0, T, False, False))),
                            after=[wlds[0]] + base, tag=f"{tag}mh{o}_{i}")
                        mm = sq.op(
                            "tensor",
                            lambda e, f=p_mm_parts:
                                f(e, passes=((C * P, 0, False, True),)),
                            after=[wlds[1], mmh], tag=f"{tag}m{o}_{i}")
                        mmh_last[o] = mmh
                    else:
                        if split3 is not None:
                            def p_mm(e, f=p_mm_parts):
                                return f(e, passes=((0, 0, True, False),
                                                    (0, T, False, False),
                                                    (C * P, 0, False,
                                                     True)))
                        mdep = list(wlds) + ([evl[g - 2]] if g >= 2 else [])
                        if o == 0:
                            mdep += res_half[0 if i < 2 else 1]
                        mm = sq.op("tensor", p_mm, after=mdep,
                                   tag=f"{tag}m{o}_{i}")
                    mm_last[o] = mm
                    evl[g] = make_evac(o, par, i, (o_, w), mm, pb)
                post_o(o, par)
                if fold_cb is not None:
                    fold_cb(o, par)
            return mm_last[C - 1]

        p4_entry = p3_bar
        p4_spills = []

        def p4_evac(o, par, i, sl, mm, pb):
            o_, w = sl
            dep = [mm] + ([p4_spills[o - 2]] if (o >= 2 and i == 0) else [])
            return sq.op("scalar",
                         lambda e, par=par, o_=o_, w=w, pb=pb:
                             e.activation(ROWA[par][:, o_:o_ + w],
                                          pb[:, 0:w], AF.Exp),
                         after=dep, tag=f"p4e{o}_{i}")

        p4_pend = {}

        def p4_post(o, par):
            p4_pend[o] = (sq.last["scalar"], par)
            if o > 0:
                dep, ppar = p4_pend.pop(o - 1)
                p4_spills.append(sq.dma(
                    "gpsimd", ek_h[(o - 1) * P:o * P, 0:T],
                    ROWA[ppar][:, 0:T], after=[dep], tag=f"p4sp{o - 1}"))

        def p4_flush():
            dep, ppar = p4_pend.pop(C - 1)
            p4_spills.append(sq.dma(
                "gpsimd", ek_h[(C - 1) * P:C * P, 0:T],
                ROWA[ppar][:, 0:T], after=[dep], tag=f"p4sp{C - 1}"))

        def p4_res_dep(c, half):
            d = list(kxsp.get((half, c), []))
            d += p2_bar if c < 8 else p3_bar
            return [x for x in d if x is not None]

        p4_lastmm = None
        if LOADTEST is None or LOADTEST >= 4:
            p4_lastmm = proj_phase(
                "p4", wk_h, p4_res_dep, None, p4_entry, p4_evac, p4_post,
                split3=(None, None, kxhi_h, kxlo_h), otf=True)
            p4_flush()

        p5_entry = sq.barrier()
        p5_spills = []

        def p5_res_dep(c, half):
            d = list(vxsp.get((half, c), [])) + [p4_lastmm]
            return [x for x in d if x is not None]
        p5_post = []
        p5_pend = {}
        p5_ekld = {}

        def p5_evac(o, par, i, sl, mm, pb):
            o_, w = sl
            if i == 0:
                p5_ekld[o] = sq.dma(
                    "sync", ROWA[par][:, 0:T], ek_h[o * P:(o + 1) * P, 0:T],
                    after=[p4_spills[o]] + (p5_entry if o < 2
                                            else [p5_post[o - 2]]),
                    tag=f"p5ek{o}")
            dep = [mm] + ([p5_post[o - 2]] if (o >= 2 and i == 0) else [])
            return sq.op("scalar",
                         lambda e, par=par, o_=o_, w=w, pb=pb:
                             e.copy(ROWB[par][:, o_:o_ + w], pb[:, 0:w]),
                         after=dep, tag=f"p5e{o}_{i}")

        def p5_post_fn(o, par):
            def p_ekv(e, par=par):
                return e.tensor_tensor(ROWB[par][:, 0:T], ROWA[par][:, 0:T],
                                       ROWB[par][:, 0:T], ALU.mult)
            ekv = sq.op("vector", p_ekv,
                        after=[p5_ekld[o], sq.last["scalar"]] +
                              ([p5_spills[o - 2]] if o >= 2 else []),
                        tag=f"p5ekv{o}")
            p5_pend[o] = (ekv, par)
            if o > 0:
                pekv, ppar = p5_pend.pop(o - 1)
                p5_spills.append(sq.dma(
                    "gpsimd", ekv_h[(o - 1) * P:o * P, 0:T],
                    ROWB[ppar][:, 0:T], after=[pekv], tag=f"p5sp{o - 1}"))

            lamb = vt["lam"][:, o:o + 1].to_broadcast([P, TH])

            def p_scan0(e, par=par, lamb=lamb):
                e.tensor_tensor_scan(SA[:, 1:TH + 1], lamb,
                                     ROWB[par][:, 0:TH], 0.0,
                                     ALU.mult, ALU.add)
                return e.tensor_tensor_scan(SBt[:, 1:TH + 1], lamb,
                                            ROWA[par][:, 0:TH], 0.0,
                                            ALU.mult, ALU.add)
            s0 = sq.op("vector", p_scan0, after=[ekv], tag=f"p5s0{o}")

            def p_cp01(e):
                e.tensor_copy(SA[:, 0:1], SA[:, TH:TH + 1])
                return e.tensor_copy(SBt[:, 0:1], SBt[:, TH:TH + 1])
            cp01 = sq.op("vector", p_cp01, strict=[s0], tag=f"p5cp{o}")

            def p_scan1(e, par=par, lamb=lamb):
                e.tensor_tensor_scan(SA[:, 1:TH + 1], lamb,
                                     ROWB[par][:, TH:T], SA[:, 0:1],
                                     ALU.mult, ALU.add)
                return e.tensor_tensor_scan(SBt[:, 1:TH + 1], lamb,
                                            ROWA[par][:, TH:T], SBt[:, 0:1],
                                            ALU.mult, ALU.add)
            s1 = sq.op("vector", p_scan1, strict=[cp01], tag=f"p5s1{o}")

            def p_car(e, o=o):
                e.tensor_copy(CARRY[:, o:o + 1], SA[:, TH:TH + 1])
                return e.tensor_copy(CARRY[:, C + o:C + o + 1],
                                     SBt[:, TH:TH + 1])
            p5_post.append(sq.op("vector", p_car, strict=[s1],
                                 tag=f"p5sc{o}"))

        p5_lastmm = None
        if LOADTEST is None or LOADTEST >= 5:
            p5_lastmm = proj_phase(
                "p5", wv_h, p5_res_dep, None, p5_entry, p5_evac, p5_post_fn,
                split3=(None, None, vxhi_h, vxlo_h), otf=True)
            _pekv, _ppar = p5_pend.pop(C - 1)
            p5_spills.append(sq.dma(
                "gpsimd", ekv_h[(C - 1) * P:C * P, 0:T],
                ROWB[_ppar][:, 0:T], after=[_pekv], tag=f"p5sp{C - 1}"))

        p6_entry = sq.barrier()
        cin_sp = sq.dma("gpsimd", cin_h, CARRY[:, :], after=p6_entry,
                        tag="p6in")
        selld = sq.dma("sync", SELM, selmask_h, after=p6_entry, tag="p6sel")

        def p_cc(e):
            return e.collective_compute(
                "AllGather", ALU.bypass,
                replica_groups=[list(range(NCORE))],
                ins=[cin_h.opt()], outs=[cout_h.opt()])
        if NO_CC:
            cc = sq.dma("gpsimd", cout_h[0:P, :], cin_h, after=[cin_sp],
                        tag="p6cc")
        else:
            cc = sq.op("gpsimd", p_cc, after=[cin_sp], tag="p6cc")
        gld = sq.dma("gpsimd",
                     GATH.rearrange("p (r f) -> p r f", r=NCORE),
                     cout_h.rearrange("(r p) f -> p r f", p=P),
                     strict=[cc], tag="p6g")

        f0 = sq.op("vector",
                   lambda e: e.tensor_tensor(GATH, GATH, SELM, ALU.mult),
                   after=[gld, selld], tag="p6f0")
        f1 = sq.op("vector",
                   lambda e: e.tensor_tensor(GATH[:, 0:128], GATH[:, 0:128],
                                             GATH[:, 128:256], ALU.add),
                   strict=[f0], tag="p6f1")
        f2 = sq.op("vector",
                   lambda e: e.tensor_tensor(GATH[:, 0:64], GATH[:, 0:64],
                                             GATH[:, 64:128], ALU.add),
                   strict=[f1], tag="p6f2")
        f3 = sq.op("vector",
                   lambda e: e.tensor_tensor(GATH[:, 0:32], GATH[:, 0:32],
                                             GATH[:, 32:64], ALU.add),
                   strict=[f2], tag="p6f3")
        csel = sq.op("vector",
                     lambda e: e.tensor_copy(CARRY[:, :], GATH[:, 0:32]),
                     strict=[f3], tag="p6csel")
        CIN = CARRY

        if STOP_AT <= 6 or LOADTEST is not None:
            sq.op("gpsimd", lambda e: None, after=sq.barrier(), tag="final")
            sq.finalize()
            with nc.Block() as block:
                @block.sync
                def _(eng):
                    sq.run_engine("sync", eng)

                @block.tensor
                def _(eng):
                    sq.run_engine("tensor", eng)

                @block.vector
                def _(eng):
                    sq.run_engine("vector", eng)

                @block.scalar
                def _(eng):
                    sq.run_engine("scalar", eng)

                @block.gpsimd
                def _(eng):
                    sq.run_engine("gpsimd", eng)
            return nc
        p7_entry = sq.barrier()
        rw_spills = []

        def p7_res_dep(c, half):
            d = list(rxsp.get((half, c), [])) + [p5_lastmm]
            return [x for x in d if x is not None]
        p7_post = []
        p7_lds = {}

        def p7_evac(o, par, i, sl, mm, pb):
            o_, w = sl
            if i == 0:
                ekvld = sq.dma(
                    "sync", ROWA[par][:, 0:T], ekv_h[o * P:(o + 1) * P, 0:T],
                    after=[p5_spills[o]] + (p7_entry if o < 2
                                            else [p7_post[o - 2]]),
                    tag=f"p7ekv{o}")
                ekld = sq.dma(
                    "sync", ROWB[par][:, 0:T], ek_h[o * P:(o + 1) * P, 0:T],
                    after=(p7_entry if o < 2 else [p7_post[o - 2]]),
                    tag=f"p7ek{o}")
                p7_lds[o] = (ekvld, ekld)
            dep = [mm] + ([rw_spills[o - 2]] if (o >= 2 and i == 0) else [])
            return sq.op("scalar",
                         lambda e, par=par, o_=o_, w=w, pb=pb:
                             e.activation(ROWC[par][:, o_:o_ + w],
                                          pb[:, 0:w], AF.Sigmoid),
                         after=dep, tag=f"p7sg{o}_{i}")

        def p7_post_fn(o, par):
            ekvld, ekld = p7_lds[o]
            sgdone = sq.last["scalar"]
            lamb = vt["lam"][:, o:o + 1].to_broadcast([P, TH])
            prev_half = None
            for th in range(2):
                sl = slice(th * TH, (th + 1) * TH)

                def p_cp(e, o=o, th=th):
                    if th == 0:
                        e.tensor_copy(SA[:, 0:1], CIN[:, o:o + 1])
                        return e.tensor_copy(SBt[:, 0:1],
                                             CIN[:, C + o:C + o + 1])
                    e.tensor_copy(SA[:, 0:1], SA[:, TH:TH + 1])
                    return e.tensor_copy(SBt[:, 0:1], SBt[:, TH:TH + 1])
                cpi = sq.op("vector", p_cp,
                            after=[ekvld, ekld, csel],
                            strict=[prev_half], tag=f"p7cp{o}_{th}")

                def p_scan2(e, par=par, sl=sl, lamb=lamb):
                    e.tensor_tensor_scan(SA[:, 1:TH + 1], lamb,
                                         ROWA[par][:, sl], SA[:, 0:1],
                                         ALU.mult, ALU.add)
                    return e.tensor_tensor_scan(SBt[:, 1:TH + 1], lamb,
                                                ROWB[par][:, sl],
                                                SBt[:, 0:1],
                                                ALU.mult, ALU.add)
                sc2 = sq.op("vector", p_scan2, strict=[cpi],
                            tag=f"p7sc{o}_{th}")

                def p_eu(e, par=par, o=o, sl=sl):
                    e.mul(ROWA[par][:, sl], ROWA[par][:, sl],
                          vt["eu"][:, o:o + 1])
                    return e.mul(ROWB[par][:, sl], ROWB[par][:, sl],
                                 vt["eu"][:, o:o + 1])
                eu = sq.op("scalar", p_eu, after=[sc2], tag=f"p7eu{o}_{th}")

                def p_wkv(e, par=par, sl=sl):
                    e.tensor_tensor(ROWA[par][:, sl], ROWA[par][:, sl],
                                    SA[:, 0:TH], ALU.add)
                    e.tensor_tensor(ROWB[par][:, sl], ROWB[par][:, sl],
                                    SBt[:, 0:TH], ALU.add)
                    e.reciprocal(ROWB[par][:, sl], ROWB[par][:, sl])
                    e.tensor_tensor(ROWA[par][:, sl], ROWA[par][:, sl],
                                    ROWB[par][:, sl], ALU.mult)
                    return e.tensor_tensor(ROWC[par][:, sl],
                                           ROWC[par][:, sl],
                                           ROWA[par][:, sl], ALU.mult)
                prev_half = sq.op("vector", p_wkv, after=[eu, sgdone],
                                  tag=f"p7wk{o}_{th}")
            p7_post.append(prev_half)
            rw_spills.append(sq.dma("gpsimd", rw_h[o * P:(o + 1) * P, 0:T],
                                    ROWC[par][:, 0:T], after=[prev_half],
                                    tag=f"p7sp{o}"))

        p7_lastmm = proj_phase(
            "p7", wr_h, p7_res_dep, None, p7_entry, p7_evac, p7_post_fn,
            split3=(wshi_h["wr"], wslo_h["wr"], rxhi_h, rxlo_h))

        p9_entry = sq.barrier()
        a2_spills = []
        p9_evs = {}
        p9_lds = {}

        def p9_evac(o, par, i, sl, mm, pb):
            o_, w = sl
            if i == 0:
                x1ld = sq.dma(
                    "sync", ROWA[par][:, 0:TX],
                    x1_h[o * P:(o + 1) * P, 0:TX],
                    after=(p9_entry if o < 2 else [p9_evs[o - 2]]),
                    tag=f"p9x1{o}")
                xpld = sq.dma(
                    "sync", ROWC[par][:, 0:TX],
                    xp_h[o * P:(o + 1) * P, 0:TX],
                    after=(p9_entry if o < 2 else [p9_evs[o - 2]]),
                    tag=f"p9xp{o}")
                p9_lds[o] = (x1ld, xpld)

            def p_ev(e, par=par, o_=o_, w=w, pb=pb):
                return e.tensor_tensor(ROWB[par][:, o_:o_ + w], pb[:, 0:w],
                                       ROWA[par][:, 1 + o_:1 + o_ + w],
                                       ALU.add)
            dep = [mm, p9_lds[o][0]] + ([a2_spills[o - 2]]
                                        if (o >= 2 and i == 0) else [])
            if o >= 2 and i == 0:
                dep += [d for d in (p9_fold_sqr.get(o - 2),
                                    p9_fold_mm.get(o - 2))
                        if d is not None]
            ev = sq.op("vector", p_ev, after=dep, tag=f"p9e{o}_{i}")

            def p_ev2(e, par=par, o_=o_, w=w):
                return e.tensor_tensor(ROWB[par][:, o_:o_ + w],
                                       ROWB[par][:, o_:o_ + w],
                                       ROWC[par][:, 1 + o_:1 + o_ + w],
                                       ALU.add)
            ev2 = sq.op("gpsimd", p_ev2, after=[ev, p9_lds[o][1]],
                        tag=f"p9g{o}_{i}")
            if i == 3:
                p9_evs[o] = ev2
            return ev

        def p9_post(o, par):
            a2_spills.append(sq.dma("gpsimd", a2_h[o * P:(o + 1) * P, 0:T],
                                    ROWB[par][:, 0:T],
                                    after=[p9_evs[o]], tag=f"p9sp{o}"))

        def p9_res_dep(c, half):
            d = [rw_spills[c] if c < len(rw_spills) else None, p7_lastmm]
            return [x for x in d if x is not None]

        p9_fold_sqr = {}
        p9_fold_mm = {}
        bsT = mk_bounds(0, [w for _, w in SL(T)])

        def p9_fold_cb(o, par):
            if o > 0:
                po = o - 1
                p9_fold_mm[po] = stats_acc(
                    f"p9f{po}",
                    lambda o_, w, pp=po % 2: ROWB[pp][:, o_:o_ + w],
                    lambda o_, w: RB[:, o_:o_ + w],
                    bsT, po == 0, False, [p9_fold_sqr[po]])
            p9_fold_sqr[o] = sq.op(
                "scalar",
                lambda e, par=par: e.activation(RB[:, 0:T],
                                                ROWB[par][:, 0:T],
                                                AF.Square),
                after=[p9_evs[o]] + ([p9_fold_mm[o - 1]] if o > 0 else []),
                tag=f"p9q{o}")

        proj_phase("p9", wo_h, p9_res_dep, rw_h, p9_entry, p9_evac,
                   p9_post, fold_cb=p9_fold_cb)
        p9_fold_mm[C - 1] = stats_acc(
            f"p9f{C - 1}",
            lambda o_, w, pp=(C - 1) % 2: ROWB[pp][:, o_:o_ + w],
            lambda o_, w: RB[:, o_:o_ + w],
            bsT, False, True, [p9_fold_sqr[C - 1]])

        p10_entry = sq.barrier()
        p10_fins = quant_phase("pa", lambda c: a2_h[c * P:(c + 1) * P, :],
                               [(0, 1024), (1024, 1024)], p10_entry,
                               packed=(bsT, [p9_fold_mm[C - 1]]))

        y_spills, _, _ = ffn_mm_phase(
            "pb", T,
            addsrc_fn=lambda o: a2_h[o * P:(o + 1) * P, 0:T],
            dst_fn=lambda o: y_h[o * P:(o + 1) * P, 0:T],
            entryA=[p10_fins[1]], entryB=[p10_fins[1]])

        sq.op("gpsimd", lambda e: None, after=y_spills, tag="final")

        sq.finalize()

        with nc.Block() as block:
            @block.sync
            def _(eng):
                sq.run_engine("sync", eng)

            @block.tensor
            def _(eng):
                sq.run_engine("tensor", eng)

            @block.vector
            def _(eng):
                sq.run_engine("vector", eng)

            @block.scalar
            def _(eng):
                sq.run_engine("scalar", eng)

            @block.gpsimd
            def _(eng):
                sq.run_engine("gpsimd", eng)

    return nc


def _vecfmt(v):
    return np.ascontiguousarray(v.reshape(C, P).T.astype(np.float32))


def prepare_in_maps(**inputs):
    x = np.asarray(inputs["L0_attn_x"], dtype=np.float32)
    Wk = np.ascontiguousarray(np.asarray(inputs["L0_attn_Wk"], np.float32))
    Wv = np.ascontiguousarray(np.asarray(inputs["L0_attn_Wv"], np.float32))
    Wr = np.ascontiguousarray(np.asarray(inputs["L0_attn_Wr"], np.float32))
    Wo = np.ascontiguousarray(np.asarray(inputs["L0_attn_Wo"], np.float32))
    Wt = np.ascontiguousarray(np.asarray(inputs["L0_ffn_W"], np.float32).T)
    vec = {
        "lnsc": _vecfmt(np.asarray(inputs["L0_attn_ln_scale"], np.float32)),
        "lnb": _vecfmt(np.asarray(inputs["L0_attn_ln_bias"], np.float32)),
        "mk": _vecfmt(np.asarray(inputs["L0_attn_mix_k"], np.float32)),
        "mv": _vecfmt(np.asarray(inputs["L0_attn_mix_v"], np.float32)),
        "mr": _vecfmt(np.asarray(inputs["L0_attn_mix_r"], np.float32)),
        "tf": _vecfmt(np.asarray(inputs["L0_attn_time_first"], np.float32)),
        "td": _vecfmt(np.asarray(inputs["L0_attn_time_decay"], np.float32)),
        "gam": _vecfmt(np.asarray(inputs["L0_ffn_gamma"], np.float32)),
    }
    ident = np.eye(P, dtype=np.float32)
    onesrow = np.ones((1, P), np.float32)
    onescol = np.ones((P, 1), np.float32)

    in_maps = []
    for core in range(NCORE):
        b, h = core // 2, core % 2
        t0 = h * T
        xf = np.zeros((D, CW), np.float32)
        xf[:, 1:TX] = x[b, t0:t0 + T, :].T
        if h == 1:
            xf[:, 0] = x[b, t0 - 1, :]
        pmask = np.full((P, 1), float(h), np.float32)
        selmask = np.zeros((P, NCORE, 32), np.float32)
        if h == 1:
            selmask[:, core - 1, :] = 1.0
        m = dict(x=np.ascontiguousarray(xf), wk=Wk, wv=Wv, wr=Wr, wo=Wo,
                 wt=Wt, ident=ident, onesrow=onesrow, onescol=onescol,
                 pmask=pmask,
                 selmask=np.ascontiguousarray(selmask.reshape(P,
                                                              NCORE * 32)))
        m.update(vec)
        in_maps.append(m)
    return in_maps


def kernel(**inputs):
    in_maps = prepare_in_maps(**inputs)
    nc = build_nc()
    res = run_bass_kernel_spmd(nc, in_maps, core_ids=list(range(NCORE)))
    out = np.empty((4, 4096, D), np.float32)
    for core in range(NCORE):
        b, h = core // 2, core % 2
        out[b, h * T:(h + 1) * T, :] = res.results[core]["y"].T
    return out
